# version 07
# speedup vs baseline: 1.5935x; 1.0429x over previous
import contextlib
import numpy as np

import concourse.bass as bass
import concourse.mybir as mybir
from concourse.bass_utils import run_bass_kernel_spmd
from concourse.aot_env import allow_ici_all_to_all

F32 = mybir.dt.float32
BF16 = mybir.dt.bfloat16
AF = mybir.ActivationFunctionType
ALU = mybir.AluOpType
AX = mybir.AxisListType

NCORES = 8
T = 16384
D = 2048
TPC = T // NCORES
NTB = TPC // 128
NKB = D // 128
ROT = 1024
EPS = 1e-5

D_INNER = 4096
D_STATE = 16
D_CONV = 4
DT_RANK = 128
HIDDEN = 8192
DPC = D_INNER // NCORES

E_MOE = 8
I_MOE = 1408
I_SH = 5632

NUM_TILES = 64
D_SLICE = 32
GRID = 16


class Em:
    ENGINES = ('sync', 'tensor', 'scalar', 'vector', 'gpsimd')

    def __init__(self, nc, stack):
        self.nc = nc
        self.stack = stack
        self.q = {e: [] for e in self.ENGINES}
        self.sems = {}
        self.cnt = {}

    def sem(self, name):
        if name in self.sems:
            return self.sems[name]
        h = self.stack.enter_context(self.nc.semaphore(name))
        self.sems[name] = h
        self.cnt[name] = 0
        return h

    def barrier(self):
        for eng in self.ENGINES:
            for name in list(self.sems):
                if self.cnt[name] > 0:
                    self.wait(eng, name)

    def wait(self, eng, name, val=None):
        if val is None:
            val = self.cnt[name]
        h = self.sems[name]
        self.q[eng].append(lambda e, h=h, v=val: e.wait_ge(h, v))

    AUTO = {'scalar': 'sc', 'vector': 've'}

    def op(self, eng, fn, inc=None, inc_by=1):
        own = self.AUTO.get(eng)
        if own is not None:
            self.wait(eng, own)
            if inc is None:
                inc, inc_by = own, 1
            assert inc == own, f'{eng} ops must inc {own}'
        if inc is None:
            self.q[eng].append(fn)
            return None
        h = self.sems[inc]
        self.cnt[inc] += inc_by
        tgt = self.cnt[inc]
        def wrapped(e, fn=fn, h=h, k=inc_by):
            fn(e).then_inc(h, k)
        self.q[eng].append(wrapped)
        return tgt

    def dma(self, eng, dst, src, inc=None):
        return self.op(eng, lambda e: e.dma_start(out=dst, in_=src), inc=inc,
                       inc_by=16)

    def setup_consts(self, values, cv_sb, cv_dram):
        if not hasattr(self, 'cv_global'):
            self.cv_global = self.stack.enter_context(
                self.nc.sbuf_tensor('cv_global', [128, 8], F32,
                                    allow_name_mangling=True))
            self.sem('ldcv')
            self.dma('sync', self.cv_global[:],
                     cv_dram.partition_broadcast(128).squeeze(1), inc='ldcv')
            for eng in ('scalar', 'vector', 'tensor', 'gpsimd'):
                self.wait(eng, 'ldcv')
            self._consts_done = set()
        for i, v in enumerate(values):
            if float(v) not in self._consts_done:
                self._consts_done.add(float(v))
                self.nc.const_aps.register(F32, float(v),
                                           self.cv_global[:, i:i + 1])

    def coll(self, kind, op, ins, outs, inc):
        h = self.sems[inc]
        self.cnt[inc] += 1
        tgt = 1
        rg = [list(range(NCORES))]
        def wrapped(e, h=h):
            with allow_ici_all_to_all():
                e.collective_compute(kind, op, replica_groups=rg, ins=ins,
                                     outs=outs).then_inc(h)
        self.q['gpsimd'].append(wrapped)
        return tgt

    def finalize(self):
        with self.nc.Block() as block:
            for eng in self.ENGINES:
                ops = self.q[eng]
                if not ops:
                    continue
                dec = getattr(block, eng)
                def body(e, ops=ops):
                    for fn in ops:
                        fn(e)
                dec(body)


class PP:
    def __init__(self, em, stack, nc, name, shape, n=2, banks=None):
        if banks is not None:
            self.ps = banks
            n = len(banks)
        else:
            self.ps = [stack.enter_context(nc.psum_tensor(
                f'{name}{i}', shape, F32, allow_name_mangling=True))
                for i in range(n)]
        self.gate = [None] * n
        self.i = 0

    def nxt(self, em):
        k = self.i % len(self.ps)
        if self.gate[k] is not None:
            em.wait('tensor', self.gate[k][0], self.gate[k][1])
        self.i += 1
        return self.ps[k]

    def done(self, sem, cnt):
        k = (self.i - 1) % len(self.ps)
        self.gate[k] = (sem, cnt)


def mm_accum(em, ps_ap, parts, inc='te', skip_group_check=False):
    n = len(parts)
    last = None
    for i, (lh, rh) in enumerate(parts):
        fn = lambda e, lh=lh, rh=rh, i=i: e.matmul(
            ps_ap, lh, rh, start=(i == 0), stop=(i == n - 1),
            skip_group_check=skip_group_check)
        if i == n - 1:
            last = em.op('tensor', fn, inc=inc)
        else:
            em.op('tensor', fn)
    return last


def emit_plain_ln(em, src, dst, stt, sq, n=D, eps=EPS):
    s_sum = stt[:, 0:1]
    s_sumsq = stt[:, 1:2]
    s_mean = stt[:, 2:3]
    s_m2 = stt[:, 3:4]
    s_var = stt[:, 4:5]
    s_rstd = stt[:, 5:6]
    s_negmr = stt[:, 6:7]
    inv_n = 1.0 / n
    em.wait('vector', 'sc')
    em.wait('scalar', 've')
    c_sq = em.op('scalar', lambda e: e.activation(sq, src, AF.Square,
                                                  accum_out=s_sumsq),
                 inc='sc')
    em.op('vector', lambda e: e.reduce_sum(s_sum, src, AX.X))
    em.wait('vector', 'sc', c_sq)
    em.op('vector', lambda e: e.tensor_scalar(s_mean, s_sum, inv_n, None,
                                              ALU.mult))
    em.op('vector', lambda e: e.tensor_tensor(s_m2, s_mean, s_mean, ALU.mult))
    c_var = em.op('vector', lambda e: e.scalar_tensor_tensor(
        s_var, s_sumsq, inv_n, s_m2, ALU.mult, ALU.subtract), inc='ve')
    em.wait('scalar', 've', c_var)
    em.op('scalar', lambda e: e.activation(s_rstd, s_var, AF.Ln, bias=eps))
    c_r = em.op('scalar', lambda e: e.activation(s_rstd, s_rstd, AF.Exp,
                                                 scale=-0.5), inc='sc')
    em.wait('vector', 'sc', c_r)
    c_nm = em.op('vector', lambda e: e.scalar_tensor_tensor(
        s_negmr, s_mean, -1.0, s_rstd, ALU.mult, ALU.mult), inc='ve')
    em.wait('scalar', 've', c_nm)
    return em.op('scalar', lambda e: e.activation(dst, src, AF.Identity,
                                                  scale=s_rstd,
                                                  bias=s_negmr), inc='sc')


def declare_io_A(nc, internal=()):
    io = {}
    io['x0'] = nc.dram_tensor('x0', [TPC, D], F32, kind=None if 'x0' in internal else 'ExternalInput').ap()
    io['wq'] = nc.dram_tensor('wq', [D, D], F32, kind=None if 'wq' in internal else 'ExternalInput').ap()
    io['wk'] = nc.dram_tensor('wk', [D, D], F32, kind=None if 'wk' in internal else 'ExternalInput').ap()
    io['wv'] = nc.dram_tensor('wv', [D, D], F32, kind=None if 'wv' in internal else 'ExternalInput').ap()
    io['bqkv'] = nc.dram_tensor('bqkv', [3, D], F32, kind=None if 'bqkv' in internal else 'ExternalInput').ap()
    io['gb'] = nc.dram_tensor('gb', [2, D], F32, kind=None if 'gb' in internal else 'ExternalInput').ap()
    io['cos'] = nc.dram_tensor('cos', [TPC, ROT // 2], F32, kind=None if 'cos' in internal else 'ExternalInput').ap()
    io['sin'] = nc.dram_tensor('sin', [TPC, ROT // 2], F32, kind=None if 'sin' in internal else 'ExternalInput').ap()
    io['ones'] = nc.dram_tensor('ones', [128, 128], F32, kind=None if 'ones' in internal else 'ExternalInput').ap()
    io['ident'] = nc.dram_tensor('ident', [128, 128], F32, kind=None if 'ident' in internal else 'ExternalInput').ap()
    io['cvals'] = nc.dram_tensor('cvals', [1, 8], F32, kind=None if 'cvals' in internal else 'ExternalInput').ap()
    io['qt'] = nc.dram_tensor('qt', [D, TPC], F32, kind=None if 'qt' in internal else 'ExternalOutput').ap()
    io['apart'] = nc.dram_tensor('apart', [1, D], F32, kind=None if 'apart' in internal else 'ExternalOutput').ap()
    io['kst'] = nc.dram_tensor('kst', [TPC, D], F32).ap()
    return io


def emit_A(nc, em, stack, io):
    X0 = io['x0']
    WQ = io['wq']
    WK = io['wk']
    WV = io['wv']
    BQKV = io['bqkv']
    GB = io['gb']
    COS = io['cos']
    SIN = io['sin']
    ONES = io['ones']
    IDENT = io['ident']
    CVALS = io['cvals']
    QT = io['qt']
    APART = io['apart']
    KST = io['kst']
    sb = lambda name, shp, dt: stack.enter_context(
        nc.sbuf_tensor(name, shp, dt, allow_name_mangling=True))
    xnT = sb('xnT', [128, NKB * TPC], BF16)
    wbig = sb('wbig', [128, NKB * D], BF16)
    xtb = sb('xtb', [128, D], F32)
    t1tb = sb('t1tb', [128, D], F32)
    sqtb = sb('sqtb', [128, D], F32)
    brtb = sqtb
    ktile = sb('ktile', [128, D], F32)
    vtile = sb('vtile', [128, D], F32)
    rot_a = sb('rot_a', [128, ROT // 2], F32)
    rot_b = sb('rot_b', [128, ROT // 2], F32)
    rot_c = sb('rot_c', [128, ROT // 2], F32)
    costb = sb('costb', [128, ROT // 2], F32)
    sintb = sb('sintb', [128, ROT // 2], F32)
    wld = sb('wld', [128, D], F32)
    stt = sb('stt', [128, 16], F32)
    ones_sb = sb('ones_sb', [128, 128], F32)
    ident_sb = sb('ident_sb', [128, 128], F32)
    gb_sb = sb('gb_sb', [128, 2 * NKB], F32)
    bq_sb = sb('bq_sb', [128, NKB], F32)
    brow = t1tb[0:1, :]
    qout = sb('qout', [128, 512], F32)
    arow = ktile[0:1, :]
    cv_sb = sb('cv_sb', [128, 8], F32)
    ppq = PP(em, stack, nc, 'psq', [128, 512])
    ppk = PP(em, stack, nc, 'psk', [128, 512])
    ppt = PP(em, stack, nc, 'pst', [128, 128])
    psb = stack.enter_context(nc.psum_tensor('psb', [128, 512], F32, allow_name_mangling=True))
    psa1 = stack.enter_context(nc.psum_tensor('psa1', [1, 512], F32,
                                              allow_name_mangling=True))
    aacc = sb('aacc', [1, D], F32)

    for s in ['ldc', 'ldx', 'ldw', 'ldb', 'ldcos', 'ldsin', 'ldk',
              'outq', 'outk', 'outa', 'sc', 've', 'te']:
        em.sem(s)

    em.setup_consts([EPS, -0.5], cv_sb, CVALS)
    em.dma('sync', ones_sb[:], ONES, inc='ldc')
    em.dma('sync', ident_sb[:], IDENT, inc='ldc')
    for i in range(2):
        for kb in range(NKB):
            em.dma('sync', gb_sb[:, i * NKB + kb:i * NKB + kb + 1],
                   GB[i:i + 1, kb * 128:(kb + 1) * 128].rearrange(
                       'a p -> p a'), inc='ldc')
    for kb in range(NKB):
        em.dma('sync', bq_sb[:, kb:kb + 1],
               BQKV[0:1, kb * 128:(kb + 1) * 128].rearrange('a p -> p a'),
               inc='ldc')
    em.wait('scalar', 'ldc')
    em.wait('vector', 'ldc')
    em.wait('tensor', 'ldc')

    em.op('vector', lambda e: e.memset(aacc[:], 0.0))
    for tb in range(NTB):
        cx = em.dma('sync', xtb[:], X0[tb * 128:(tb + 1) * 128, :],
                    inc='ldx')
        em.wait('scalar', 'ldx', cx)
        em.wait('vector', 'ldx', cx)
        emit_plain_ln(em, xtb[:], t1tb[:], stt, sqtb[:])
        c = emit_plain_ln(em, t1tb[:], xtb[:], stt, sqtb[:])
        em.wait('tensor', 'sc', c)
        for kb in range(NKB):
            pstk = ppt.nxt(em)
            ct = em.op('tensor', lambda e, kb=kb, pstk=pstk: e.transpose(
                pstk[:], xtb[:, kb * 128:(kb + 1) * 128], ident_sb[:]),
                inc='te')
            em.wait('scalar', 'te', ct)
            ce2 = em.op('scalar', lambda e, kb=kb, tb=tb, pstk=pstk:
                        e.activation(
                xnT[:, kb * TPC + tb * 128: kb * TPC + (tb + 1) * 128],
                pstk[:], AF.Identity, scale=gb_sb[:, kb:kb + 1],
                bias=gb_sb[:, NKB + kb:NKB + kb + 1]), inc='sc')
            ppt.done('sc', ce2)
        em.wait('sync', 'sc')

    def load_w(WSRC):
        em.wait('scalar', 'te')
        for kb in range(NKB):
            cx = em.dma('sync', wld[:], WSRC[kb * 128:(kb + 1) * 128, :],
                        inc='ldw')
            em.wait('scalar', 'ldw', cx)
            c = em.op('scalar', lambda e, kb=kb: e.activation(
                wbig[:, kb * D:(kb + 1) * D], wld[:], AF.Copy), inc='sc')
            em.wait('sync', 'sc', c)
        return em.cnt['sc']

    cw = load_w(WQ)
    em.wait('tensor', 'sc', cw)
    for cb in range(NKB):
        for tch in range(TPC // 512):
            parts = [(wbig[:, kb * D + cb * 128:kb * D + (cb + 1) * 128],
                      xnT[:, kb * TPC + tch * 512:kb * TPC + (tch + 1) * 512])
                     for kb in range(NKB)]
            psq = ppq.nxt(em)
            cm = mm_accum(em, psq[:], parts, inc='te')
            em.wait('scalar', 'te', cm)
            ce = em.op('scalar', lambda e, cb=cb, psq=psq: e.activation(
                qout[:], psq[:], AF.Identity, bias=bq_sb[:, cb:cb + 1]),
                inc='sc')
            ppq.done('sc', ce)
            em.wait('gpsimd', 'sc', ce)
            cd = em.op('gpsimd', lambda e, cb=cb, tch=tch: e.dma_start(
                out=QT[cb * 128:(cb + 1) * 128, tch * 512:(tch + 1) * 512],
                in_=qout[:]), inc='outq', inc_by=16)
            em.wait('scalar', 'outq', cd)

    def kv_pass(which):
        WSRC = WK if which == 0 else WV
        tile = ktile if which == 0 else vtile
        em.wait('sync', 've')
        em.wait('sync', 'outk')
        cbr = em.dma('sync', brow, BQKV[1 + which:2 + which, :],
                     inc='ldb')
        cw = load_w(WSRC)
        em.wait('tensor', 'sc', cw)
        em.wait('tensor', 'ldb', cbr)
        em.wait('scalar', 've')
        for ncx in range(4):
            em.wait('tensor', 'sc')
            cm = em.op('tensor', lambda e, ncx=ncx: e.matmul(
                psb[:], ones_sb[0:1, :],
                brow[:, ncx * 512:(ncx + 1) * 512], start=True, stop=True),
                inc='te')
            em.wait('scalar', 'te', cm)
            em.op('scalar', lambda e, ncx=ncx: e.activation(
                brtb[:, ncx * 512:(ncx + 1) * 512], psb[:], AF.Copy),
                inc='sc')
        em.wait('tensor', 'sc')
        em.wait('vector', 'sc')

        for tb in range(NTB):
            if tb > 0:
                em.wait('sync', 've')
            c_cos = em.dma('sync', costb[:],
                           COS[tb * 128:(tb + 1) * 128, :], inc='ldcos')
            c_sin = em.dma('sync', sintb[:],
                           SIN[tb * 128:(tb + 1) * 128, :], inc='ldsin')
            if which == 1:
                c_krb = em.dma('sync', ktile[:],
                               KST[tb * 128:(tb + 1) * 128, :], inc='ldk')
            for ncx in range(4):
                parts = [(xnT[:, kb * TPC + tb * 128:kb * TPC + (tb + 1) * 128],
                          wbig[:, kb * D + ncx * 512:kb * D + (ncx + 1) * 512])
                         for kb in range(NKB)]
                psk = ppk.nxt(em)
                cm = mm_accum(em, psk[:], parts, inc='te')
                em.wait('vector', 'te', cm)
                cv = em.op('vector', lambda e, ncx=ncx, psk=psk:
                           e.tensor_tensor(
                    tile[:, ncx * 512:(ncx + 1) * 512], psk[:],
                    brtb[:, ncx * 512:(ncx + 1) * 512], ALU.add),
                    inc='ve')
                ppk.done('ve', cv)
            em.wait('vector', 'ldcos', c_cos)
            em.wait('vector', 'ldsin', c_sin)
            r3 = tile[:, 0:ROT].rearrange('p (j two) -> p two j', two=2)
            ev = r3[:, 0, :]
            od = r3[:, 1, :]
            em.op('vector', lambda e, ev=ev: e.tensor_tensor(
                rot_a[:], ev, costb[:], ALU.mult))
            em.op('vector', lambda e, od=od: e.tensor_tensor(
                rot_b[:], od, sintb[:], ALU.mult))
            em.op('vector', lambda e: e.tensor_tensor(
                rot_a[:], rot_a[:], rot_b[:], ALU.subtract))
            em.op('vector', lambda e, od=od: e.tensor_tensor(
                rot_b[:], od, costb[:], ALU.mult))
            em.op('vector', lambda e, ev=ev: e.tensor_tensor(
                rot_c[:], ev, sintb[:], ALU.mult))
            em.op('vector', lambda e, od=od: e.tensor_tensor(
                od, rot_b[:], rot_c[:], ALU.add))
            cv = em.op('vector', lambda e, ev=ev: e.tensor_copy(
                ev, rot_a[:]), inc='ve')
            if which == 0:
                em.wait('gpsimd', 've', cv)
                cd = em.op('gpsimd', lambda e, tb=tb: e.dma_start(
                    out=KST[tb * 128:(tb + 1) * 128, :], in_=ktile[:]),
                    inc='outk', inc_by=16)
                em.wait('vector', 'outk', cd)
            else:
                em.wait('scalar', 've', cv)
                em.wait('scalar', 'ldk', c_krb)
                em.op('scalar', lambda e: e.activation(
                    t1tb[:], ktile[:], AF.Square, accum_out=stt[:, 8:9]))
                em.op('scalar', lambda e: e.activation(
                    t1tb[:], vtile[:], AF.Square, accum_out=stt[:, 9:10]))
                em.op('scalar', lambda e: e.activation(
                    stt[:, 10:11], stt[:, 8:9], AF.Ln))
                cs = em.op('scalar', lambda e: e.activation(
                    stt[:, 11:12], stt[:, 9:10], AF.Ln), inc='sc')
                em.wait('vector', 'sc', cs)
                cv2 = em.op('vector', lambda e: e.tensor_tensor(
                    stt[:, 13:14], stt[:, 10:11], stt[:, 11:12], ALU.add),
                    inc='ve')
                em.wait('scalar', 've', cv2)
                cs2 = em.op('scalar', lambda e: e.activation(
                    stt[:, 14:15], stt[:, 13:14], AF.Exp, scale=-0.5),
                    inc='sc')
                em.wait('vector', 'sc', cs2)
                em.op('vector', lambda e: e.tensor_tensor(
                    t1tb[:], ktile[:], vtile[:], ALU.mult))
                cv3 = em.op('vector', lambda e: e.tensor_scalar(
                    t1tb[:], t1tb[:], stt[:, 14:15], None, ALU.mult),
                    inc='ve')
                em.wait('tensor', 've', cv3)
                for ncx in range(4):
                    cm = em.op('tensor', lambda e, ncx=ncx: e.matmul(
                        psa1[:], ones_sb[:, 0:1],
                        t1tb[:, ncx * 512:(ncx + 1) * 512],
                        start=True, stop=True), inc='te')
                    em.wait('vector', 'te', cm)
                    cva = em.op('vector', lambda e, ncx=ncx: e.tensor_tensor(
                        aacc[:, ncx * 512:(ncx + 1) * 512],
                        aacc[:, ncx * 512:(ncx + 1) * 512], psa1[:], ALU.add),
                        inc='ve')
                    em.wait('tensor', 've', cva)

    kv_pass(0)
    kv_pass(1)
    em.wait('gpsimd', 've')
    em.dma('gpsimd', APART, aacc[:], inc='outa')
    em.wait('gpsimd', 'outa')
    em.wait('gpsimd', 'outq')



def build_launchA():
    nc = bass.Bass('TRN2', target_bir_lowering=False)
    io = declare_io_A(nc)
    stack = contextlib.ExitStack()
    with stack:
        em = Em(nc, stack)
        emit_A(nc, em, stack, io)
        em.finalize()
    return nc


BIGC = 65536.0

def declare_io_B(nc, internal=()):
    io = {}
    io['x0'] = nc.dram_tensor('x0', [TPC, D], F32, kind=None if 'x0' in internal else 'ExternalInput').ap()
    io['qt'] = nc.dram_tensor('qt', [D, TPC], F32, kind=None if 'qt' in internal else 'ExternalInput').ap()
    io['aparts'] = nc.dram_tensor('aparts', [NCORES, D], F32, kind=None if 'aparts' in internal else 'ExternalInput').ap()
    io['wout'] = nc.dram_tensor('wout', [D, D], F32, kind=None if 'wout' in internal else 'ExternalInput').ap()
    io['bout'] = nc.dram_tensor('bout', [1, D], F32, kind=None if 'bout' in internal else 'ExternalInput').ap()
    io['gb2'] = nc.dram_tensor('gb2', [2, D], F32, kind=None if 'gb2' in internal else 'ExternalInput').ap()
    io['slopes'] = nc.dram_tensor('slopes', [NUM_TILES, D_SLICE * GRID], F32, kind=None if 'slopes' in internal else 'ExternalInput').ap()
    io['ones'] = nc.dram_tensor('ones', [128, 128], F32, kind=None if 'ones' in internal else 'ExternalInput').ap()
    io['ident'] = nc.dram_tensor('ident', [128, 128], F32, kind=None if 'ident' in internal else 'ExternalInput').ap()
    io['cvals'] = nc.dram_tensor('cvals', [1, 8], F32, kind=None if 'cvals' in internal else 'ExternalInput').ap()
    io['tab'] = nc.dram_tensor('tab', [3, 64], F32, kind=None if 'tab' in internal else 'ExternalInput').ap()
    io['x1'] = nc.dram_tensor('x1', [TPC, D], F32, kind=None if 'x1' in internal else 'ExternalOutput').ap()
    io['oh'] = nc.dram_tensor('oh', [TPC, 64], F32, kind=None if 'oh' in internal else 'ExternalOutput').ap()
    io['xsl'] = nc.dram_tensor('xsl', [TPC, 32], F32, kind=None if 'xsl' in internal else 'ExternalOutput').ap()
    io['minp'] = nc.dram_tensor('minp', [64, 1], F32, kind=None if 'minp' in internal else 'ExternalOutput').ap()
    io['maxp'] = nc.dram_tensor('maxp', [64, 1], F32, kind=None if 'maxp' in internal else 'ExternalOutput').ap()
    io['ss'] = nc.dram_tensor('ss', [NUM_TILES, D_SLICE], F32).ap()
    return io


def emit_B(nc, em, stack, io):
    X0 = io['x0']
    QT = io['qt']
    APARTS = io.get('aparts')
    WOUT = io['wout']
    BOUT = io['bout']
    GB2 = io['gb2']
    SLOPES = io['slopes']
    ONES = io['ones']
    IDENT = io['ident']
    CVALS = io['cvals']
    TAB = io['tab']
    X1 = io['x1']
    OH = io['oh']
    XSL = io['xsl']
    MINP = io['minp']
    MAXP = io['maxp']
    SS = io['ss']
    sb = lambda name, shp, dt: stack.enter_context(
        nc.sbuf_tensor(name, shp, dt, allow_name_mangling=True))
    ybig = sb('ybig', [128, NKB * TPC], BF16)
    wbig = sb('wbig', [128, NKB * D], BF16)
    xtb = sb('xtb', [128, D], F32)
    t1tb = sb('t1tb', [128, D], F32)
    t2tb = sb('t2tb', [128, D], F32)
    sqtb = sb('sqtb', [128, D], F32)
    wld = sb('wld', [128, D], F32)
    brtb = sb('brtb', [128, D], F32)
    g2rep = brtb
    b2rep = wld
    stt = sb('stt', [128, 16], F32)
    ones_sb = sb('ones_sb', [128, 128], F32)
    ident_sb = sb('ident_sb', [128, 128], F32)
    afac = sb('afac', [128, NKB], F32)
    apld = sb('apld', [128, NKB], F32)
    cv_sb = sb('cv_sb', [128, 8], F32)
    ssrep = sb('ssrep', [128, D], F32)
    sl_sb = sb('sl_sb', [64, D_SLICE * GRID], F32)
    ss_sb = sb('ss_sb', [64, D_SLICE], F32)
    iot64 = sb('iot64', [128, 64], F32)
    cidx = sb('cidx', [128, 64], F32)
    iot8 = sb('iot8', [128, 8], F32)
    iot64m = sb('iot64m', [128, 64], F32)
    iot8m = sb('iot8m', [128, 8], F32)
    scor = sb('scor', [128, 64], F32)
    msk = sb('msk', [128, 64], F32)
    ohtb = sb('ohtb', [128, 64], F32)
    xsl_sb = sb('xsl_sb', [128, 32], F32)
    mmtb = sb('mmtb', [128, 64], F32)
    minacc = sb('minacc', [64, 4], F32)
    redsc = sb('redsc', [64, 128], F32)
    pst = stack.enter_context(nc.psum_tensor('pst', [128, 128], F32, allow_name_mangling=True))
    psb = stack.enter_context(nc.psum_tensor('psb', [128, 512], F32, allow_name_mangling=True))
    ppy = PP(em, stack, nc, 'psy', [128, 512])
    psm = stack.enter_context(nc.psum_tensor('psm', [64, 128], F32, allow_name_mangling=True))

    for s in ['ldc', 'lda0', 'lda1', 'ldx', 'ldw', 'ldq', 'ldb', 'ldss',
              'outx', 'outo', 'outs', 'outm', 'outss', 'sc', 've', 'te']:
        em.sem(s)

    em.setup_consts([EPS, -0.5], cv_sb, CVALS)
    em.dma('sync', ones_sb[:], ONES, inc='ldc')
    em.dma('sync', ident_sb[:], IDENT, inc='ldc')
    em.dma('sync', iot64[:], TAB[0:1, :].partition_broadcast(128).squeeze(1),
           inc='ldc')
    em.dma('sync', cidx[:], TAB[1:2, :].partition_broadcast(128).squeeze(1),
           inc='ldc')
    em.dma('sync', iot8[:], TAB[2:3, 0:8].partition_broadcast(128).squeeze(1),
           inc='ldc')
    em.dma('sync', sl_sb[:], SLOPES, inc='ldc')
    em.wait('vector', 'ldc')
    em.wait('scalar', 'ldc')
    em.wait('tensor', 'ldc')
    em.op('vector', lambda e: e.tensor_scalar(iot64m[:], iot64[:], -BIGC,
                                              None, ALU.add))
    em.op('vector', lambda e: e.tensor_scalar(iot8m[:], iot8[:], -BIGC,
                                              None, ALU.add))
    em.op('vector', lambda e: e.memset(minacc[:, 0:1], BIGC))
    em.op('vector', lambda e: e.memset(minacc[:, 1:2], -BIGC))

    if 'asum' in io:
        for kb in range(NKB):
            em.dma('sync', apld[:, kb:kb + 1],
                   io['asum'][0:1, kb * 128:(kb + 1) * 128].rearrange(
                       'a p -> p a'), inc='lda0')
        em.wait('vector', 'lda0')
    else:
      for r in range(NCORES):
        dst = apld if r == 0 else afac
        sem_r = 'lda0' if r % 2 == 0 else 'lda1'
        for kb in range(NKB):
            em.dma('sync', dst[:, kb:kb + 1],
                   APARTS[r:r + 1, kb * 128:(kb + 1) * 128].rearrange(
                       'a p -> p a'), inc=sem_r)
        em.wait('vector', sem_r)
        if r > 0:
            em.op('vector', lambda e: e.tensor_tensor(
                apld[:], apld[:], afac[:], ALU.add))
        em.wait('sync', 've')
    cA = em.op('vector', lambda e: e.tensor_scalar(
        apld[:], apld[:], 0.5, 0.5, ALU.mult, ALU.add), inc='ve')

    cb = em.dma('sync', t1tb[0:1, :], BOUT, inc='ldb')
    em.wait('tensor', 'ldb', cb)
    for ncx in range(4):
        em.wait('tensor', 'sc')
        cm = em.op('tensor', lambda e, ncx=ncx: e.matmul(
            psb[:], ones_sb[0:1, :], t1tb[0:1, ncx * 512:(ncx + 1) * 512],
            start=True, stop=True), inc='te')
        em.wait('scalar', 'te', cm)
        em.op('scalar', lambda e, ncx=ncx: e.activation(
            brtb[:, ncx * 512:(ncx + 1) * 512], psb[:], AF.Copy),
            inc='sc')

    em.op('vector', lambda e: e.tensor_reduce(
        ss_sb[:], sl_sb[:].rearrange('p (j g) -> p j g', g=GRID),
        AX.X, ALU.add))
    em.wait('scalar', 've')
    cs = em.op('scalar', lambda e: e.activation(
        ss_sb[:], ss_sb[:], AF.Sign), inc='sc')
    em.wait('gpsimd', 'sc', cs)
    cd = em.op('gpsimd', lambda e: e.dma_start(out=SS, in_=ss_sb[:]),
               inc='outss', inc_by=16)
    em.wait('sync', 'outss', cd)
    cr = em.dma('sync', ssrep[:],
                SS.rearrange('a b -> (a b)').unsqueeze(0)
                .partition_broadcast(128).squeeze(1), inc='ldss')

    em.wait('scalar', 've', cA)
    for kb in range(NKB):
        cq = em.dma('sync', wld[:], QT[kb * 128:(kb + 1) * 128, :],
                    inc='ldq')
        em.wait('scalar', 'ldq', cq)
        c = em.op('scalar', lambda e, kb=kb: e.activation(
            ybig[:, kb * TPC:(kb + 1) * TPC], wld[:], AF.Identity,
            scale=apld[:, kb:kb + 1]), inc='sc')
        em.wait('sync', 'sc', c)

    em.wait('scalar', 'te')
    for kb in range(NKB):
        cx = em.dma('sync', wld[:], WOUT[kb * 128:(kb + 1) * 128, :],
                    inc='ldw')
        em.wait('scalar', 'ldw', cx)
        c = em.op('scalar', lambda e, kb=kb: e.activation(
            wbig[:, kb * D:(kb + 1) * D], wld[:], AF.Copy), inc='sc')
        em.wait('sync', 'sc', c)
    em.wait('tensor', 'sc')

    for tb in range(NTB):
        cx = em.dma('sync', xtb[:], X0[tb * 128:(tb + 1) * 128, :],
                    inc='ldx')
        em.wait('scalar', 'ldx', cx)
        em.wait('vector', 'ldx', cx)
        emit_plain_ln(em, xtb[:], t1tb[:], stt, sqtb[:])
        em.wait('vector', 'sc')
        for ncx in range(4):
            parts = [(ybig[:, kb * TPC + tb * 128:kb * TPC + (tb + 1) * 128],
                      wbig[:, kb * D + ncx * 512:kb * D + (ncx + 1) * 512])
                     for kb in range(NKB)]
            psy = ppy.nxt(em)
            cm = mm_accum(em, psy[:], parts, inc='te')
            em.wait('vector', 'te', cm)
            em.op('vector', lambda e, ncx=ncx, psy=psy: e.tensor_tensor(
                t2tb[:, ncx * 512:(ncx + 1) * 512], psy[:],
                brtb[:, ncx * 512:(ncx + 1) * 512], ALU.add))
            cv = em.op('vector', lambda e, ncx=ncx: e.tensor_tensor(
                t2tb[:, ncx * 512:(ncx + 1) * 512],
                t2tb[:, ncx * 512:(ncx + 1) * 512],
                t1tb[:, ncx * 512:(ncx + 1) * 512], ALU.add), inc='ve')
            ppy.done('ve', cv)
        emit_plain_ln(em, t2tb[:], t2tb[:], stt, sqtb[:])
        em.wait('vector', 'sc')
        cv = em.op('vector', lambda e: e.tensor_tensor(
            xtb[:], xtb[:], t2tb[:], ALU.add), inc='ve')
        em.wait('gpsimd', 've', cv)
        cdx = em.op('gpsimd', lambda e, tb=tb: e.dma_start(
            out=X1[tb * 128:(tb + 1) * 128, :], in_=xtb[:]),
            inc='outx', inc_by=16)
        em.wait('sync', 'outx', cdx)

    em.wait('scalar', 've')
    em.wait('tensor', 've')
    for i in range(2):
        dst = g2rep if i == 0 else b2rep
        cb = em.dma('sync', t1tb[0:1, :], GB2[i:i + 1, :], inc='ldb')
        em.wait('tensor', 'ldb', cb)
        for ncx in range(4):
            em.wait('tensor', 'sc')
            cm = em.op('tensor', lambda e, ncx=ncx: e.matmul(
                psb[:], ones_sb[0:1, :],
                t1tb[0:1, ncx * 512:(ncx + 1) * 512],
                start=True, stop=True), inc='te')
            em.wait('scalar', 'te', cm)
            em.op('scalar', lambda e, dst=dst, ncx=ncx: e.activation(
                dst[:, ncx * 512:(ncx + 1) * 512], psb[:], AF.Copy),
                inc='sc')
        em.wait('sync', 'sc')

    em.wait('vector', 'ldss', cr)

    for tb in range(NTB):
        em.wait('sync', 'sc')
        em.wait('sync', 've')
        cx = em.dma('sync', xtb[:], X1[tb * 128:(tb + 1) * 128, :],
                    inc='ldx')
        em.wait('scalar', 'ldx', cx)
        em.wait('vector', 'ldx', cx)
        emit_plain_ln(em, xtb[:], t1tb[:], stt, sqtb[:])
        c3 = emit_plain_ln(em, t1tb[:], t2tb[:], stt, sqtb[:])
        em.wait('vector', 'sc', c3)
        em.op('vector', lambda e: e.tensor_tensor(
            t2tb[:], t2tb[:], g2rep[:], ALU.mult))
        cxn = em.op('vector', lambda e: e.tensor_tensor(
            t2tb[:], t2tb[:], b2rep[:], ALU.add), inc='ve')
        em.wait('scalar', 've', cxn)
        em.op('scalar', lambda e: e.activation(t1tb[:], t2tb[:], AF.Sign),
              inc='sc')
        em.wait('vector', 'sc')
        em.op('vector', lambda e: e.tensor_tensor(
            sqtb[:], t1tb[:], ssrep[:], ALU.mult))
        em.op('vector', lambda e: e.tensor_reduce(
            scor[:], sqtb[:].rearrange('p (s i) -> p s i', i=D_SLICE),
            AX.X, ALU.add))
        em.op('vector', lambda e: e.tensor_reduce(
            stt[:, 8:16], scor[:].rearrange('p (c i) -> p c i', i=8),
            AX.X, ALU.add))
        em.op('vector', lambda e: e.tensor_reduce(
            stt[:, 7:8], stt[:, 8:16], AX.X, ALU.max))
        em.op('vector', lambda e: e.tensor_scalar(
            msk[:, 0:8], stt[:, 8:16], stt[:, 7:8], None, ALU.is_ge))
        em.op('vector', lambda e: e.tensor_tensor(
            msk[:, 8:16], msk[:, 0:8], iot8m[:], ALU.mult))
        em.op('vector', lambda e: e.tensor_scalar(
            msk[:, 8:16], msk[:, 8:16], BIGC, None, ALU.add))
        em.op('vector', lambda e: e.tensor_reduce(
            stt[:, 0:1], msk[:, 8:16], AX.X, ALU.min))
        em.op('vector', lambda e: e.tensor_scalar(
            msk[:], cidx[:], stt[:, 0:1], None, ALU.is_equal))
        em.op('vector', lambda e: e.tensor_scalar(
            mmtb[:], scor[:], BIGC, None, ALU.add))
        em.op('vector', lambda e: e.tensor_tensor(
            mmtb[:], mmtb[:], msk[:], ALU.mult))
        em.op('vector', lambda e: e.tensor_scalar(
            mmtb[:], mmtb[:], -BIGC, None, ALU.add))
        em.op('vector', lambda e: e.tensor_reduce(
            stt[:, 1:2], mmtb[:], AX.X, ALU.max))
        em.op('vector', lambda e: e.tensor_scalar(
            scor[:], mmtb[:], stt[:, 1:2], None, ALU.is_ge))
        em.op('vector', lambda e: e.tensor_tensor(
            scor[:], scor[:], iot64m[:], ALU.mult))
        em.op('vector', lambda e: e.tensor_scalar(
            scor[:], scor[:], BIGC, None, ALU.add))
        em.op('vector', lambda e: e.tensor_reduce(
            stt[:, 2:3], scor[:], AX.X, ALU.min))
        cohv = em.op('vector', lambda e: e.tensor_scalar(
            ohtb[:], iot64[:], stt[:, 2:3], None, ALU.is_equal), inc='ve')
        em.wait('gpsimd', 've', cohv)
        cdo = em.op('gpsimd', lambda e, tb=tb: e.dma_start(
            out=OH[tb * 128:(tb + 1) * 128, :], in_=ohtb[:]),
            inc='outo', inc_by=16)
        em.op('vector', lambda e: e.memset(xsl_sb[:], 0.0))
        for i in range(NUM_TILES):
            em.op('vector', lambda e, i=i: e.scalar_tensor_tensor(
                xsl_sb[:], t2tb[:, i * D_SLICE:(i + 1) * D_SLICE],
                ohtb[:, i:i + 1], xsl_sb[:], ALU.mult, ALU.add))
        cxs = em.op('vector', lambda e: e.tensor_copy(
            xsl_sb[:], xsl_sb[:]), inc='ve')
        em.wait('gpsimd', 've', cxs)
        cds = em.op('gpsimd', lambda e, tb=tb: e.dma_start(
            out=XSL[tb * 128:(tb + 1) * 128, :], in_=xsl_sb[:]),
            inc='outs', inc_by=16)
        em.op('vector', lambda e: e.tensor_reduce(
            stt[:, 3:4], xsl_sb[:], AX.X, ALU.min))
        em.op('vector', lambda e: e.tensor_reduce(
            stt[:, 4:5], xsl_sb[:], AX.X, ALU.max))
        em.op('vector', lambda e: e.tensor_scalar(
            stt[:, 5:6], stt[:, 3:4], -BIGC, None, ALU.add))
        em.op('vector', lambda e: e.tensor_scalar(
            stt[:, 6:7], stt[:, 4:5], BIGC, None, ALU.add))
        cv1 = em.op('vector', lambda e: e.tensor_scalar(
            mmtb[:], ohtb[:], stt[:, 5:6], BIGC, ALU.mult, ALU.add),
            inc='ve')
        em.wait('tensor', 've', cv1)
        em.wait('tensor', 'sc')
        ct = em.op('tensor', lambda e: e.transpose(
            psm[:], mmtb[:, 0:64], ident_sb[:]), inc='te')
        em.wait('scalar', 'te', ct)
        cev = em.op('scalar', lambda e: e.activation(
            redsc[:], psm[:], AF.Copy), inc='sc')
        em.wait('vector', 'sc', cev)
        em.op('vector', lambda e: e.tensor_reduce(
            minacc[:, 2:3], redsc[:], AX.X, ALU.min))
        cv2 = em.op('vector', lambda e: e.tensor_tensor(
            minacc[:, 0:1], minacc[:, 0:1], minacc[:, 2:3], ALU.min),
            inc='ve')
        cv3 = em.op('vector', lambda e: e.tensor_scalar(
            mmtb[:], ohtb[:], stt[:, 6:7], -BIGC, ALU.mult, ALU.add),
            inc='ve')
        em.wait('tensor', 've', cv3)
        em.wait('tensor', 'sc')
        ct = em.op('tensor', lambda e: e.transpose(
            psm[:], mmtb[:, 0:64], ident_sb[:]), inc='te')
        em.wait('scalar', 'te', ct)
        cev = em.op('scalar', lambda e: e.activation(
            redsc[:], psm[:], AF.Copy), inc='sc')
        em.wait('vector', 'sc', cev)
        em.op('vector', lambda e: e.tensor_reduce(
            minacc[:, 3:4], redsc[:], AX.X, ALU.max))
        em.op('vector', lambda e: e.tensor_tensor(
            minacc[:, 1:2], minacc[:, 1:2], minacc[:, 3:4], ALU.max))
        em.wait('vector', 'outo', cdo)
        em.wait('vector', 'outs', cds)

    cfin = em.op('vector', lambda e: e.tensor_copy(
        minacc[:, 0:2], minacc[:, 0:2]), inc='ve')
    em.wait('gpsimd', 've', cfin)
    em.dma('gpsimd', MINP, minacc[:, 0:1], inc='outm')
    em.dma('gpsimd', MAXP, minacc[:, 1:2], inc='outm')
    em.wait('gpsimd', 'outm')
    em.wait('gpsimd', 'outx')



def build_launchB():
    nc = bass.Bass('TRN2', target_bir_lowering=False)
    io = declare_io_B(nc)
    stack = contextlib.ExitStack()
    with stack:
        em = Em(nc, stack)
        emit_B(nc, em, stack, io)
        em.finalize()
    return nc


def host_prep_launchB(inputs, qts, aparts):
    X = np.asarray(inputs['L0_attn_X'])[0]
    W_out = np.ascontiguousarray(np.asarray(inputs['L0_attn_W_out']))
    b_out = np.asarray(inputs['L0_attn_b_out']).reshape(1, D)
    gb2 = np.stack([np.asarray(inputs['L0_ffn_ln_gamma']),
                    np.asarray(inputs['L0_ffn_ln_beta'])], axis=0)
    slopes = np.ascontiguousarray(
        np.asarray(inputs['L0_ffn_spline_slopes']).reshape(NUM_TILES,
                                                           D_SLICE * GRID))
    ones = np.ones((128, 128), np.float32)
    ident = np.eye(128, dtype=np.float32)
    cvals = np.array([[EPS, -0.5, 0, 0, 0, 0, 0, 0]], np.float32)
    tab = np.zeros((3, 64), np.float32)
    tab[0] = np.arange(64)
    tab[1] = np.arange(64) // 8
    tab[2, 0:8] = np.arange(8)
    ap = np.stack(aparts, axis=0).reshape(NCORES, D)
    in_maps = []
    for c in range(NCORES):
        in_maps.append(dict(
            x0=np.ascontiguousarray(X[c * TPC:(c + 1) * TPC]),
            qt=qts[c], aparts=ap, wout=W_out, bout=b_out, gb2=gb2,
            slopes=slopes, ones=ones, ident=ident, cvals=cvals, tab=tab))
    return in_maps


def declare_io_C(nc, internal=()):
    io = {}
    io['x1'] = nc.dram_tensor('x1', [TPC, D], F32, kind=None if 'x1' in internal else 'ExternalInput').ap()
    io['oh'] = nc.dram_tensor('oh', [TPC, 64], F32, kind=None if 'oh' in internal else 'ExternalInput').ap()
    io['xsl'] = nc.dram_tensor('xsl', [TPC, 32], F32, kind=None if 'xsl' in internal else 'ExternalInput').ap()
    io['mins'] = nc.dram_tensor('mins', [64, NCORES], F32, kind=None if 'mins' in internal else 'ExternalInput').ap()
    io['maxs'] = nc.dram_tensor('maxs', [64, NCORES], F32, kind=None if 'maxs' in internal else 'ExternalInput').ap()
    io['bases'] = nc.dram_tensor('bases', [64, 512], F32, kind=None if 'bases' in internal else 'ExternalInput').ap()
    io['slopes'] = nc.dram_tensor('slopes', [64, 512], F32, kind=None if 'slopes' in internal else 'ExternalInput').ap()
    io['osc'] = nc.dram_tensor('osc', [64, 1], F32, kind=None if 'osc' in internal else 'ExternalInput').ap()
    io['gb1'] = nc.dram_tensor('gb1', [2, D], F32, kind=None if 'gb1' in internal else 'ExternalInput').ap()
    io['ident'] = nc.dram_tensor('ident', [128, 128], F32, kind=None if 'ident' in internal else 'ExternalInput').ap()
    io['cvals'] = nc.dram_tensor('cvals', [1, 8], F32, kind=None if 'cvals' in internal else 'ExternalInput').ap()
    io['x2'] = nc.dram_tensor('x2', [TPC, D], F32, kind=None if 'x2' in internal else 'ExternalOutput').ap()
    io['ut'] = nc.dram_tensor('ut', [D, TPC], BF16, kind=None if 'ut' in internal else 'ExternalOutput').ap()
    return io


def emit_C(nc, em, stack, io):
    X1 = io['x1']
    OH = io['oh']
    XSL = io['xsl']
    MINS = io.get('mins')
    MAXS = io.get('maxs')
    BASES = io['bases']
    SLOPES = io['slopes']
    OSC = io['osc']
    GB1 = io['gb1']
    IDENT = io['ident']
    CVALS = io['cvals']
    X2 = io['x2']
    UT = io['ut']
    sb = lambda name, shp, dt: stack.enter_context(
        nc.sbuf_tensor(name, shp, dt, allow_name_mangling=True))
    xtb = sb('xtb', [128, D], F32)
    t1tb = sb('t1tb', [128, D], F32)
    sqtb = sb('sqtb', [128, D], F32)
    stt = sb('stt', [128, 16], F32)
    ident_sb = sb('ident_sb', [128, 128], F32)
    cv_sb = sb('cv_sb', [128, 8], F32)
    gb_sb = sb('gb_sb', [128, 2 * NKB], F32)
    rhs = sb('rhs', [64, 1048], F32)
    ohtb = sb('ohtb', [128, 64], F32)
    ohT = sb('ohT', [64, 128], F32)
    xsl_sb = sb('xsl_sb', [128, 32], F32)
    btok = sb('btok', [128, 512], F32)
    stok = sb('stok', [128, 512], F32)
    mm3 = sb('mm3', [128, 3], F32)
    u_sb = sb('u_sb', [128, 32], F32)
    gi_sb = sb('gi_sb', [128, 32], F32)
    xl_sb = sb('xl_sb', [128, 32], F32)
    f_sb = sb('f_sb', [128, 512], F32)
    sel_sb = sb('sel_sb', [128, 32], F32)
    tmp32 = sb('tmp32', [128, 32], F32)
    dsb = sb('dsb', [128, 32], F32)
    utst = [sb('utst0', [128, 128], BF16), sb('utst1', [128, 128], BF16)]
    utgate = [0, 0]
    psm = stack.enter_context(nc.psum_tensor('psm', [64, 128], F32, allow_name_mangling=True))
    psg = stack.enter_context(nc.psum_tensor('psg', [128, 512], F32, allow_name_mangling=True))
    psh = stack.enter_context(nc.psum_tensor('psh', [128, 512], F32, allow_name_mangling=True))
    ps3 = stack.enter_context(nc.psum_tensor('ps3', [128, 3], F32, allow_name_mangling=True))
    ppt = PP(em, stack, nc, 'pst', [128, 128])

    for s in ['ldc', 'ldx', 'ldo', 'lds', 'outx', 'outu', 'sc', 've',
              'te']:
        em.sem(s)

    em.setup_consts([EPS, -0.5], cv_sb, CVALS)
    em.dma('sync', ident_sb[:], IDENT, inc='ldc')
    for i in range(2):
        for kb in range(NKB):
            em.dma('sync', gb_sb[:, i * NKB + kb:i * NKB + kb + 1],
                   GB1[i:i + 1, kb * 128:(kb + 1) * 128].rearrange(
                       'a p -> p a'), inc='ldc')
    em.dma('sync', rhs[:, 0:512], BASES, inc='ldc')
    em.dma('sync', rhs[:, 512:1024], SLOPES, inc='ldc')
    em.dma('sync', rhs[:, 1024:1025], OSC, inc='ldc')
    if 'minr' in io:
        em.dma('sync', rhs[:, 1025:1026], io['minr'], inc='ldc')
        em.dma('sync', rhs[:, 1026:1027], io['maxr'], inc='ldc')
        em.wait('vector', 'ldc')
        em.wait('scalar', 'ldc')
        em.wait('tensor', 'ldc')
        cmm = em.op('vector', lambda e: e.tensor_copy(
            rhs[:, 1025:1027], rhs[:, 1025:1027]), inc='ve')
    else:
        em.dma('sync', rhs[:, 1027:1027 + NCORES], MINS, inc='ldc')
        em.dma('sync', rhs[:, 1027 + NCORES:1027 + 2 * NCORES], MAXS,
               inc='ldc')
        em.wait('vector', 'ldc')
        em.wait('scalar', 'ldc')
        em.wait('tensor', 'ldc')
        em.op('vector', lambda e: e.tensor_reduce(
            rhs[:, 1025:1026], rhs[:, 1027:1027 + NCORES], AX.X, ALU.min))
        cmm = em.op('vector', lambda e: e.tensor_reduce(
            rhs[:, 1026:1027], rhs[:, 1027 + NCORES:1027 + 2 * NCORES],
            AX.X, ALU.max), inc='ve')
    em.wait('tensor', 've', cmm)

    for tb in range(NTB):
        r0, r1 = tb * 128, (tb + 1) * 128
        em.wait('sync', 'sc')
        em.wait('sync', 've')
        cx = em.dma('sync', xtb[:], X1[r0:r1, :], inc='ldx')
        co = em.dma('sync', ohtb[:], OH[r0:r1, :], inc='ldo')
        cs = em.dma('sync', xsl_sb[:], XSL[r0:r1, :], inc='lds')
        em.wait('scalar', 'ldx', cx)
        em.wait('vector', 'ldx', cx)
        emit_plain_ln(em, xtb[:], t1tb[:], stt, sqtb[:])
        em.wait('tensor', 'ldo', co)
        em.wait('tensor', 'sc')
        ct = em.op('tensor', lambda e: e.transpose(
            psm[:], ohtb[:, 0:64], ident_sb[:]), inc='te')
        em.wait('scalar', 'te', ct)
        cev = em.op('scalar', lambda e: e.activation(
            ohT[:], psm[:], AF.Copy), inc='sc')
        em.wait('tensor', 'sc', cev)
        cm1 = em.op('tensor', lambda e: e.matmul(
            psg[:], ohT[:], rhs[:, 0:512], start=True, stop=True),
            inc='te')
        cm2 = em.op('tensor', lambda e: e.matmul(
            psh[:], ohT[:], rhs[:, 512:1024], start=True, stop=True),
            inc='te')
        cm3 = em.op('tensor', lambda e: e.matmul(
            ps3[:], ohT[:], rhs[:, 1024:1027], start=True, stop=True),
            inc='te')
        em.wait('scalar', 'te', cm3)
        em.op('scalar', lambda e: e.activation(btok[:], psg[:], AF.Copy))
        em.op('scalar', lambda e: e.activation(stok[:], psh[:], AF.Copy))
        cev = em.op('scalar', lambda e: e.activation(
            mm3[:], ps3[:], AF.Copy), inc='sc')
        em.wait('vector', 'sc', cev)
        em.wait('vector', 'lds', cs)
        em.op('vector', lambda e: e.tensor_scalar(
            stt[:, 8:9], mm3[:, 1:2], -1.0, None, ALU.mult))
        em.op('vector', lambda e: e.scalar_tensor_tensor(
            stt[:, 9:10], mm3[:, 1:2], -1.0, mm3[:, 2:3], ALU.mult,
            ALU.add))
        em.op('vector', lambda e: e.tensor_scalar(
            stt[:, 9:10], stt[:, 9:10], 1e-8, None, ALU.add))
        em.op('vector', lambda e: e.reciprocal(
            stt[:, 10:11], stt[:, 9:10]))
        em.op('vector', lambda e: e.tensor_scalar(
            u_sb[:], xsl_sb[:], stt[:, 8:9], stt[:, 10:11], ALU.add,
            ALU.mult))
        em.op('vector', lambda e: e.tensor_scalar(
            u_sb[:], u_sb[:], 0.0, 1.0 - 1e-06, ALU.max, ALU.min))
        em.op('vector', lambda e: e.tensor_scalar(
            u_sb[:], u_sb[:], float(GRID), None, ALU.mult))
        em.op('vector', lambda e: e.tensor_scalar(
            gi_sb[:], u_sb[:], 1.0, None, ALU.is_ge))
        for kk in range(2, GRID):
            em.op('vector', lambda e, kk=kk: e.scalar_tensor_tensor(
                gi_sb[:], u_sb[:], float(kk), gi_sb[:], ALU.is_ge,
                ALU.add))
        em.op('vector', lambda e: e.tensor_tensor(
            xl_sb[:], u_sb[:], gi_sb[:], ALU.subtract))
        xlb = xl_sb[:].unsqueeze(2).broadcast_to([128, 32, GRID])
        em.op('vector', lambda e, xlb=xlb: e.tensor_tensor(
            f_sb[:].rearrange('p (j g) -> p j g', g=GRID),
            stok[:].rearrange('p (j g) -> p j g', g=GRID), xlb, ALU.mult))
        em.op('vector', lambda e: e.tensor_tensor(
            f_sb[:], f_sb[:], btok[:], ALU.add))
        em.op('vector', lambda e: e.memset(dsb[:], 0.0))
        for g in range(GRID):
            em.op('vector', lambda e, g=g: e.tensor_scalar(
                sel_sb[:], gi_sb[:], float(g), None, ALU.is_equal))
            fsl = f_sb[:].rearrange('p (j g) -> p g j', g=GRID)[:, g, :]
            em.op('vector', lambda e, fsl=fsl: e.tensor_tensor(
                tmp32[:], sel_sb[:], fsl, ALU.mult))
            em.op('vector', lambda e: e.tensor_tensor(
                dsb[:], dsb[:], tmp32[:], ALU.add))
        em.op('vector', lambda e: e.tensor_scalar(
            dsb[:], dsb[:], mm3[:, 0:1], None, ALU.mult))
        em.wait('vector', 'ldo', co)
        em.wait('vector', 'sc')
        for i in range(NUM_TILES):
            em.op('vector', lambda e, i=i: e.scalar_tensor_tensor(
                t1tb[:, i * D_SLICE:(i + 1) * D_SLICE], dsb[:],
                ohtb[:, i:i + 1], t1tb[:, i * D_SLICE:(i + 1) * D_SLICE],
                ALU.mult, ALU.add))
        emit_plain_ln(em, t1tb[:], t1tb[:], stt, sqtb[:])
        em.wait('vector', 'sc')
        cv = em.op('vector', lambda e: e.tensor_tensor(
            xtb[:], xtb[:], t1tb[:], ALU.add), inc='ve')
        em.wait('gpsimd', 've', cv)
        cdx = em.op('gpsimd', lambda e, r0=r0, r1=r1: e.dma_start(
            out=X2[r0:r1, :], in_=xtb[:]), inc='outx', inc_by=16)
        em.wait('scalar', 've', cv)
        c5 = emit_plain_ln(em, xtb[:], t1tb[:], stt, sqtb[:])
        em.wait('tensor', 'sc', c5)
        for kb in range(NKB):
            pstk = ppt.nxt(em)
            ct = em.op('tensor', lambda e, kb=kb, pstk=pstk: e.transpose(
                pstk[:], t1tb[:, kb * 128:(kb + 1) * 128], ident_sb[:]),
                inc='te')
            em.wait('scalar', 'te', ct)
            kp = kb % 2
            em.wait('scalar', 'outu', utgate[kp])
            cu = em.op('scalar', lambda e, kb=kb, pstk=pstk, kp=kp:
                       e.activation(
                utst[kp][:], pstk[:], AF.Identity,
                scale=gb_sb[:, kb:kb + 1],
                bias=gb_sb[:, NKB + kb:NKB + kb + 1]), inc='sc')
            ppt.done('sc', cu)
            em.wait('gpsimd', 'sc', cu)
            cdu = em.op('gpsimd', lambda e, kb=kb, r0=r0, kp=kp: e.dma_start(
                out=UT[kb * 128:(kb + 1) * 128, r0:r0 + 128],
                in_=utst[kp][:]), inc='outu', inc_by=16)
            utgate[kp] = cdu
        em.wait('sync', 'outx', cdx)

    em.wait('gpsimd', 'outx')
    em.wait('gpsimd', 'outu')



def build_launchC():
    nc = bass.Bass('TRN2', target_bir_lowering=False)
    io = declare_io_C(nc)
    stack = contextlib.ExitStack()
    with stack:
        em = Em(nc, stack)
        emit_C(nc, em, stack, io)
        em.finalize()
    return nc


def host_prep_launchC(inputs, x1s, ohs, xsls, minps, maxps):
    bases = np.ascontiguousarray(
        np.asarray(inputs['L0_ffn_spline_bases']).reshape(64, 512))
    slopes = np.ascontiguousarray(
        np.asarray(inputs['L0_ffn_spline_slopes']).reshape(64, 512))
    osc = np.asarray(inputs['L0_ffn_output_scale']).reshape(64, 1)
    gb1 = np.stack([np.asarray(inputs['L1_attn_ln1_w']),
                    np.asarray(inputs['L1_attn_ln1_b'])], axis=0)
    mins = np.ascontiguousarray(np.stack([m[:, 0] for m in minps], axis=1))
    maxs = np.ascontiguousarray(np.stack([m[:, 0] for m in maxps], axis=1))
    ident = np.eye(128, dtype=np.float32)
    cvals = np.array([[EPS, -0.5, 0, 0, 0, 0, 0, 0]], np.float32)
    in_maps = []
    for c in range(NCORES):
        in_maps.append(dict(
            x1=x1s[c], oh=ohs[c], xsl=xsls[c], mins=mins, maxs=maxs,
            bases=bases, slopes=slopes, osc=osc, gb1=gb1, ident=ident,
            cvals=cvals))
    return in_maps


TCH = 1024
NCH = T // TCH

def declare_io_D(nc, internal=()):
    io = {}
    io['utall'] = nc.dram_tensor('utall', [NCORES, D, TPC], BF16, kind=None if 'utall' in internal else 'ExternalInput').ap()
    io['wxmt'] = nc.dram_tensor('wxmt', [D, DPC], F32, kind=None if 'wxmt' in internal else 'ExternalInput').ap()
    io['wzt'] = nc.dram_tensor('wzt', [D, DPC], F32, kind=None if 'wzt' in internal else 'ExternalInput').ap()
    io['cw'] = nc.dram_tensor('cw', [4, 128, D_CONV], F32, kind=None if 'cw' in internal else 'ExternalInput').ap()
    io['cb'] = nc.dram_tensor('cb', [4, 128], F32, kind=None if 'cb' in internal else 'ExternalInput').ap()
    io['xpt'] = nc.dram_tensor('xpt', [DPC, 160], F32, kind=None if 'xpt' in internal else 'ExternalInput').ap()
    io['dtpt'] = nc.dram_tensor('dtpt', [DT_RANK, DPC], F32, kind=None if 'dtpt' in internal else 'ExternalInput').ap()
    io['dtb'] = nc.dram_tensor('dtb', [4, 128], F32, kind=None if 'dtb' in internal else 'ExternalInput').ap()
    io['alog'] = nc.dram_tensor('alog', [4, 128, D_STATE], F32, kind=None if 'alog' in internal else 'ExternalInput').ap()
    io['dsk'] = nc.dram_tensor('dsk', [4, 128], F32, kind=None if 'dsk' in internal else 'ExternalInput').ap()
    io['cvals'] = nc.dram_tensor('cvals', [1, 8], F32, kind=None if 'cvals' in internal else 'ExternalInput').ap()
    io['g'] = nc.dram_tensor('g', [NCORES, DPC, TPC], BF16, kind=None if 'g' in internal else 'ExternalOutput').ap()
    io['xcst'] = nc.dram_tensor('xcst', [DPC, T], F32).ap()
    io['zst'] = nc.dram_tensor('zst', [DPC, T], F32).ap()
    io['xdblp'] = nc.dram_tensor('xdblp', [160, T], F32).ap()
    io['xdbl'] = nc.dram_tensor('xdbl', [160, T], F32).ap()
    return io


def emit_D(nc, em, stack, io):
    XC = io['xcst']
    Z = io['zst']
    UTALL = io['utall']
    WXMT = io['wxmt']
    WZT = io['wzt']
    CW = io['cw']
    CB = io['cb']
    XPT = io['xpt']
    DTPT = io['dtpt']
    DTB = io['dtb']
    ALOG = io['alog']
    DSK = io['dsk']
    CVALS = io['cvals']
    G = io['g']
    XCST = io['xcst']
    ZST = io['zst']
    XDBLP = io['xdblp']
    XDBL = io['xdbl']
    sb = lambda name, shp, dt: stack.enter_context(
        nc.sbuf_tensor(name, shp, dt, allow_name_mangling=True))
    wxm = sb('wxm', [128, NKB * DPC], BF16)
    wz = sb('wz', [128, NKB * DPC], BF16)
    xpt_bf = sb('xpt_bf', [128, 4 * 160], BF16)
    dtpt_bf = sb('dtpt_bf', [128, DPC], BF16)
    wld = sb('wld', [128, DPC], F32)
    utch = sb('utch', [128, NKB * 512], BF16)
    ext = sb('ext', [128, 4 * 515], F32)
    xcsb = sb('xcsb', [128, 512], F32)
    xcbf = sb('xcbf', [128, 512], BF16)
    zsb = sb('zsb', [128, 512], F32)
    xdsb0 = sb('xdsb0', [128, 512], F32)
    xdsb1 = sb('xdsb1', [32, 512], F32)
    cw_sb = sb('cw_sb', [128, 16], F32)
    cb_sb = sb('cb_sb', [128, 4], F32)
    dtb_sb = sb('dtb_sb', [128, 4], F32)
    dsk_sb = sb('dsk_sb', [128, 4], F32)
    a_sb4 = sb('a_sb4', [128, 4 * D_STATE], F32)
    st_sb = sb('st_sb', [128, 4 * D_STATE], F32)
    cv_sb = sb('cv_sb', [128, 8], F32)
    dt_sb = sb('dt_sb', [128, 4 * TCH], F32)
    dtx_sb = sb('dtx_sb', [128, 4 * TCH], F32)
    xcch = sb('xcch', [128, 4 * TCH], F32)
    yacc = sb('yacc', [128, 4 * TCH], F32)
    asb = [sb('asb0', [128, TCH], F32), sb('asb1', [128, TCH], F32)]
    usb = sb('usb', [128, TCH], F32)
    wsb = [sb('wsb0', [128, TCH], F32), sb('wsb1', [128, TCH], F32)]
    hsb = sb('hsb', [128, TCH], F32)
    brB = [sb(f'brB{i}', [128, TCH], F32) for i in range(2)]
    brC = [sb(f'brC{i}', [128, TCH], F32) for i in range(2)]
    xdbb = sb('xdbb', [128, TCH], BF16)
    zch = sb('zch', [128, TCH], F32)
    gb16 = sb('gb16', [128, TCH], BF16)
    sp1 = sb('sp1', [128, 512], F32)
    sp2 = sb('sp2', [128, 512], F32)
    ppx = PP(em, stack, nc, 'psx', [128, 512])
    ppz = PP(em, stack, nc, 'psz', [128, 512])
    psd0 = stack.enter_context(nc.psum_tensor('psd0', [128, 512], F32, allow_name_mangling=True))
    psd1 = stack.enter_context(nc.psum_tensor('psd1', [32, 512], F32, allow_name_mangling=True))
    psdt = stack.enter_context(nc.psum_tensor('psdt', [128, 512], F32, allow_name_mangling=True))

    for s in ['ldc', 'ldw', 'ldu', 'ldxd', 'ldxc', 'ldz', 'ldbr0',
              'ldbr1', 'outxc', 'outz', 'outxd', 'outg', 'cc', 'bar',
              'po', 'sc', 've', 'te']:
        em.sem(s)

    em.setup_consts([EPS, -0.5, 1.0, -1.0], cv_sb, CVALS)
    for pb in range(4):
        for t_, srcap in [(cw_sb, None), ]:
            pass
    for pb in range(4):
        em.dma('sync', cw_sb[:, pb * 4:(pb + 1) * 4], CW[pb], inc='ldc')
        em.dma('sync', cb_sb[:, pb:pb + 1],
               CB[pb:pb + 1, :].rearrange('a p -> p a'), inc='ldc')
        em.dma('sync', dtb_sb[:, pb:pb + 1],
               DTB[pb:pb + 1, :].rearrange('a p -> p a'), inc='ldc')
        em.dma('sync', dsk_sb[:, pb:pb + 1],
               DSK[pb:pb + 1, :].rearrange('a p -> p a'), inc='ldc')
        em.dma('sync', a_sb4[:, pb * D_STATE:(pb + 1) * D_STATE],
               ALOG[pb], inc='ldc')
    em.wait('scalar', 'ldc')
    em.wait('vector', 'ldc')
    em.op('scalar', lambda e: e.activation(a_sb4[:], a_sb4[:], AF.Exp))
    em.wait('vector', 'sc')
    em.op('vector', lambda e: e.tensor_scalar(a_sb4[:], a_sb4[:], -1.0,
                                              None, ALU.mult))
    em.op('vector', lambda e: e.memset(st_sb[:], 0.0))
    em.op('vector', lambda e: e.memset(ext[:], 0.0))
    def loadw(dst, W, ncols):
        for kb in range(W.shape[0] // 128):
            c = em.dma('sync', wld[:, 0:ncols], W[kb * 128:(kb + 1) * 128, :],
                       inc='ldw')
            em.wait('scalar', 'ldw', c)
            c2 = em.op('scalar', lambda e, kb=kb, dst=dst, ncols=ncols:
                       e.activation(dst[:, kb * ncols:(kb + 1) * ncols],
                                    wld[:, 0:ncols], AF.Copy), inc='sc')
            em.wait('sync', 'sc', c2)
    loadw(wxm, WXMT, DPC)
    loadw(wz, WZT, DPC)
    loadw(xpt_bf, XPT, 160)
    loadw(dtpt_bf, DTPT, DPC)
    em.wait('tensor', 'sc')

    em.wait('scalar', 've')
    em.wait('tensor', 've')
    for tch in range(T // 512):
        j, w0 = tch // 4, (tch % 4) * 512
        for kb in range(NKB):
            em.dma('sync', utch[:, kb * 512:(kb + 1) * 512],
                   UTALL[j, kb * 128:(kb + 1) * 128, w0:w0 + 512],
                   inc='ldu')
        cu = em.cnt['ldu']
        em.wait('tensor', 'ldu', cu)
        for pb in range(4):
            parts = [(wxm[:, kb * DPC + pb * 128:kb * DPC + (pb + 1) * 128],
                      utch[:, kb * 512:(kb + 1) * 512])
                     for kb in range(NKB)]
            psx = ppx.nxt(em)
            cm = mm_accum(em, psx[:], parts, inc='te')
            em.wait('scalar', 'te', cm)
            cev = em.op('scalar', lambda e, pb=pb, psx=psx: e.activation(
                ext[:, pb * 515 + 3:(pb + 1) * 515], psx[:], AF.Copy),
                inc='sc')
            ppx.done('sc', cev)
            em.wait('vector', 'sc', cev)
            em.op('vector', lambda e, pb=pb: e.tensor_scalar(
                xcsb[:], ext[:, pb * 515:pb * 515 + 512],
                cw_sb[:, pb * 4:pb * 4 + 1], None, ALU.mult))
            for k in range(1, 4):
                em.op('vector', lambda e, pb=pb, k=k: e.scalar_tensor_tensor(
                    xcsb[:], ext[:, pb * 515 + k:pb * 515 + k + 512],
                    cw_sb[:, pb * 4 + k:pb * 4 + k + 1], xcsb[:],
                    ALU.mult, ALU.add))
            cvt = em.op('vector', lambda e, pb=pb: e.tensor_copy(
                ext[:, pb * 515:pb * 515 + 3],
                ext[:, pb * 515 + 512:pb * 515 + 515]), inc='ve')
            em.wait('scalar', 've', cvt)
            csl = em.op('scalar', lambda e, pb=pb: e.activation(
                xcsb[:], xcsb[:], AF.Silu, bias=cb_sb[:, pb:pb + 1]),
                inc='sc')
            em.wait('gpsimd', 'sc', csl)
            cdx = em.op('gpsimd', lambda e, pb=pb, tch=tch: e.dma_start(
                out=XC[pb * 128:(pb + 1) * 128, tch * 512:(tch + 1) * 512],
                in_=xcsb[:]), inc='outxc', inc_by=16)
            cbf = em.op('scalar', lambda e: e.activation(
                xcbf[:], xcsb[:], AF.Copy), inc='sc')
            em.wait('tensor', 'sc', cbf)
            em.op('tensor', lambda e, pb=pb: e.matmul(
                psd0[:], xpt_bf[:, pb * 160:pb * 160 + 128], xcbf[:],
                start=(pb == 0), stop=(pb == 3), skip_group_check=True))
            cm2 = em.op('tensor', lambda e, pb=pb: e.matmul(
                psd1[:], xpt_bf[:, pb * 160 + 128:pb * 160 + 160], xcbf[:],
                start=(pb == 0), stop=(pb == 3), skip_group_check=True),
                inc='te')
            em.wait('scalar', 'outxc', cdx)
            em.wait('vector', 'outxc', cdx)
            parts = [(wz[:, kb * DPC + pb * 128:kb * DPC + (pb + 1) * 128],
                      utch[:, kb * 512:(kb + 1) * 512])
                     for kb in range(NKB)]
            psz = ppz.nxt(em)
            cmz = mm_accum(em, psz[:], parts, inc='te')
            em.wait('scalar', 'te', cmz)
            cez = em.op('scalar', lambda e, psz=psz: e.activation(
                zsb[:], psz[:], AF.Copy), inc='sc')
            ppz.done('sc', cez)
            em.wait('gpsimd', 'sc', cez)
            cdz = em.op('gpsimd', lambda e, pb=pb, tch=tch: e.dma_start(
                out=Z[pb * 128:(pb + 1) * 128, tch * 512:(tch + 1) * 512],
                in_=zsb[:]), inc='outz', inc_by=16)
            em.wait('scalar', 'outz', cdz)
        em.wait('scalar', 'te', cm2)
        em.op('scalar', lambda e: e.activation(
            xdsb0[:], psd0[:], AF.Copy))
        cxd = em.op('scalar', lambda e: e.activation(
            xdsb1[:], psd1[:], AF.Copy), inc='sc')
        em.wait('tensor', 'sc', cxd)
        em.wait('gpsimd', 'sc', cxd)
        em.op('gpsimd', lambda e, tch=tch: e.dma_start(
            out=XDBLP[0:128, tch * 512:(tch + 1) * 512], in_=xdsb0[:]),
            inc='outxd', inc_by=16)
        cdd = em.op('gpsimd', lambda e, tch=tch: e.dma_start(
            out=XDBLP[128:160, tch * 512:(tch + 1) * 512], in_=xdsb1[:]),
            inc='outxd', inc_by=16)
        em.wait('scalar', 'outxd', cdd)
        em.wait('sync', 'te')

    em.wait('gpsimd', 'outxd')
    em.coll('AllReduce', ALU.add, [XDBLP.opt()], [XDBL.opt()], 'cc')
    em.wait('gpsimd', 'cc', 1)
    em.op('gpsimd', lambda e: e.engine_nop(), inc='bar', inc_by=1)
    cgate = em.cnt['bar']
    em.wait('sync', 'bar', cgate)
    em.wait('scalar', 'bar', cgate)
    em.wait('vector', 'bar', cgate)
    em.wait('tensor', 'bar', cgate)

    for ch in range(NCH):
        c0 = ch * TCH
        cxd = em.dma('sync', sp1[:, 0:512], XDBL[0:128, c0:c0 + 512],
                     inc='ldxd')
        for half in range(2):
            if half == 1:
                cxd = em.dma('sync', sp1[:, 0:512],
                             XDBL[0:128, c0 + 512:c0 + 1024], inc='ldxd')
            em.wait('scalar', 'ldxd', cxd)
            ch2 = em.op('scalar', lambda e, half=half: e.activation(
                xdbb[:, half * 512:(half + 1) * 512], sp1[:, 0:512],
                AF.Copy), inc='sc')
            em.wait('sync', 'sc', ch2)
        em.wait('tensor', 'sc')
        em.wait('sync', 've')
        for pb in range(4):
            em.dma('sync', xcch[:, pb * TCH:(pb + 1) * TCH],
                   XC[pb * 128:(pb + 1) * 128, c0:c0 + TCH], inc='ldxc')
        cxcall = em.cnt['ldxc']
        for pb in range(4):
            for tsub in range(TCH // 512):
                cm = em.op('tensor', lambda e, pb=pb, tsub=tsub: e.matmul(
                    psdt[:], dtpt_bf[:, pb * 128:(pb + 1) * 128],
                    xdbb[:, tsub * 512:(tsub + 1) * 512],
                    start=True, stop=True), inc='te')
                em.wait('scalar', 'te', cm)
                em.wait('scalar', 've')
                em.op('scalar', lambda e, pb=pb: e.activation(
                    sp1[:, 0:512], psdt[:], AF.Identity,
                    bias=dtb_sb[:, pb:pb + 1]))
                em.op('scalar', lambda e: e.activation(
                    sp2[:, 0:512], sp1[:, 0:512], AF.Abs))
                em.op('scalar', lambda e: e.activation(
                    sp2[:, 0:512], sp2[:, 0:512], AF.Exp, scale=-1.0))
                em.op('scalar', lambda e: e.activation(
                    sp2[:, 0:512], sp2[:, 0:512], AF.Ln, bias=1.0))
                csr = em.op('scalar', lambda e: e.activation(
                    sp1[:, 0:512], sp1[:, 0:512], AF.Relu), inc='sc')
                em.wait('tensor', 'sc', csr)
                em.wait('vector', 'sc', csr)
                cdt = em.op('vector', lambda e, pb=pb, tsub=tsub:
                            e.tensor_tensor(
                    dt_sb[:, pb * TCH + tsub * 512:pb * TCH + (tsub + 1) * 512],
                    sp1[:, 0:512], sp2[:, 0:512], ALU.add), inc='ve')
            em.wait('vector', 'ldxc', cxcall)
            em.op('vector', lambda e, pb=pb: e.tensor_tensor(
                dtx_sb[:, pb * TCH:(pb + 1) * TCH],
                dt_sb[:, pb * TCH:(pb + 1) * TCH],
                xcch[:, pb * TCH:(pb + 1) * TCH], ALU.mult))
        em.wait('scalar', 've')
        asb_gate = [0, 0]
        brep_gate = [0, 0]
        wgate2 = [0, 0]
        for s in range(D_STATE):
            par = s % 2
            em.wait('sync', 've', brep_gate[par])
            cbr = em.dma('sync', brB[par][:],
                         XDBL[128 + s:129 + s, c0:c0 + TCH]
                         .partition_broadcast(128).squeeze(1),
                         inc=f'ldbr{par}')
            ccr = em.dma('sync', brC[par][:],
                         XDBL[144 + s:145 + s, c0:c0 + TCH]
                         .partition_broadcast(128).squeeze(1),
                         inc=f'ldbr{par}')
            em.wait('vector', f'ldbr{par}', ccr)
            for pb in range(4):
                col = pb * D_STATE + s
                it = s * 4 + pb
                ap2 = it % 2
                em.wait('scalar', 've', asb_gate[ap2])
                ca = em.op('scalar', lambda e, pb=pb, col=col, ap2=ap2:
                           e.activation(
                    asb[ap2][:], dt_sb[:, pb * TCH:(pb + 1) * TCH], AF.Exp,
                    scale=a_sb4[:, col:col + 1]), inc='sc')
                em.op('vector', lambda e, pb=pb, par=par: e.tensor_tensor(
                    usb[:], dtx_sb[:, pb * TCH:(pb + 1) * TCH],
                    brB[par][:], ALU.mult))
                em.wait('vector', 'sc', ca)
                em.op('vector', lambda e, col=col, ap2=ap2:
                      e.tensor_tensor_scan(
                    hsb[:], asb[ap2][:], usb[:], st_sb[:, col:col + 1],
                    ALU.mult, ALU.add))
                asb_gate[ap2] = em.cnt['ve']
                em.op('vector', lambda e, col=col: e.tensor_copy(
                    st_sb[:, col:col + 1], hsb[:, TCH - 1:TCH]))
                em.wait('vector', 'po', wgate2[ap2])
                cy = em.op('vector', lambda e, par=par, ap2=ap2:
                           e.tensor_tensor(
                    wsb[ap2][:], hsb[:], brC[par][:], ALU.mult), inc='ve')
                em.wait('gpsimd', 'po')
                em.wait('gpsimd', 've', cy)
                sfirst = (s == 0)
                if sfirst:
                    cp = em.op('gpsimd', lambda e, pb=pb, ap2=ap2:
                               e.tensor_copy(
                        yacc[:, pb * TCH:(pb + 1) * TCH], wsb[ap2][:]),
                        inc='po')
                else:
                    cp = em.op('gpsimd', lambda e, pb=pb, ap2=ap2:
                               e.tensor_tensor(
                        yacc[:, pb * TCH:(pb + 1) * TCH],
                        yacc[:, pb * TCH:(pb + 1) * TCH], wsb[ap2][:],
                        ALU.add), inc='po')
                wgate2[ap2] = cp
            brep_gate[par] = em.cnt['ve']
        em.wait('vector', 'po')
        for pb in range(4):
            cz = em.dma('sync', zch[:], Z[pb * 128:(pb + 1) * 128,
                                          c0:c0 + TCH], inc='ldz')
            em.wait('scalar', 'ldz', cz)
            csz = em.op('scalar', lambda e: e.activation(
                zch[:], zch[:], AF.Silu), inc='sc')
            em.wait('vector', 'sc', csz)
            em.op('vector', lambda e, pb=pb: e.tensor_scalar(
                usb[:], xcch[:, pb * TCH:(pb + 1) * TCH],
                dsk_sb[:, pb:pb + 1], None, ALU.mult))
            em.op('vector', lambda e, pb=pb: e.tensor_tensor(
                usb[:], usb[:], yacc[:, pb * TCH:(pb + 1) * TCH], ALU.add))
            cg = em.op('vector', lambda e: e.tensor_tensor(
                usb[:], usb[:], zch[:], ALU.mult), inc='ve')
            em.wait('scalar', 've', cg)
            cgb = em.op('scalar', lambda e: e.activation(
                gb16[:], usb[:], AF.Copy), inc='sc')
            em.wait('gpsimd', 'sc', cgb)
            cdg = em.op('gpsimd', lambda e, ch=ch, pb=pb: e.dma_start(
                out=G[ch // 2, pb * 128:(pb + 1) * 128,
                      (ch % 2) * TCH:(ch % 2 + 1) * TCH], in_=gb16[:]),
                inc='outg', inc_by=16)
            em.wait('scalar', 'outg', cdg)
            em.wait('sync', 've')
    em.wait('gpsimd', 'outg')



def build_launchD():
    nc = bass.Bass('TRN2', target_bir_lowering=False)
    io = declare_io_D(nc)
    stack = contextlib.ExitStack()
    with stack:
        em = Em(nc, stack)
        emit_D(nc, em, stack, io)
        em.finalize()
    return nc


def host_prep_launchD(inputs, uts):
    utall = np.ascontiguousarray(np.stack(uts, axis=0))
    ipw = np.asarray(inputs['L1_attn_in_proj_w'])
    cw = np.asarray(inputs['L1_attn_conv_w'])[:, 0, :]
    cb = np.asarray(inputs['L1_attn_conv_b'])
    xpw = np.asarray(inputs['L1_attn_x_proj_w'])
    dpw = np.asarray(inputs['L1_attn_dt_proj_w'])
    dpb = np.asarray(inputs['L1_attn_dt_proj_b'])
    alog = np.asarray(inputs['L1_attn_A_log'])
    dsk = np.asarray(inputs['L1_attn_D_skip'])
    cvals = np.array([[EPS, -0.5, 1.0, -1.0, 0, 0, 0, 0]], np.float32)
    in_maps = []
    for c in range(NCORES):
        s0, s1 = c * DPC, (c + 1) * DPC
        in_maps.append(dict(
            utall=utall,
            wxmt=np.ascontiguousarray(ipw[s0:s1, :].T),
            wzt=np.ascontiguousarray(ipw[4096 + s0:4096 + s1, :].T),
            cw=np.ascontiguousarray(cw[s0:s1].reshape(4, 128, D_CONV)),
            cb=np.ascontiguousarray(cb[s0:s1].reshape(4, 128)),
            xpt=np.ascontiguousarray(xpw[:, s0:s1].T),
            dtpt=np.ascontiguousarray(dpw[s0:s1, :].T),
            dtb=np.ascontiguousarray(dpb[s0:s1].reshape(4, 128)),
            alog=np.ascontiguousarray(alog[s0:s1].reshape(4, 128, D_STATE)),
            dsk=np.ascontiguousarray(dsk[s0:s1].reshape(4, 128)),
            cvals=cvals))
    return in_maps


def declare_io_E(nc, internal=()):
    io = {}
    io['gfull'] = nc.dram_tensor('gfull', [D_INNER, TPC], BF16, kind=None if 'gfull' in internal else 'ExternalInput').ap()
    io['x2'] = nc.dram_tensor('x2', [TPC, D], F32, kind=None if 'x2' in internal else 'ExternalInput').ap()
    io['opt'] = nc.dram_tensor('opt', [D_INNER, D], F32, kind=None if 'opt' in internal else 'ExternalInput').ap()
    io['gb2l'] = nc.dram_tensor('gb2l', [2, D], F32, kind=None if 'gb2l' in internal else 'ExternalInput').ap()
    io['fpt'] = nc.dram_tensor('fpt', [D, HIDDEN], F32, kind=None if 'fpt' in internal else 'ExternalInput').ap()
    io['fot'] = nc.dram_tensor('fot', [D_INNER, D], F32, kind=None if 'fot' in internal else 'ExternalInput').ap()
    io['ident'] = nc.dram_tensor('ident', [128, 128], F32, kind=None if 'ident' in internal else 'ExternalInput').ap()
    io['cvals'] = nc.dram_tensor('cvals', [1, 8], F32, kind=None if 'cvals' in internal else 'ExternalInput').ap()
    io['x3'] = nc.dram_tensor('x3', [TPC, D], F32, kind=None if 'x3' in internal else 'ExternalOutput').ap()
    io['mix'] = nc.dram_tensor('mix', [TPC, D], F32).ap()
    io['ares'] = nc.dram_tensor('ares', [TPC, D], F32).ap()
    io['ffa'] = nc.dram_tensor('ffa', [D_INNER, TPC], BF16).ap()
    io['t6'] = nc.dram_tensor('t6', [TPC, D], F32).ap()

    return io


def emit_E(nc, em, stack, io):
    NH = 8
    WCH = D // NH
    GFULL = io['gfull']
    X2 = io['x2']
    OPT = io['opt']
    GB2L = io['gb2l']
    FPT = io['fpt']
    FOT = io['fot']
    IDENT = io['ident']
    CVALS = io['cvals']
    X3 = io['x3']
    MIX = io['mix']
    ARES = io['ares']
    FFA = io['ffa']
    T6 = io['t6']
    sb = lambda name, shp, dt: stack.enter_context(
        nc.sbuf_tensor(name, shp, dt, allow_name_mangling=True))
    big = sb('big', [128, 32 * TPC], BF16)
    warena = [sb('warena0', [128, 32 * WCH], BF16),
              sb('warena1', [128, 32 * WCH], BF16)]
    fa_bf = [sb('fa_bf0', [128, NKB * 128], BF16)] * 2
    fg_bf = [sb('fg_bf0', [128, NKB * 128], BF16)] * 2
    wlda = sb('wlda', [128, 16 * WCH], F32)
    wldg = sb('wldg', [128, 16 * WCH], F32)
    t1tb = wldg[:, 0:D]
    sq2 = wlda[:, 0:D]
    xtb = wlda[:, D:2 * D] if 16 * WCH >= 2 * D else None
    assert 16 * WCH >= 2 * D
    stt = sb('stt', [128, 16], F32)
    ident_sb = sb('ident_sb', [128, 128], F32)
    cv_sb = sb('cv_sb', [128, 8], F32)
    gb_sb = sb('gb_sb', [128, 2 * NKB], F32)
    sg_sb = sb('sg_sb', [128, 512], F32)
    ffat = sb('ffat', [128, 512], BF16)
    mixt = sg_sb[:, 0:WCH]
    arch = sg_sb[:, WCH:2 * WCH]
    ppy = PP(em, stack, nc, 'psy', [128, WCH])
    ppa = PP(em, stack, nc, 'psa', [128, 512])
    ppg = PP(em, stack, nc, 'psg', [128, 512])
    ppt = PP(em, stack, nc, 'pst', [128, 128])

    for s in ['ldc', 'ldg', 'ldw', 'ldwg', 'ldx', 'ldm', 'lda', 'ldf',
              'outm', 'outa', 'outf', 'outt', 'outx', 'sc', 've', 'te']:
        em.sem(s)

    em.setup_consts([EPS, -0.5], cv_sb, CVALS)
    em.dma('sync', ident_sb[:], IDENT, inc='ldc')
    for i in range(2):
        for kb in range(NKB):
            em.dma('sync', gb_sb[:, i * NKB + kb:i * NKB + kb + 1],
                   GB2L[i:i + 1, kb * 128:(kb + 1) * 128].rearrange(
                       'a p -> p a'), inc='ldc')
    em.wait('scalar', 'ldc')
    em.wait('vector', 'ldc')
    em.wait('tensor', 'ldc')

    for kb in range(32):
        em.dma('sync', big[:, kb * TPC:(kb + 1) * TPC],
               GFULL[kb * 128:(kb + 1) * 128, :], inc='ldg')
    cg = em.cnt['ldg']
    em.wait('tensor', 'ldg', cg)

    wgate = [0, 0]
    def load_chunk(W, nh, par):
        em.wait('scalar', 'te', wgate[par])
        srcw = W[:, nh * WCH:(nh + 1) * WCH].rearrange(
            '(kb p) c -> p kb c', p=128)
        half = 16 * WCH
        cwa = em.dma('sync',
                     wlda[:].rearrange('p (kb c) -> p kb c', kb=16),
                     srcw[:, 0:16, :], inc='ldw')
        cwg = em.dma('sync',
                     wldg[:].rearrange('p (kb c) -> p kb c', kb=16),
                     srcw[:, 16:32, :], inc='ldwg')
        em.wait('scalar', 'ldw', cwa)
        em.op('scalar', lambda e, par=par: e.activation(
            warena[par][:, 0:half], wlda[:], AF.Copy), inc='sc')
        em.wait('scalar', 'ldwg', cwg)
        c2 = em.op('scalar', lambda e, par=par, half=half: e.activation(
            warena[par][:, half:2 * half], wldg[:], AF.Copy), inc='sc')
        em.wait('sync', 'sc', c2)
        return em.cnt['sc']

    for nh in range(NH):
        par = nh % 2
        cwl = load_chunk(OPT, nh, par)
        em.wait('tensor', 'sc', cwl)
        for tb in range(NTB):
            parts = [(big[:, kb * TPC + tb * 128:kb * TPC + (tb + 1) * 128],
                      warena[par][:, kb * WCH:(kb + 1) * WCH])
                     for kb in range(32)]
            psy = ppy.nxt(em)
            cm = mm_accum(em, psy[:], parts, inc='te')
            if tb == NTB - 1:
                wgate[par] = cm
            em.wait('scalar', 'te', cm)
            ce = em.op('scalar', lambda e, psy=psy: e.activation(
                mixt[:], psy[:], AF.Copy), inc='sc')
            ppy.done('sc', ce)
            em.wait('gpsimd', 'sc', ce)
            cd = em.op('gpsimd', lambda e, tb=tb, nh=nh: e.dma_start(
                out=MIX[tb * 128:(tb + 1) * 128, nh * WCH:(nh + 1) * WCH],
                in_=mixt[:]), inc='outm', inc_by=16)
            em.wait('scalar', 'outm', cd)

    em.wait('scalar', 'te')
    em.wait('sync', 'te')
    for tb in range(NTB):
        cx = em.dma('sync', xtb[:], X2[tb * 128:(tb + 1) * 128, :],
                    inc='ldx')
        cmx = em.dma('sync', t1tb[:], MIX[tb * 128:(tb + 1) * 128, :],
                     inc='ldm')
        em.wait('scalar', 'ldx', cx)
        em.wait('vector', 'ldx', cx)
        emit_plain_ln(em, xtb[:], xtb[:], stt, sq2[:])
        em.wait('vector', 'sc')
        em.wait('vector', 'ldm', cmx)
        cv = em.op('vector', lambda e: e.tensor_tensor(
            xtb[:], xtb[:], t1tb[:], ALU.add), inc='ve')
        em.wait('gpsimd', 've', cv)
        cda = em.op('gpsimd', lambda e, tb=tb: e.dma_start(
            out=ARES[tb * 128:(tb + 1) * 128, :], in_=xtb[:]),
            inc='outa', inc_by=16)
        em.wait('scalar', 've', cv)
        c2 = emit_plain_ln(em, xtb[:], t1tb[:], stt, sq2[:])
        em.wait('tensor', 'sc', c2)
        for kb in range(NKB):
            pstk = ppt.nxt(em)
            ct = em.op('tensor', lambda e, kb=kb, pstk=pstk: e.transpose(
                pstk[:], t1tb[:, kb * 128:(kb + 1) * 128], ident_sb[:]),
                inc='te')
            em.wait('scalar', 'te', ct)
            ce2 = em.op('scalar', lambda e, kb=kb, tb=tb, pstk=pstk:
                        e.activation(
                big[:, kb * TPC + tb * 128:kb * TPC + (tb + 1) * 128],
                pstk[:], AF.Identity, scale=gb_sb[:, kb:kb + 1],
                bias=gb_sb[:, NKB + kb:NKB + kb + 1]), inc='sc')
            ppt.done('sc', ce2)
        em.wait('sync', 'sc')
        em.wait('sync', 'outa', cda)

    em.wait('sync', 'sc')
    for cb in range(32):
        jp = 0
        em.wait('scalar', 'te')
        src_a = FPT[:, cb * 128:(cb + 1) * 128].rearrange(
            '(kb p) c -> p kb c', p=128)
        src_g = FPT[:, 4096 + cb * 128:4096 + (cb + 1) * 128].rearrange(
            '(kb p) c -> p kb c', p=128)
        cwa = em.dma('sync',
                     wlda[:, 0:NKB * 128].rearrange('p (kb c) -> p kb c',
                                                    kb=NKB), src_a, inc='ldw')
        cwg = em.dma('sync',
                     wldg[:, 0:NKB * 128].rearrange('p (kb c) -> p kb c',
                                                    kb=NKB), src_g,
                     inc='ldwg')
        em.wait('scalar', 'ldw', cwa)
        em.op('scalar', lambda e, jp=jp: e.activation(
            fa_bf[jp][:], wlda[:, 0:NKB * 128], AF.Copy), inc='sc')
        em.wait('scalar', 'ldwg', cwg)
        c2 = em.op('scalar', lambda e, jp=jp: e.activation(
            fg_bf[jp][:], wldg[:, 0:NKB * 128], AF.Copy), inc='sc')
        em.wait('sync', 'sc', c2)
        em.wait('tensor', 'sc', c2)
        for tch in range(4):
            parts = [(fa_bf[jp][:, kb * 128:(kb + 1) * 128],
                      big[:, kb * TPC + tch * 512:kb * TPC + (tch + 1) * 512])
                     for kb in range(NKB)]
            psa = ppa.nxt(em)
            mm_accum(em, psa[:], parts, inc='te')
            parts = [(fg_bf[jp][:, kb * 128:(kb + 1) * 128],
                      big[:, kb * TPC + tch * 512:kb * TPC + (tch + 1) * 512])
                     for kb in range(NKB)]
            psg = ppg.nxt(em)
            cm = mm_accum(em, psg[:], parts, inc='te')
            if tch == 3:
                wgate[jp] = cm
            em.wait('scalar', 'te', cm)
            cs = em.op('scalar', lambda e, psg=psg: e.activation(
                sg_sb[:], psg[:], AF.Silu), inc='sc')
            ppg.done('sc', cs)
            em.wait('vector', 'sc', cs)
            cv = em.op('vector', lambda e, psa=psa: e.tensor_tensor(
                ffat[:], sg_sb[:], psa[:], ALU.mult), inc='ve')
            ppa.done('ve', cv)
            em.wait('gpsimd', 've', cv)
            cd = em.op('gpsimd', lambda e, cb=cb, tch=tch: e.dma_start(
                out=FFA[cb * 128:(cb + 1) * 128,
                        tch * 512:(tch + 1) * 512], in_=ffat[:]),
                inc='outf', inc_by=16)
            em.wait('vector', 'outf', cd)

    em.wait('sync', 'te')
    em.wait('sync', 'outf')
    for kb in range(32):
        em.dma('sync', big[:, kb * TPC:(kb + 1) * TPC],
               FFA[kb * 128:(kb + 1) * 128, :], inc='ldf')
    cf = em.cnt['ldf']
    em.wait('tensor', 'ldf', cf)
    for nh in range(NH):
        par = nh % 2
        cwl = load_chunk(FOT, nh, par)
        em.wait('tensor', 'sc', cwl)
        for tb in range(NTB):
            parts = [(big[:, kb * TPC + tb * 128:kb * TPC + (tb + 1) * 128],
                      warena[par][:, kb * WCH:(kb + 1) * WCH])
                     for kb in range(32)]
            psy = ppy.nxt(em)
            cm = mm_accum(em, psy[:], parts, inc='te')
            if tb == NTB - 1:
                wgate[par] = cm
            ca = em.dma('sync', arch[:],
                        ARES[tb * 128:(tb + 1) * 128,
                             nh * WCH:(nh + 1) * WCH], inc='lda')
            em.wait('scalar', 'te', cm)
            ce = em.op('scalar', lambda e, psy=psy: e.activation(
                mixt[:], psy[:], AF.Copy), inc='sc')
            ppy.done('sc', ce)
            em.wait('vector', 'sc', ce)
            em.wait('vector', 'lda', ca)
            cv = em.op('vector', lambda e: e.tensor_tensor(
                mixt[:], mixt[:], arch[:], ALU.add), inc='ve')
            em.wait('gpsimd', 've', cv)
            cd = em.op('gpsimd', lambda e, tb=tb, nh=nh: e.dma_start(
                out=T6[tb * 128:(tb + 1) * 128, nh * WCH:(nh + 1) * WCH],
                in_=mixt[:]), inc='outt', inc_by=16)
            em.wait('scalar', 'outt', cd)
            em.wait('vector', 'outt', cd)
            em.wait('sync', 've', cv)

    for tb in range(NTB):
        em.wait('sync', 'sc')
        em.wait('sync', 've')
        cx = em.dma('sync', xtb[:], X2[tb * 128:(tb + 1) * 128, :],
                    inc='ldx')
        ct6 = em.dma('sync', t1tb[:], T6[tb * 128:(tb + 1) * 128, :],
                     inc='ldm')
        em.wait('scalar', 'ldm', ct6)
        em.wait('vector', 'ldm', ct6)
        emit_plain_ln(em, t1tb[:], t1tb[:], stt, sq2[:])
        em.wait('vector', 'sc')
        em.wait('vector', 'ldx', cx)
        cv = em.op('vector', lambda e: e.tensor_tensor(
            xtb[:], xtb[:], t1tb[:], ALU.add), inc='ve')
        em.wait('gpsimd', 've', cv)
        cd = em.op('gpsimd', lambda e, tb=tb: e.dma_start(
            out=X3[tb * 128:(tb + 1) * 128, :], in_=xtb[:]),
            inc='outx', inc_by=16)
        em.wait('sync', 'outx', cd)
    em.wait('gpsimd', 'outx')



def build_launchE():
    nc = bass.Bass('TRN2', target_bir_lowering=False)
    io = declare_io_E(nc)
    stack = contextlib.ExitStack()
    with stack:
        em = Em(nc, stack)
        emit_E(nc, em, stack, io)
        em.finalize()
    return nc


def host_prep_launchE(inputs, gs, x2s):
    gfulls = []
    for j in range(NCORES):
        gfulls.append(np.ascontiguousarray(
            np.concatenate([gs[c][j] for c in range(NCORES)], axis=0)))
    opt = np.ascontiguousarray(np.asarray(inputs['L1_attn_out_proj_w']).T)
    gb2l = np.stack([np.asarray(inputs['L1_attn_ln2_w']),
                     np.asarray(inputs['L1_attn_ln2_b'])], axis=0)
    fpt = np.ascontiguousarray(np.asarray(inputs['L1_attn_ff_proj_w']).T)
    fot = np.ascontiguousarray(np.asarray(inputs['L1_attn_ff_out_w']).T)
    ident = np.eye(128, dtype=np.float32)
    cvals = np.array([[EPS, -0.5, 0, 0, 0, 0, 0, 0]], np.float32)
    in_maps = []
    for c in range(NCORES):
        in_maps.append(dict(gfull=gfulls[c], x2=x2s[c], opt=opt, gb2l=gb2l,
                            fpt=fpt, fot=fot, ident=ident, cvals=cvals))
    return in_maps


def declare_io_F(nc, internal=()):
    io = {}
    io['x3'] = nc.dram_tensor('x3', [TPC, D], F32, kind=None if 'x3' in internal else 'ExternalInput').ap()
    io['rw9'] = nc.dram_tensor('rw9', [D, 9], F32, kind=None if 'rw9' in internal else 'ExternalInput').ap()
    io['gupt'] = nc.dram_tensor('gupt', [E_MOE, D, 2 * I_MOE], F32, kind=None if 'gupt' in internal else 'ExternalInput').ap()
    io['dpt'] = nc.dram_tensor('dpt', [E_MOE, I_MOE, D], F32, kind=None if 'dpt' in internal else 'ExternalInput').ap()
    io['sgt'] = nc.dram_tensor('sgt', [D, I_SH], F32, kind=None if 'sgt' in internal else 'ExternalInput').ap()
    io['sut'] = nc.dram_tensor('sut', [D, I_SH], F32, kind=None if 'sut' in internal else 'ExternalInput').ap()
    io['sdt'] = nc.dram_tensor('sdt', [I_SH, D], F32, kind=None if 'sdt' in internal else 'ExternalInput').ap()
    io['ident'] = nc.dram_tensor('ident', [128, 128], F32, kind=None if 'ident' in internal else 'ExternalInput').ap()
    io['cvals'] = nc.dram_tensor('cvals', [1, 8], F32, kind=None if 'cvals' in internal else 'ExternalInput').ap()
    io['tab'] = nc.dram_tensor('tab', [2, 8], F32, kind=None if 'tab' in internal else 'ExternalInput').ap()
    io['x4'] = nc.dram_tensor('x4', [TPC, D], F32, kind=None if 'x4' in internal else 'ExternalOutput').ap()
    io['oacc'] = nc.dram_tensor('oacc', [TPC, D], F32).ap()
    io['shact'] = nc.dram_tensor('shact', [I_SH, TPC], BF16).ap()
    io['combt'] = nc.dram_tensor('combt', [E_MOE, TPC], F32).ap()

    return io


def emit_F(nc, em, stack, io):
    NJ = I_MOE // 128
    NJS = I_SH // 128
    X3 = io['x3']
    RW9 = io['rw9']
    GUPT = io['gupt']
    DPT = io['dpt']
    SGT = io['sgt']
    SUT = io['sut']
    SDT = io['sdt']
    IDENT = io['ident']
    CVALS = io['cvals']
    TAB = io['tab']
    X4 = io['x4']
    OACC = io['oacc']
    SHACT = io['shact']
    COMBT = io['combt']
    sb = lambda name, shp, dt: stack.enter_context(
        nc.sbuf_tensor(name, shp, dt, allow_name_mangling=True))
    big = sb('big', [128, NKB * TPC], BF16)
    act = sb('act', [128, NJ * TPC], BF16)
    fa_bf = [sb('fa_bf0', [128, NKB * 128], BF16),
             sb('fa_bf1', [128, NKB * 128], BF16)]
    fg_bf = [sb('fg_bf0', [128, NKB * 128], BF16),
             sb('fg_bf1', [128, NKB * 128], BF16)]
    shtb = sb('shtb', [128, NJS * 128], BF16)
    wld = sb('wld', [128, 512], F32)
    wldd = sb('wldd', [128, 6 * 512], F32)
    xtb = sb('xtb', [128, D], F32)
    t1tb = sb('t1tb', [128, D], F32)
    sq2 = sb('sq2', [128, D], F32)
    wlda = sq2
    wldg = t1tb
    stt = sb('stt', [128, 16], F32)
    ident_sb = sb('ident_sb', [128, 128], F32)
    cv_sb = sb('cv_sb', [128, 8], F32)
    rw_bf = sb('rw_bf', [128, NKB * 9], BF16)
    lg = sb('lg', [128, 16], F32)
    iot8 = sb('iot8', [128, 8], F32)
    iot8m = sb('iot8m', [128, 8], F32)
    comb = sb('comb', [128, 8], F32)
    combT = sb('combT', [8, TPC], F32)
    sg_all = sb('sg_all', [128, NTB], F32)
    cre = sb('cre', [128, 4 * 512], F32)
    sg_sb = sb('sg_sb', [128, 512], F32)
    au_sb = sb('au_sb', [128, 512], F32)
    act16 = sb('act16', [128, 512], BF16)
    otile = sb('otile', [128, 512], F32)
    oacc_t = sb('oacc_t', [128, 512], F32)
    psr = stack.enter_context(nc.psum_tensor('psr', [128, 9], F32, allow_name_mangling=True))
    ppa = PP(em, stack, nc, 'psa', [128, 512])
    ppg = PP(em, stack, nc, 'psg', [128, 512])
    ppd = PP(em, stack, nc, 'psd', [128, 512])
    ppt = PP(em, stack, nc, 'pst', None,
             banks=[ppg.ps[0][:, 0:128], ppa.ps[0][:, 0:128]])
    pst = ppg.ps[1]

    for s in ['ldc', 'ldx', 'ldw', 'ldwg', 'ldacc', 'ldsh', 'ldcre',
              'outacc', 'outsh', 'outx', 'outct', 'sc', 've', 'te']:
        em.sem(s)

    em.setup_consts([EPS, -0.5], cv_sb, CVALS)
    em.dma('sync', ident_sb[:], IDENT, inc='ldc')
    em.dma('sync', iot8[:], TAB[0:1, :].partition_broadcast(128).squeeze(1),
           inc='ldc')
    em.dma('sync', iot8m[:], TAB[1:2, :].partition_broadcast(128).squeeze(1),
           inc='ldc')
    for kb in range(NKB):
        cw = em.dma('sync', wld[:, 0:9], RW9[kb * 128:(kb + 1) * 128, :],
                    inc='ldw')
        em.wait('scalar', 'ldw', cw)
        c2 = em.op('scalar', lambda e, kb=kb: e.activation(
            rw_bf[:, kb * 9:(kb + 1) * 9], wld[:, 0:9], AF.Copy),
            inc='sc')
        em.wait('sync', 'sc', c2)
    em.wait('scalar', 'ldc')
    em.wait('vector', 'ldc')
    em.wait('tensor', 'ldc')
    em.wait('tensor', 'sc')

    for tb in range(NTB):
        cx = em.dma('sync', xtb[:], X3[tb * 128:(tb + 1) * 128, :],
                    inc='ldx')
        em.wait('scalar', 'ldx', cx)
        em.wait('vector', 'ldx', cx)
        c7 = emit_plain_ln(em, xtb[:], t1tb[:], stt, sq2[:])
        em.wait('tensor', 'sc', c7)
        for kb in range(NKB):
            pstk = ppt.nxt(em)
            ct = em.op('tensor', lambda e, kb=kb, pstk=pstk: e.transpose(
                pstk[:], t1tb[:, kb * 128:(kb + 1) * 128], ident_sb[:]),
                inc='te')
            em.wait('scalar', 'te', ct)
            ce2 = em.op('scalar', lambda e, kb=kb, tb=tb, pstk=pstk:
                        e.activation(
                big[:, kb * TPC + tb * 128:kb * TPC + (tb + 1) * 128],
                pstk[:], AF.Copy), inc='sc')
            ppt.done('sc', ce2)
        em.wait('tensor', 'sc')
        parts = [(big[:, kb * TPC + tb * 128:kb * TPC + (tb + 1) * 128],
                  rw_bf[:, kb * 9:(kb + 1) * 9]) for kb in range(NKB)]
        cm = mm_accum(em, psr[:], parts, inc='te')
        em.wait('scalar', 'te', cm)
        ce = em.op('scalar', lambda e: e.activation(
            lg[:, 0:9], psr[:], AF.Copy), inc='sc')
        em.wait('tensor', 'sc', ce)
        em.wait('vector', 'sc', ce)
        em.op('vector', lambda e: e.tensor_reduce(
            stt[:, 8:9], lg[:, 0:8], AX.X, ALU.max, negate=True))
        cs = em.op('vector', lambda e: e.tensor_copy(
            stt[:, 8:9], stt[:, 8:9]), inc='ve')
        em.wait('scalar', 've', cs)
        cex = em.op('scalar', lambda e: e.activation(
            lg[:, 0:8], lg[:, 0:8], AF.Exp, bias=stt[:, 8:9],
            accum_out=stt[:, 9:10]), inc='sc')
        em.wait('vector', 'sc', cex)
        em.op('vector', lambda e: e.reciprocal(stt[:, 10:11],
                                               stt[:, 9:10]))
        em.op('vector', lambda e: e.tensor_scalar(
            lg[:, 0:8], lg[:, 0:8], stt[:, 10:11], None, ALU.mult))
        em.op('vector', lambda e: e.tensor_reduce(
            stt[:, 11:12], lg[:, 0:8], AX.X, ALU.max))
        em.op('vector', lambda e: e.tensor_scalar(
            comb[:], lg[:, 0:8], stt[:, 11:12], None, ALU.is_ge))
        em.op('vector', lambda e: e.tensor_tensor(
            comb[:], comb[:], iot8m[:], ALU.mult))
        em.op('vector', lambda e: e.tensor_scalar(
            comb[:], comb[:], BIGC, None, ALU.add))
        em.op('vector', lambda e: e.tensor_reduce(
            stt[:, 12:13], comb[:], AX.X, ALU.min))
        em.op('vector', lambda e: e.tensor_scalar(
            comb[:], iot8[:], stt[:, 12:13], None, ALU.is_equal))
        em.op('vector', lambda e: e.tensor_tensor(
            sq2[:, 16:24], lg[:, 0:8], comb[:], ALU.mult))
        em.op('vector', lambda e: e.tensor_tensor(
            sq2[:, 16:24], lg[:, 0:8], sq2[:, 16:24], ALU.subtract))
        em.op('vector', lambda e: e.tensor_reduce(
            stt[:, 13:14], sq2[:, 16:24], AX.X, ALU.max))
        em.op('vector', lambda e: e.tensor_scalar(
            sq2[:, 0:8], sq2[:, 16:24], stt[:, 13:14], None, ALU.is_ge))
        em.op('vector', lambda e: e.tensor_tensor(
            sq2[:, 0:8], sq2[:, 0:8], iot8m[:], ALU.mult))
        em.op('vector', lambda e: e.tensor_scalar(
            sq2[:, 0:8], sq2[:, 0:8], BIGC, None, ALU.add))
        em.op('vector', lambda e: e.tensor_reduce(
            stt[:, 14:15], sq2[:, 0:8], AX.X, ALU.min))
        em.op('vector', lambda e: e.tensor_tensor(
            stt[:, 15:16], stt[:, 11:12], stt[:, 13:14], ALU.add))
        em.op('vector', lambda e: e.reciprocal(stt[:, 15:16],
                                               stt[:, 15:16]))
        em.op('vector', lambda e: e.tensor_tensor(
            stt[:, 11:12], stt[:, 11:12], stt[:, 15:16], ALU.mult))
        em.op('vector', lambda e: e.tensor_tensor(
            stt[:, 13:14], stt[:, 13:14], stt[:, 15:16], ALU.mult))
        em.op('vector', lambda e: e.tensor_scalar(
            comb[:], comb[:], stt[:, 11:12], None, ALU.mult))
        em.op('vector', lambda e: e.tensor_scalar(
            sq2[:, 8:16], iot8[:], stt[:, 14:15], None, ALU.is_equal))
        cv = em.op('vector', lambda e: e.scalar_tensor_tensor(
            comb[:], sq2[:, 8:16], stt[:, 13:14], comb[:], ALU.mult,
            ALU.add), inc='ve')
        em.wait('tensor', 've', cv)
        em.wait('tensor', 'sc')
        ct = em.op('tensor', lambda e: e.transpose(
            pst[0:8, 0:128], comb[:], ident_sb[:]), inc='te')
        em.wait('scalar', 'te', ct)
        em.op('scalar', lambda e, tb=tb: e.activation(
            combT[:, tb * 128:(tb + 1) * 128], pst[0:8, 0:128], AF.Copy),
            inc='sc')
        csg = em.op('scalar', lambda e, tb=tb: e.activation(
            sg_all[:, tb:tb + 1], lg[:, 8:9], AF.Sigmoid), inc='sc')
        em.wait('sync', 'sc', csg)
        em.wait('sync', 've')
    em.wait('gpsimd', 'sc')
    cct = em.dma('gpsimd', COMBT, combT[:], inc='outct')
    em.wait('sync', 'outct', cct)

    fggate = [0, 0]
    def load_pair(Wsrc, coloff, cb):
        pass
    for e in range(E_MOE):
        em.wait('sync', 've')
        ccre = em.dma('sync', cre[:],
                      COMBT[e:e + 1, :].partition_broadcast(128)
                      .squeeze(1), inc='ldcre')
        em.wait('vector', 'ldcre', ccre)
        for j in range(NJ):
            jp = j % 2
            em.wait('scalar', 'te', fggate[jp])
            src_a = GUPT[e, :, j * 128:(j + 1) * 128].rearrange(
                '(kb p) c -> p kb c', p=128)
            src_g = GUPT[e, :, I_MOE + j * 128:I_MOE + (j + 1) * 128].rearrange('(kb p) c -> p kb c', p=128)
            cwa = em.dma('sync', wlda[:].rearrange('p (kb c) -> p kb c',
                                                   kb=NKB), src_a,
                         inc='ldw')
            cwg = em.dma('sync', wldg[:].rearrange('p (kb c) -> p kb c',
                                                   kb=NKB), src_g,
                         inc='ldwg')
            em.wait('scalar', 'ldw', cwa)
            em.op('scalar', lambda e_, jp=jp: e_.activation(
                fa_bf[jp][:], wlda[:], AF.Copy), inc='sc')
            em.wait('scalar', 'ldwg', cwg)
            c2 = em.op('scalar', lambda e_, jp=jp: e_.activation(
                fg_bf[jp][:], wldg[:], AF.Copy), inc='sc')
            em.wait('sync', 'sc', c2)
            em.wait('tensor', 'sc', c2)
            for tch in range(4):
                parts = [(fa_bf[jp][:, kb * 128:(kb + 1) * 128],
                          big[:, kb * TPC + tch * 512:
                              kb * TPC + (tch + 1) * 512])
                         for kb in range(NKB)]
                psg = ppg.nxt(em)
                mm_accum(em, psg[:], parts, inc='te')
                parts = [(fg_bf[jp][:, kb * 128:(kb + 1) * 128],
                          big[:, kb * TPC + tch * 512:
                              kb * TPC + (tch + 1) * 512])
                         for kb in range(NKB)]
                psa = ppa.nxt(em)
                cm = mm_accum(em, psa[:], parts, inc='te')
                if tch == 3:
                    fggate[jp] = cm
                em.wait('scalar', 'te', cm)
                cs = em.op('scalar', lambda e_, psg=psg: e_.activation(
                    sg_sb[:], psg[:], AF.Silu), inc='sc')
                ppg.done('sc', cs)
                em.wait('vector', 'sc', cs)
                cv0 = em.op('vector', lambda e_, psa=psa: e_.tensor_tensor(
                    au_sb[:], sg_sb[:], psa[:], ALU.mult), inc='ve')
                ppa.done('ve', cv0)
                cv = em.op('vector', lambda e_, tch=tch: e_.tensor_tensor(
                    au_sb[:], au_sb[:], cre[:, tch * 512:(tch + 1) * 512],
                    ALU.mult), inc='ve')
                em.wait('scalar', 've', cv)
                ca = em.op('scalar', lambda e_, j=j, tch=tch: e_.activation(
                    act[:, j * TPC + tch * 512:j * TPC + (tch + 1) * 512],
                    au_sb[:], AF.Copy), inc='sc')
                em.wait('vector', 'sc', ca)
        em.wait('sync', 'outacc')
        em.wait('tensor', 'sc')
        for nh in range(4):
            em.wait('scalar', 'te')
            src_d = DPT[e, :, nh * 512:(nh + 1) * 512].rearrange(
                '(j p) c -> p j c', p=128)
            for j0 in range(0, NJ, 6):
                nj = min(6, NJ - j0)
                cw = em.dma('sync',
                            wldd[:, 0:nj * 512].rearrange('p (j c) -> p j c',
                                                          j=nj),
                            src_d[:, j0:j0 + nj, :], inc='ldw')
                em.wait('scalar', 'ldw', cw)
                c2 = em.op('scalar', lambda e_, j0=j0, nj=nj: e_.activation(
                    shtb[:, j0 * 512:(j0 + nj) * 512], wldd[:, 0:nj * 512],
                    AF.Copy), inc='sc')
                em.wait('sync', 'sc', c2)
            em.wait('tensor', 'sc')
            for tb in range(NTB):
                parts = [(act[:, j * TPC + tb * 128:j * TPC + (tb + 1) * 128],
                          shtb[:, j * 512:(j + 1) * 512])
                         for j in range(NJ)]
                psd = ppd.nxt(em)
                cm = mm_accum(em, psd[:], parts, inc='te')
                em.wait('scalar', 'te', cm)
                ce = em.op('scalar', lambda e_, psd=psd: e_.activation(
                    otile[:], psd[:], AF.Copy), inc='sc')
                ppd.done('sc', ce)
                if e == 0:
                    em.wait('gpsimd', 'sc', ce)
                    cd = em.op('gpsimd', lambda e_, tb=tb, nh=nh:
                               e_.dma_start(
                        out=OACC[tb * 128:(tb + 1) * 128,
                                 nh * 512:(nh + 1) * 512], in_=otile[:]),
                        inc='outacc', inc_by=16)
                    em.wait('scalar', 'outacc', cd)
                else:
                    cl = em.dma('sync', oacc_t[:],
                                OACC[tb * 128:(tb + 1) * 128,
                                     nh * 512:(nh + 1) * 512], inc='ldacc')
                    em.wait('vector', 'ldacc', cl)
                    em.wait('vector', 'sc', ce)
                    cv = em.op('vector', lambda e_: e_.tensor_tensor(
                        oacc_t[:], oacc_t[:], otile[:], ALU.add), inc='ve')
                    em.wait('gpsimd', 've', cv)
                    cd = em.op('gpsimd', lambda e_, tb=tb, nh=nh:
                               e_.dma_start(
                        out=OACC[tb * 128:(tb + 1) * 128,
                                 nh * 512:(nh + 1) * 512], in_=oacc_t[:]),
                        inc='outacc', inc_by=16)
                    em.wait('scalar', 'outacc', cd)
                    em.wait('vector', 'outacc', cd)
                    em.wait('sync', 'outacc', cd)

    for cb in range(NJS):
        jp = cb % 2
        em.wait('scalar', 'te', fggate[jp])
        src_a = SGT[:, cb * 128:(cb + 1) * 128].rearrange(
            '(kb p) c -> p kb c', p=128)
        src_g = SUT[:, cb * 128:(cb + 1) * 128].rearrange(
            '(kb p) c -> p kb c', p=128)
        cwa = em.dma('sync', wlda[:].rearrange('p (kb c) -> p kb c',
                                               kb=NKB), src_a, inc='ldw')
        cwg = em.dma('sync', wldg[:].rearrange('p (kb c) -> p kb c',
                                               kb=NKB), src_g, inc='ldwg')
        em.wait('scalar', 'ldw', cwa)
        em.op('scalar', lambda e_, jp=jp: e_.activation(
            fa_bf[jp][:], wlda[:], AF.Copy), inc='sc')
        em.wait('scalar', 'ldwg', cwg)
        c2 = em.op('scalar', lambda e_, jp=jp: e_.activation(
            fg_bf[jp][:], wldg[:], AF.Copy), inc='sc')
        em.wait('sync', 'sc', c2)
        em.wait('tensor', 'sc', c2)
        for tch in range(4):
            parts = [(fa_bf[jp][:, kb * 128:(kb + 1) * 128],
                      big[:, kb * TPC + tch * 512:kb * TPC + (tch + 1) * 512])
                     for kb in range(NKB)]
            psg = ppg.nxt(em)
            mm_accum(em, psg[:], parts, inc='te')
            parts = [(fg_bf[jp][:, kb * 128:(kb + 1) * 128],
                      big[:, kb * TPC + tch * 512:kb * TPC + (tch + 1) * 512])
                     for kb in range(NKB)]
            psa = ppa.nxt(em)
            cm = mm_accum(em, psa[:], parts, inc='te')
            if tch == 3:
                fggate[jp] = cm
            em.wait('scalar', 'te', cm)
            cs = em.op('scalar', lambda e_, psg=psg: e_.activation(
                sg_sb[:], psg[:], AF.Silu), inc='sc')
            ppg.done('sc', cs)
            em.wait('vector', 'sc', cs)
            cv = em.op('vector', lambda e_, psa=psa: e_.tensor_tensor(
                act16[:], sg_sb[:], psa[:], ALU.mult), inc='ve')
            ppa.done('ve', cv)
            em.wait('gpsimd', 've', cv)
            cd = em.op('gpsimd', lambda e_, cb=cb, tch=tch: e_.dma_start(
                out=SHACT[cb * 128:(cb + 1) * 128,
                          tch * 512:(tch + 1) * 512], in_=act16[:]),
                inc='outsh', inc_by=16)
            em.wait('vector', 'outsh', cd)

    em.wait('sync', 'te')
    em.wait('sync', 'outsh')
    for nh in range(4):
        em.wait('scalar', 'te')
        src_d = SDT[:, nh * 512:(nh + 1) * 512].rearrange(
            '(j p) c -> p j c', p=128)
        for j0 in range(0, NJS, 6):
            nj = min(6, NJS - j0)
            cw = em.dma('sync',
                        wldd[:, 0:nj * 512].rearrange('p (j c) -> p j c',
                                                      j=nj),
                        src_d[:, j0:j0 + nj, :], inc='ldw')
            em.wait('scalar', 'ldw', cw)
            c2 = em.op('scalar', lambda e_, j0=j0, nj=nj: e_.activation(
                act[:, j0 * 512:(j0 + nj) * 512], wldd[:, 0:nj * 512],
                AF.Copy), inc='sc')
            em.wait('sync', 'sc', c2)
        em.wait('tensor', 'sc')
        for tb in range(NTB):
            em.wait('sync', 'te')
            csh = em.dma('sync',
                         shtb[:].rearrange('p (j c) -> p j c', j=NJS),
                         SHACT[:, tb * 128:(tb + 1) * 128].rearrange(
                             '(j p) c -> p j c', p=128), inc='ldsh')
            em.wait('tensor', 'ldsh', csh)
            parts = [(shtb[:, j * 128:(j + 1) * 128],
                      act[:, j * 512:(j + 1) * 512])
                     for j in range(NJS)]
            psd = ppd.nxt(em)
            cm = mm_accum(em, psd[:], parts, inc='te')
            em.wait('scalar', 'te', cm)
            ce = em.op('scalar', lambda e_, tb=tb, psd=psd: e_.activation(
                otile[:], psd[:], AF.Identity,
                scale=sg_all[:, tb:tb + 1]), inc='sc')
            ppd.done('sc', ce)
            cl = em.dma('sync', oacc_t[:],
                        OACC[tb * 128:(tb + 1) * 128,
                             nh * 512:(nh + 1) * 512], inc='ldacc')
            em.wait('vector', 'ldacc', cl)
            em.wait('vector', 'sc', ce)
            cv = em.op('vector', lambda e_: e_.tensor_tensor(
                oacc_t[:], oacc_t[:], otile[:], ALU.add), inc='ve')
            em.wait('gpsimd', 've', cv)
            cd = em.op('gpsimd', lambda e_, tb=tb, nh=nh: e_.dma_start(
                out=OACC[tb * 128:(tb + 1) * 128,
                         nh * 512:(nh + 1) * 512], in_=oacc_t[:]),
                inc='outacc', inc_by=16)
            em.wait('scalar', 'outacc', cd)
            em.wait('vector', 'outacc', cd)
            em.wait('sync', 'outacc', cd)

    em.wait('sync', 'outacc')
    for tb in range(NTB):
        em.wait('sync', 'sc')
        em.wait('sync', 've')
        cx = em.dma('sync', xtb[:], X3[tb * 128:(tb + 1) * 128, :],
                    inc='ldx')
        ct8 = em.dma('sync', t1tb[:], OACC[tb * 128:(tb + 1) * 128, :],
                     inc='ldacc')
        em.wait('scalar', 'ldacc', ct8)
        em.wait('vector', 'ldacc', ct8)
        emit_plain_ln(em, t1tb[:], t1tb[:], stt, sq2[:])
        em.wait('vector', 'sc')
        em.wait('vector', 'ldx', cx)
        cv = em.op('vector', lambda e_: e_.tensor_tensor(
            xtb[:], xtb[:], t1tb[:], ALU.add), inc='ve')
        em.wait('gpsimd', 've', cv)
        cd = em.op('gpsimd', lambda e_, tb=tb: e_.dma_start(
            out=X4[tb * 128:(tb + 1) * 128, :], in_=xtb[:]),
            inc='outx', inc_by=16)
        em.wait('sync', 'outx', cd)
    em.wait('gpsimd', 'outx')



def build_launchF():
    nc = bass.Bass('TRN2', target_bir_lowering=False)
    io = declare_io_F(nc)
    stack = contextlib.ExitStack()
    with stack:
        em = Em(nc, stack)
        emit_F(nc, em, stack, io)
        em.finalize()
    return nc


def host_prep_launchF(inputs, x3s):
    rw9 = np.ascontiguousarray(np.concatenate(
        [np.asarray(inputs['L1_ffn_router_weight']).T,
         np.asarray(inputs['L1_ffn_shared_gate_w']).T], axis=1))
    gupt = np.ascontiguousarray(
        np.asarray(inputs['L1_ffn_gate_up_proj']).transpose(0, 2, 1))
    dpt = np.ascontiguousarray(
        np.asarray(inputs['L1_ffn_down_proj']).transpose(0, 2, 1))
    sgt = np.ascontiguousarray(np.asarray(inputs['L1_ffn_sh_gate_w']).T)
    sut = np.ascontiguousarray(np.asarray(inputs['L1_ffn_sh_up_w']).T)
    sdt = np.ascontiguousarray(np.asarray(inputs['L1_ffn_sh_down_w']).T)
    ident = np.eye(128, dtype=np.float32)
    cvals = np.array([[EPS, -0.5, 0, 0, 0, 0, 0, 0]], np.float32)
    tab = np.zeros((2, 8), np.float32)
    tab[0] = np.arange(8)
    tab[1] = np.arange(8) - BIGC
    in_maps = []
    for c in range(NCORES):
        in_maps.append(dict(x3=x3s[c], rw9=rw9, gupt=gupt, dpt=dpt, sgt=sgt,
                            sut=sut, sdt=sdt, ident=ident, cvals=cvals,
                            tab=tab))
    return in_maps


def build_fused():
    nc = bass.Bass('TRN2', target_bir_lowering=False)
    IO = {}
    def ext(nm, shp, dt):
        IO[nm] = nc.dram_tensor(nm, shp, dt, kind='ExternalInput').ap()
    def outt(nm, shp, dt):
        IO[nm] = nc.dram_tensor(nm, shp, dt, kind='ExternalOutput').ap()
    def itn(nm, shp, dt):
        IO[nm] = nc.dram_tensor(nm, shp, dt).ap()

    ext('x0', [TPC, D], F32)
    ext('wq', [D, D], F32); ext('wk', [D, D], F32); ext('wv', [D, D], F32)
    ext('bqkv', [3, D], F32); ext('gb', [2, D], F32)
    ext('cos', [TPC, ROT // 2], F32); ext('sin', [TPC, ROT // 2], F32)
    ext('ones', [128, 128], F32); ext('ident', [128, 128], F32)
    ext('cvals', [1, 8], F32)
    ext('wout', [D, D], F32); ext('bout', [1, D], F32); ext('gb2', [2, D], F32)
    ext('slopes', [NUM_TILES, D_SLICE * GRID], F32); ext('tab', [3, 64], F32)
    ext('bases', [64, 512], F32); ext('osc', [64, 1], F32); ext('gb1', [2, D], F32)
    ext('wxmt', [D, DPC], F32); ext('wzt', [D, DPC], F32)
    ext('cw', [4, 128, D_CONV], F32); ext('cb', [4, 128], F32)
    ext('xpt', [DPC, 160], F32); ext('dtpt', [DT_RANK, DPC], F32)
    ext('dtb', [4, 128], F32); ext('alog', [4, 128, D_STATE], F32)
    ext('dsk', [4, 128], F32)
    ext('opt', [D_INNER, D], F32); ext('gb2l', [2, D], F32)
    ext('fpt', [D, HIDDEN], F32); ext('fot', [D_INNER, D], F32)
    ext('rw9', [D, 9], F32)
    ext('gupt', [E_MOE, D, 2 * I_MOE], F32); ext('dpt', [E_MOE, I_MOE, D], F32)
    ext('sgt', [D, I_SH], F32); ext('sut', [D, I_SH], F32)
    ext('sdt', [I_SH, D], F32); ext('tab8', [2, 8], F32)
    outt('x4', [TPC, D], F32)
    itn('qt', [D, TPC], F32); itn('apart', [1, D], F32); itn('asum', [1, D], F32)
    itn('kst', [TPC, D], F32)
    itn('x1', [TPC, D], F32); itn('oh', [TPC, 64], F32); itn('xsl', [TPC, 32], F32)
    itn('minp', [64, 1], F32); itn('maxp', [64, 1], F32)
    itn('minr', [64, 1], F32); itn('maxr', [64, 1], F32)
    itn('ss', [NUM_TILES, D_SLICE], F32)
    itn('x2', [TPC, D], F32); itn('ut', [D, TPC], BF16)
    itn('utallf', [NCORES * D, TPC], BF16)
    itn('xcst', [DPC, T], F32); itn('zst', [DPC, T], F32)
    itn('xdblp', [160, T], F32); itn('xdbl', [160, T], F32)
    itn('gf2', [NCORES * DPC, TPC], BF16); itn('gout', [NCORES * DPC, TPC], BF16)
    itn('x3', [TPC, D], F32)
    itn('mix', [TPC, D], F32); itn('ares', [TPC, D], F32)
    itn('ffa', [D_INNER, TPC], BF16); itn('t6', [TPC, D], F32)
    itn('oacc', [TPC, D], F32); itn('shact', [I_SH, TPC], BF16)
    itn('combt', [E_MOE, TPC], F32)

    gstack = contextlib.ExitStack()
    with gstack:
        em = Em(nc, gstack)
        em.setup_consts([EPS, -0.5, 1.0, -1.0], None, IO['cvals'])

        def coll(name, kind, op, in_ap, out_ap):
            h = em.sem(name)
            def fn(e, h=h, in_ap=in_ap, out_ap=out_ap):
                with allow_ici_all_to_all():
                    e.collective_compute(
                        kind, op, replica_groups=[list(range(NCORES))],
                        ins=[in_ap.opt()], outs=[out_ap.opt()]).then_inc(h)
            em.q['gpsimd'].append(fn)
            for eng in Em.ENGINES:
                em.wait(eng, name, 1)

        with contextlib.ExitStack() as st:
            emit_A(nc, em, st, IO)
            em.barrier()
        coll('ccA', 'AllReduce', ALU.add, IO['apart'], IO['asum'])
        ioB = dict(IO)
        with contextlib.ExitStack() as st:
            emit_B(nc, em, st, ioB)
            em.barrier()
        coll('ccMn', 'AllReduce', ALU.min, IO['minp'], IO['minr'])
        coll('ccMx', 'AllReduce', ALU.max, IO['maxp'], IO['maxr'])
        with contextlib.ExitStack() as st:
            emit_C(nc, em, st, IO)
            em.barrier()
        coll('ccU', 'AllGather', ALU.bypass, IO['ut'], IO['utallf'])
        ioD = dict(IO)
        ioD['utall'] = IO['utallf'].rearrange('(j d) t -> j d t', d=D)
        ioD['g'] = IO['gf2'].rearrange('(j d) t -> j d t', d=DPC)
        with contextlib.ExitStack() as st:
            emit_D(nc, em, st, ioD)
            em.barrier()
        coll('ccG', 'AllToAll', ALU.bypass, IO['gf2'], IO['gout'])
        ioE = dict(IO)
        ioE['gfull'] = IO['gout']
        with contextlib.ExitStack() as st:
            emit_E(nc, em, st, ioE)
            em.barrier()
        ioF = dict(IO)
        ioF['tab'] = IO['tab8']
        with contextlib.ExitStack() as st:
            emit_F(nc, em, st, ioF)
            em.barrier()
        em.finalize()
    return nc


def host_prep_fused(inputs):
    import ml_dtypes
    dummy = np.zeros((1, 1), np.float32)
    d8 = [dummy] * NCORES
    hA = host_prep_launchA(inputs)
    hB = host_prep_launchB(inputs, d8, [np.zeros((1, D), np.float32)] * NCORES)
    hC = host_prep_launchC(inputs, d8, d8, d8,
                           [np.zeros((64, 1), np.float32)] * NCORES,
                           [np.zeros((64, 1), np.float32)] * NCORES)
    hD = host_prep_launchD(inputs, [np.zeros((1, 1), ml_dtypes.bfloat16)] * NCORES)
    hE = host_prep_launchE(inputs, [[np.zeros((1, 1), ml_dtypes.bfloat16)] * NCORES] * NCORES, d8)
    hF = host_prep_launchF(inputs, d8)
    in_maps = []
    for c in range(NCORES):
        m = {}
        m.update(hA[c])
        for k in ['wout', 'bout', 'gb2', 'slopes', 'tab']:
            m[k] = hB[c][k]
        for k in ['bases', 'osc', 'gb1']:
            m[k] = hC[c][k]
        for k in ['wxmt', 'wzt', 'cw', 'cb', 'xpt', 'dtpt', 'dtb', 'alog',
                  'dsk']:
            m[k] = hD[c][k]
        for k in ['opt', 'gb2l', 'fpt', 'fot']:
            m[k] = hE[c][k]
        for k in ['rw9', 'gupt', 'dpt', 'sgt', 'sut', 'sdt']:
            m[k] = hF[c][k]
        m['tab8'] = hF[c]['tab']
        m['cvals'] = np.array([[EPS, -0.5, 1.0, -1.0, 0, 0, 0, 0]], np.float32)
        in_maps.append(m)
    return in_maps


def _rotary_tables(t0, t1):
    j = np.arange(ROT // 2, dtype=np.float64)
    inv = 10000.0 ** (-2.0 * j / ROT)
    pos = np.arange(t0, t1, dtype=np.float64)[:, None]
    fr = pos * inv[None, :]
    return (np.cos(fr).astype(np.float32), np.sin(fr).astype(np.float32))


def host_prep_launchA(inputs):
    X = np.asarray(inputs['L0_attn_X'])[0]
    W_in = np.asarray(inputs['L0_attn_W_in'])
    b_in = np.asarray(inputs['L0_attn_b_in'])
    gamma = np.asarray(inputs['L0_attn_gamma'])
    beta = np.asarray(inputs['L0_attn_beta'])
    wq = np.ascontiguousarray(W_in[:, 0:D])
    wk = np.ascontiguousarray(W_in[:, D:2 * D])
    wv = np.ascontiguousarray(W_in[:, 2 * D:3 * D])
    bqkv = np.ascontiguousarray(b_in.reshape(3, D))
    gb = np.stack([gamma, beta], axis=0)
    ones = np.ones((128, 128), np.float32)
    ident = np.eye(128, dtype=np.float32)
    in_maps = []
    for c in range(NCORES):
        cos, sin = _rotary_tables(c * TPC, (c + 1) * TPC)
        in_maps.append(dict(
            x0=np.ascontiguousarray(X[c * TPC:(c + 1) * TPC]),
            wq=wq, wk=wk, wv=wv, bqkv=bqkv, gb=gb,
            cos=cos, sin=sin, ones=ones, ident=ident,
            cvals=np.array([[EPS, -0.5, 0, 0, 0, 0, 0, 0]], np.float32)))
    return in_maps


def kernel(**inputs):
    cores = list(range(NCORES))
    nc = build_fused()
    in_maps = host_prep_fused(inputs)
    with allow_ici_all_to_all():
        res = run_bass_kernel_spmd(nc, in_maps, core_ids=cores)
    x4 = np.concatenate([res.results[c]['x4'] for c in range(NCORES)], axis=0)
    return x4.reshape(1, T, D).astype(np.float32)


def kernel_multilaunch(**inputs):

    import ml_dtypes

    def as_bf16(a):
        a = np.asarray(a)
        if a.itemsize == 2 and a.dtype != ml_dtypes.bfloat16:
            return a.view(ml_dtypes.bfloat16)
        return a

    cores = list(range(NCORES))
    with allow_ici_all_to_all():
        resA = run_bass_kernel_spmd(build_launchA(), host_prep_launchA(inputs),
                                    core_ids=cores)
        qts = [resA.results[c]['qt'] for c in range(NCORES)]
        aparts = [resA.results[c]['apart'] for c in range(NCORES)]
        resB = run_bass_kernel_spmd(build_launchB(),
                                    host_prep_launchB(inputs, qts, aparts),
                                    core_ids=cores)
        x1s = [resB.results[c]['x1'] for c in range(NCORES)]
        ohs = [resB.results[c]['oh'] for c in range(NCORES)]
        xsls = [resB.results[c]['xsl'] for c in range(NCORES)]
        minps = [resB.results[c]['minp'] for c in range(NCORES)]
        maxps = [resB.results[c]['maxp'] for c in range(NCORES)]
        resC = run_bass_kernel_spmd(
            build_launchC(),
            host_prep_launchC(inputs, x1s, ohs, xsls, minps, maxps),
            core_ids=cores)
        x2s = [resC.results[c]['x2'] for c in range(NCORES)]
        uts = [as_bf16(resC.results[c]['ut']) for c in range(NCORES)]
        resD = run_bass_kernel_spmd(build_launchD(),
                                    host_prep_launchD(inputs, uts),
                                    core_ids=cores)
        gs = [as_bf16(resD.results[c]['g']) for c in range(NCORES)]
        resE = run_bass_kernel_spmd(build_launchE(),
                                    host_prep_launchE(inputs, gs, x2s),
                                    core_ids=cores)
        x3s = [resE.results[c]['x3'] for c in range(NCORES)]
        resF = run_bass_kernel_spmd(build_launchF(),
                                    host_prep_launchF(inputs, x3s),
                                    core_ids=cores)
        x4 = np.concatenate([resF.results[c]['x4'] for c in range(NCORES)],
                            axis=0)
    return x4.reshape(1, T, D).astype(np.float32)


# version 10
# speedup vs baseline: 1.7564x; 1.1022x over previous
import contextlib
import numpy as np

import concourse.bass as bass
import concourse.mybir as mybir
from concourse.bass_utils import run_bass_kernel_spmd
from concourse.aot_env import allow_ici_all_to_all

F32 = mybir.dt.float32
BF16 = mybir.dt.bfloat16
AF = mybir.ActivationFunctionType
ALU = mybir.AluOpType
AX = mybir.AxisListType

NCORES = 8
T = 16384
D = 2048
TPC = T // NCORES
NTB = TPC // 128
NKB = D // 128
ROT = 1024
EPS = 1e-5

D_INNER = 4096
D_STATE = 16
D_CONV = 4
DT_RANK = 128
HIDDEN = 8192
DPC = D_INNER // NCORES

E_MOE = 8
I_MOE = 1408
I_SH = 5632

NUM_TILES = 64
D_SLICE = 32
GRID = 16


class Em:
    ENGINES = ('sync', 'tensor', 'scalar', 'vector', 'gpsimd')

    def __init__(self, nc, stack):
        self.nc = nc
        self.stack = stack
        self.q = {e: [] for e in self.ENGINES}
        self.sems = {}
        self.cnt = {}

    def sem(self, name):
        if name in self.sems:
            return self.sems[name]
        h = self.stack.enter_context(self.nc.semaphore(name))
        self.sems[name] = h
        self.cnt[name] = 0
        return h

    def barrier(self):
        for eng in self.ENGINES:
            for name in list(self.sems):
                if self.cnt[name] > 0:
                    self.wait(eng, name)

    def wait(self, eng, name, val=None):
        if val is None:
            val = self.cnt[name]
        h = self.sems[name]
        self.q[eng].append(lambda e, h=h, v=val: e.wait_ge(h, v))

    AUTO = {'scalar': 'sc', 'vector': 've'}

    def op(self, eng, fn, inc=None, inc_by=1):
        own = self.AUTO.get(eng)
        if own is not None:
            self.wait(eng, own)
            if inc is None:
                inc, inc_by = own, 1
            assert inc == own, f'{eng} ops must inc {own}'
        if inc is None:
            self.q[eng].append(fn)
            return None
        h = self.sems[inc]
        self.cnt[inc] += inc_by
        tgt = self.cnt[inc]
        def wrapped(e, fn=fn, h=h, k=inc_by):
            fn(e).then_inc(h, k)
        self.q[eng].append(wrapped)
        return tgt

    def dma(self, eng, dst, src, inc=None):
        return self.op(eng, lambda e: e.dma_start(out=dst, in_=src), inc=inc,
                       inc_by=16)

    def setup_consts(self, values, cv_sb, cv_dram):
        if not hasattr(self, 'cv_global'):
            self.cv_global = self.stack.enter_context(
                self.nc.sbuf_tensor('cv_global', [128, 8], F32,
                                    allow_name_mangling=True))
            self.sem('ldcv')
            self.dma('sync', self.cv_global[:],
                     cv_dram.partition_broadcast(128).squeeze(1), inc='ldcv')
            for eng in ('scalar', 'vector', 'tensor', 'gpsimd'):
                self.wait(eng, 'ldcv')
            self._consts_done = set()
        for i, v in enumerate(values):
            if float(v) not in self._consts_done:
                self._consts_done.add(float(v))
                self.nc.const_aps.register(F32, float(v),
                                           self.cv_global[:, i:i + 1])

    def coll(self, kind, op, ins, outs, inc):
        h = self.sems[inc]
        self.cnt[inc] += 1
        tgt = 1
        rg = [list(range(NCORES))]
        def wrapped(e, h=h):
            with allow_ici_all_to_all():
                e.collective_compute(kind, op, replica_groups=rg, ins=ins,
                                     outs=outs).then_inc(h)
        self.q['gpsimd'].append(wrapped)
        return tgt

    def finalize(self):
        with self.nc.Block() as block:
            for eng in self.ENGINES:
                ops = self.q[eng]
                if not ops:
                    continue
                dec = getattr(block, eng)
                def body(e, ops=ops):
                    for fn in ops:
                        fn(e)
                dec(body)


class PP:
    def __init__(self, em, stack, nc, name, shape, n=2, banks=None):
        if banks is not None:
            self.ps = banks
            n = len(banks)
        else:
            self.ps = [stack.enter_context(nc.psum_tensor(
                f'{name}{i}', shape, F32, allow_name_mangling=True))
                for i in range(n)]
        self.gate = [None] * n
        self.i = 0

    def nxt(self, em):
        k = self.i % len(self.ps)
        if self.gate[k] is not None:
            em.wait('tensor', self.gate[k][0], self.gate[k][1])
        self.i += 1
        return self.ps[k]

    def done(self, sem, cnt):
        k = (self.i - 1) % len(self.ps)
        self.gate[k] = (sem, cnt)


def mm_accum(em, ps_ap, parts, inc='te', skip_group_check=False):
    n = len(parts)
    last = None
    for i, (lh, rh) in enumerate(parts):
        fn = lambda e, lh=lh, rh=rh, i=i: e.matmul(
            ps_ap, lh, rh, start=(i == 0), stop=(i == n - 1),
            skip_group_check=skip_group_check)
        if i == n - 1:
            last = em.op('tensor', fn, inc=inc)
        else:
            em.op('tensor', fn)
    return last


def emit_plain_ln(em, src, dst, stt, sq, n=D, eps=EPS):
    s_sum = stt[:, 0:1]
    s_sumsq = stt[:, 1:2]
    s_mean = stt[:, 2:3]
    s_m2 = stt[:, 3:4]
    s_var = stt[:, 4:5]
    s_rstd = stt[:, 5:6]
    s_negmr = stt[:, 6:7]
    inv_n = 1.0 / n
    em.wait('vector', 'sc')
    em.wait('scalar', 've')
    c_sq = em.op('scalar', lambda e: e.activation(sq, src, AF.Square,
                                                  accum_out=s_sumsq),
                 inc='sc')
    em.op('vector', lambda e: e.reduce_sum(s_sum, src, AX.X))
    em.wait('vector', 'sc', c_sq)
    em.op('vector', lambda e: e.tensor_scalar(s_mean, s_sum, inv_n, None,
                                              ALU.mult))
    em.op('vector', lambda e: e.tensor_tensor(s_m2, s_mean, s_mean, ALU.mult))
    c_var = em.op('vector', lambda e: e.scalar_tensor_tensor(
        s_var, s_sumsq, inv_n, s_m2, ALU.mult, ALU.subtract), inc='ve')
    em.wait('scalar', 've', c_var)
    em.op('scalar', lambda e: e.activation(s_rstd, s_var, AF.Ln, bias=eps))
    c_r = em.op('scalar', lambda e: e.activation(s_rstd, s_rstd, AF.Exp,
                                                 scale=-0.5), inc='sc')
    em.wait('vector', 'sc', c_r)
    c_nm = em.op('vector', lambda e: e.scalar_tensor_tensor(
        s_negmr, s_mean, -1.0, s_rstd, ALU.mult, ALU.mult), inc='ve')
    em.wait('scalar', 've', c_nm)
    return em.op('scalar', lambda e: e.activation(dst, src, AF.Identity,
                                                  scale=s_rstd,
                                                  bias=s_negmr), inc='sc')


def declare_io_A(nc, internal=()):
    io = {}
    io['x0'] = nc.dram_tensor('x0', [TPC, D], F32, kind=None if 'x0' in internal else 'ExternalInput').ap()
    io['wq'] = nc.dram_tensor('wq', [D, D], F32, kind=None if 'wq' in internal else 'ExternalInput').ap()
    io['wk'] = nc.dram_tensor('wk', [D, D], F32, kind=None if 'wk' in internal else 'ExternalInput').ap()
    io['wv'] = nc.dram_tensor('wv', [D, D], F32, kind=None if 'wv' in internal else 'ExternalInput').ap()
    io['bqkv'] = nc.dram_tensor('bqkv', [3, D], F32, kind=None if 'bqkv' in internal else 'ExternalInput').ap()
    io['gb'] = nc.dram_tensor('gb', [2, D], F32, kind=None if 'gb' in internal else 'ExternalInput').ap()
    io['cos'] = nc.dram_tensor('cos', [TPC, ROT // 2], F32, kind=None if 'cos' in internal else 'ExternalInput').ap()
    io['sin'] = nc.dram_tensor('sin', [TPC, ROT // 2], F32, kind=None if 'sin' in internal else 'ExternalInput').ap()
    io['ones'] = nc.dram_tensor('ones', [128, 128], F32, kind=None if 'ones' in internal else 'ExternalInput').ap()
    io['ident'] = nc.dram_tensor('ident', [128, 128], F32, kind=None if 'ident' in internal else 'ExternalInput').ap()
    io['cvals'] = nc.dram_tensor('cvals', [1, 8], F32, kind=None if 'cvals' in internal else 'ExternalInput').ap()
    io['qt'] = nc.dram_tensor('qt', [D, TPC], F32, kind=None if 'qt' in internal else 'ExternalOutput').ap()
    io['apart'] = nc.dram_tensor('apart', [1, D], F32, kind=None if 'apart' in internal else 'ExternalOutput').ap()
    io['kst'] = nc.dram_tensor('kst', [TPC, D], F32).ap()
    return io


def emit_A(nc, em, stack, io):
    X0 = io['x0']
    WQ = io['wq']
    WK = io['wk']
    WV = io['wv']
    BQKV = io['bqkv']
    GB = io['gb']
    COS = io['cos']
    SIN = io['sin']
    ONES = io['ones']
    IDENT = io['ident']
    CVALS = io['cvals']
    QT = io['qt']
    APART = io['apart']
    KST = io['kst']
    sb = lambda name, shp, dt: stack.enter_context(
        nc.sbuf_tensor(name, shp, dt, allow_name_mangling=True))
    xnT = sb('xnT', [128, NKB * TPC], BF16)
    wbig = sb('wbig', [128, NKB * D], BF16)
    xtb = sb('xtb', [128, D], F32)
    t1tb = sb('t1tb', [128, D], F32)
    sqtb = sb('sqtb', [128, D], F32)
    brtb = sqtb
    ktile = sb('ktile', [128, D], F32)
    vtile = sb('vtile', [128, D], F32)
    rot_a = sb('rot_a', [128, ROT // 2], F32)
    rot_b = sb('rot_b', [128, ROT // 2], F32)
    rot_c = sb('rot_c', [128, ROT // 2], F32)
    costb = sb('costb', [128, ROT // 2], F32)
    sintb = sb('sintb', [128, ROT // 2], F32)
    wld = sb('wld', [128, D], F32)
    stt = sb('stt', [128, 16], F32)
    ones_sb = sb('ones_sb', [128, 128], F32)
    ident_sb = sb('ident_sb', [128, 128], F32)
    gb_sb = sb('gb_sb', [128, 2 * NKB], F32)
    bq_sb = sb('bq_sb', [128, NKB], F32)
    brow = t1tb[0:1, :]
    qout = sb('qout', [128, 512], F32)
    arow = ktile[0:1, :]
    cv_sb = sb('cv_sb', [128, 8], F32)
    ppq = PP(em, stack, nc, 'psq', [128, 512])
    ppk = PP(em, stack, nc, 'psk', [128, 512])
    ppt = PP(em, stack, nc, 'pst', [128, 128])
    psb = stack.enter_context(nc.psum_tensor('psb', [128, 512], F32, allow_name_mangling=True))
    psa1 = stack.enter_context(nc.psum_tensor('psa1', [1, 512], F32,
                                              allow_name_mangling=True))
    aacc = sb('aacc', [1, D], F32)

    for s in ['ldc', 'ldx', 'ldw', 'ldb', 'ldcos', 'ldsin', 'ldk',
              'outq', 'outk', 'outa', 'sc', 've', 'te']:
        em.sem(s)

    em.setup_consts([EPS, -0.5], cv_sb, CVALS)
    em.dma('sync', ones_sb[:], ONES, inc='ldc')
    em.dma('sync', ident_sb[:], IDENT, inc='ldc')
    for i in range(2):
        for kb in range(NKB):
            em.dma('sync', gb_sb[:, i * NKB + kb:i * NKB + kb + 1],
                   GB[i:i + 1, kb * 128:(kb + 1) * 128].rearrange(
                       'a p -> p a'), inc='ldc')
    for kb in range(NKB):
        em.dma('sync', bq_sb[:, kb:kb + 1],
               BQKV[0:1, kb * 128:(kb + 1) * 128].rearrange('a p -> p a'),
               inc='ldc')
    em.wait('scalar', 'ldc')
    em.wait('vector', 'ldc')
    em.wait('tensor', 'ldc')

    em.op('vector', lambda e: e.memset(aacc[:], 0.0))
    for tb in range(NTB):
        cx = em.dma('sync', xtb[:], X0[tb * 128:(tb + 1) * 128, :],
                    inc='ldx')
        em.wait('scalar', 'ldx', cx)
        em.wait('vector', 'ldx', cx)
        emit_plain_ln(em, xtb[:], t1tb[:], stt, sqtb[:])
        c = emit_plain_ln(em, t1tb[:], xtb[:], stt, sqtb[:])
        em.wait('tensor', 'sc', c)
        for kb in range(NKB):
            pstk = ppt.nxt(em)
            ct = em.op('tensor', lambda e, kb=kb, pstk=pstk: e.transpose(
                pstk[:], xtb[:, kb * 128:(kb + 1) * 128], ident_sb[:]),
                inc='te')
            em.wait('scalar', 'te', ct)
            ce2 = em.op('scalar', lambda e, kb=kb, tb=tb, pstk=pstk:
                        e.activation(
                xnT[:, kb * TPC + tb * 128: kb * TPC + (tb + 1) * 128],
                pstk[:], AF.Identity, scale=gb_sb[:, kb:kb + 1],
                bias=gb_sb[:, NKB + kb:NKB + kb + 1]), inc='sc')
            ppt.done('sc', ce2)
        em.wait('sync', 'sc')

    def load_w(WSRC):
        em.wait('scalar', 'te')
        for kb in range(NKB):
            cx = em.dma('sync', wld[:], WSRC[kb * 128:(kb + 1) * 128, :],
                        inc='ldw')
            em.wait('scalar', 'ldw', cx)
            c = em.op('scalar', lambda e, kb=kb: e.activation(
                wbig[:, kb * D:(kb + 1) * D], wld[:], AF.Copy), inc='sc')
            em.wait('sync', 'sc', c)
        return em.cnt['sc']

    cw = load_w(WQ)
    em.wait('tensor', 'sc', cw)
    for cb in range(NKB):
        for tch in range(TPC // 512):
            parts = [(wbig[:, kb * D + cb * 128:kb * D + (cb + 1) * 128],
                      xnT[:, kb * TPC + tch * 512:kb * TPC + (tch + 1) * 512])
                     for kb in range(NKB)]
            psq = ppq.nxt(em)
            cm = mm_accum(em, psq[:], parts, inc='te')
            em.wait('scalar', 'te', cm)
            ce = em.op('scalar', lambda e, cb=cb, psq=psq: e.activation(
                qout[:], psq[:], AF.Identity, bias=bq_sb[:, cb:cb + 1]),
                inc='sc')
            ppq.done('sc', ce)
            em.wait('gpsimd', 'sc', ce)
            cd = em.op('gpsimd', lambda e, cb=cb, tch=tch: e.dma_start(
                out=QT[cb * 128:(cb + 1) * 128, tch * 512:(tch + 1) * 512],
                in_=qout[:]), inc='outq', inc_by=16)
            em.wait('scalar', 'outq', cd)

    def kv_pass(which):
        WSRC = WK if which == 0 else WV
        tile = ktile if which == 0 else vtile
        em.wait('sync', 've')
        em.wait('sync', 'outk')
        cbr = em.dma('sync', brow, BQKV[1 + which:2 + which, :],
                     inc='ldb')
        cw = load_w(WSRC)
        em.wait('tensor', 'sc', cw)
        em.wait('tensor', 'ldb', cbr)
        em.wait('scalar', 've')
        for ncx in range(4):
            em.wait('tensor', 'sc')
            cm = em.op('tensor', lambda e, ncx=ncx: e.matmul(
                psb[:], ones_sb[0:1, :],
                brow[:, ncx * 512:(ncx + 1) * 512], start=True, stop=True),
                inc='te')
            em.wait('scalar', 'te', cm)
            em.op('scalar', lambda e, ncx=ncx: e.activation(
                brtb[:, ncx * 512:(ncx + 1) * 512], psb[:], AF.Copy),
                inc='sc')
        em.wait('tensor', 'sc')
        em.wait('vector', 'sc')

        for tb in range(NTB):
            if tb > 0:
                em.wait('sync', 've')
            c_cos = em.dma('sync', costb[:],
                           COS[tb * 128:(tb + 1) * 128, :], inc='ldcos')
            c_sin = em.dma('sync', sintb[:],
                           SIN[tb * 128:(tb + 1) * 128, :], inc='ldsin')
            if which == 1:
                c_krb = em.dma('sync', ktile[:],
                               KST[tb * 128:(tb + 1) * 128, :], inc='ldk')
            for ncx in range(4):
                parts = [(xnT[:, kb * TPC + tb * 128:kb * TPC + (tb + 1) * 128],
                          wbig[:, kb * D + ncx * 512:kb * D + (ncx + 1) * 512])
                         for kb in range(NKB)]
                psk = ppk.nxt(em)
                cm = mm_accum(em, psk[:], parts, inc='te')
                em.wait('vector', 'te', cm)
                cv = em.op('vector', lambda e, ncx=ncx, psk=psk:
                           e.tensor_tensor(
                    tile[:, ncx * 512:(ncx + 1) * 512], psk[:],
                    brtb[:, ncx * 512:(ncx + 1) * 512], ALU.add),
                    inc='ve')
                ppk.done('ve', cv)
            em.wait('vector', 'ldcos', c_cos)
            em.wait('vector', 'ldsin', c_sin)
            r3 = tile[:, 0:ROT].rearrange('p (j two) -> p two j', two=2)
            ev = r3[:, 0, :]
            od = r3[:, 1, :]
            em.op('vector', lambda e, ev=ev: e.tensor_tensor(
                rot_a[:], ev, costb[:], ALU.mult))
            em.op('vector', lambda e, od=od: e.tensor_tensor(
                rot_b[:], od, sintb[:], ALU.mult))
            em.op('vector', lambda e: e.tensor_tensor(
                rot_a[:], rot_a[:], rot_b[:], ALU.subtract))
            em.op('vector', lambda e, od=od: e.tensor_tensor(
                rot_b[:], od, costb[:], ALU.mult))
            em.op('vector', lambda e, ev=ev: e.tensor_tensor(
                rot_c[:], ev, sintb[:], ALU.mult))
            em.op('vector', lambda e, od=od: e.tensor_tensor(
                od, rot_b[:], rot_c[:], ALU.add))
            cv = em.op('vector', lambda e, ev=ev: e.tensor_copy(
                ev, rot_a[:]), inc='ve')
            if which == 0:
                em.wait('gpsimd', 've', cv)
                cd = em.op('gpsimd', lambda e, tb=tb: e.dma_start(
                    out=KST[tb * 128:(tb + 1) * 128, :], in_=ktile[:]),
                    inc='outk', inc_by=16)
                em.wait('vector', 'outk', cd)
            else:
                em.wait('scalar', 've', cv)
                em.wait('scalar', 'ldk', c_krb)
                em.op('scalar', lambda e: e.activation(
                    t1tb[:], ktile[:], AF.Square, accum_out=stt[:, 8:9]))
                em.op('scalar', lambda e: e.activation(
                    t1tb[:], vtile[:], AF.Square, accum_out=stt[:, 9:10]))
                em.op('scalar', lambda e: e.activation(
                    stt[:, 10:11], stt[:, 8:9], AF.Ln))
                cs = em.op('scalar', lambda e: e.activation(
                    stt[:, 11:12], stt[:, 9:10], AF.Ln), inc='sc')
                em.wait('vector', 'sc', cs)
                cv2 = em.op('vector', lambda e: e.tensor_tensor(
                    stt[:, 13:14], stt[:, 10:11], stt[:, 11:12], ALU.add),
                    inc='ve')
                em.wait('scalar', 've', cv2)
                cs2 = em.op('scalar', lambda e: e.activation(
                    stt[:, 14:15], stt[:, 13:14], AF.Exp, scale=-0.5),
                    inc='sc')
                em.wait('vector', 'sc', cs2)
                em.op('vector', lambda e: e.tensor_tensor(
                    t1tb[:], ktile[:], vtile[:], ALU.mult))
                cv3 = em.op('vector', lambda e: e.tensor_scalar(
                    t1tb[:], t1tb[:], stt[:, 14:15], None, ALU.mult),
                    inc='ve')
                em.wait('tensor', 've', cv3)
                for ncx in range(4):
                    cm = em.op('tensor', lambda e, ncx=ncx: e.matmul(
                        psa1[:], ones_sb[:, 0:1],
                        t1tb[:, ncx * 512:(ncx + 1) * 512],
                        start=True, stop=True), inc='te')
                    em.wait('vector', 'te', cm)
                    cva = em.op('vector', lambda e, ncx=ncx: e.tensor_tensor(
                        aacc[:, ncx * 512:(ncx + 1) * 512],
                        aacc[:, ncx * 512:(ncx + 1) * 512], psa1[:], ALU.add),
                        inc='ve')
                    em.wait('tensor', 've', cva)

    kv_pass(0)
    kv_pass(1)
    em.wait('gpsimd', 've')
    em.dma('gpsimd', APART, aacc[:], inc='outa')
    em.wait('gpsimd', 'outa')
    em.wait('gpsimd', 'outq')



def build_launchA():
    nc = bass.Bass('TRN2', target_bir_lowering=False)
    io = declare_io_A(nc)
    stack = contextlib.ExitStack()
    with stack:
        em = Em(nc, stack)
        emit_A(nc, em, stack, io)
        em.finalize()
    return nc


BIGC = 65536.0

def declare_io_B(nc, internal=()):
    io = {}
    io['x0'] = nc.dram_tensor('x0', [TPC, D], F32, kind=None if 'x0' in internal else 'ExternalInput').ap()
    io['qt'] = nc.dram_tensor('qt', [D, TPC], F32, kind=None if 'qt' in internal else 'ExternalInput').ap()
    io['aparts'] = nc.dram_tensor('aparts', [NCORES, D], F32, kind=None if 'aparts' in internal else 'ExternalInput').ap()
    io['wout'] = nc.dram_tensor('wout', [D, D], F32, kind=None if 'wout' in internal else 'ExternalInput').ap()
    io['bout'] = nc.dram_tensor('bout', [1, D], F32, kind=None if 'bout' in internal else 'ExternalInput').ap()
    io['gb2'] = nc.dram_tensor('gb2', [2, D], F32, kind=None if 'gb2' in internal else 'ExternalInput').ap()
    io['slopes'] = nc.dram_tensor('slopes', [NUM_TILES, D_SLICE * GRID], F32, kind=None if 'slopes' in internal else 'ExternalInput').ap()
    io['ones'] = nc.dram_tensor('ones', [128, 128], F32, kind=None if 'ones' in internal else 'ExternalInput').ap()
    io['ident'] = nc.dram_tensor('ident', [128, 128], F32, kind=None if 'ident' in internal else 'ExternalInput').ap()
    io['cvals'] = nc.dram_tensor('cvals', [1, 8], F32, kind=None if 'cvals' in internal else 'ExternalInput').ap()
    io['tab'] = nc.dram_tensor('tab', [3, 64], F32, kind=None if 'tab' in internal else 'ExternalInput').ap()
    io['x1'] = nc.dram_tensor('x1', [TPC, D], F32, kind=None if 'x1' in internal else 'ExternalOutput').ap()
    io['oh'] = nc.dram_tensor('oh', [TPC, 64], F32, kind=None if 'oh' in internal else 'ExternalOutput').ap()
    io['xsl'] = nc.dram_tensor('xsl', [TPC, 32], F32, kind=None if 'xsl' in internal else 'ExternalOutput').ap()
    io['minp'] = nc.dram_tensor('minp', [64, 1], F32, kind=None if 'minp' in internal else 'ExternalOutput').ap()
    io['maxp'] = nc.dram_tensor('maxp', [64, 1], F32, kind=None if 'maxp' in internal else 'ExternalOutput').ap()
    io['ss'] = nc.dram_tensor('ss', [NUM_TILES, D_SLICE], F32).ap()
    return io


def emit_B(nc, em, stack, io):
    X0 = io['x0']
    QT = io['qt']
    APARTS = io.get('aparts')
    WOUT = io['wout']
    BOUT = io['bout']
    GB2 = io['gb2']
    SLOPES = io['slopes']
    ONES = io['ones']
    IDENT = io['ident']
    CVALS = io['cvals']
    TAB = io['tab']
    X1 = io['x1']
    OH = io['oh']
    XSL = io['xsl']
    MINP = io['minp']
    MAXP = io['maxp']
    SS = io['ss']
    sb = lambda name, shp, dt: stack.enter_context(
        nc.sbuf_tensor(name, shp, dt, allow_name_mangling=True))
    ybig = sb('ybig', [128, NKB * TPC], BF16)
    wbig = sb('wbig', [128, NKB * D], BF16)
    xtb = sb('xtb', [128, D], F32)
    t1tb = sb('t1tb', [128, D], F32)
    t2tb = sb('t2tb', [128, D], F32)
    sqtb = sb('sqtb', [128, D], F32)
    wld = sb('wld', [128, D], F32)
    brtb = sb('brtb', [128, D], F32)
    g2rep = brtb
    b2rep = wld
    stt = sb('stt', [128, 16], F32)
    ones_sb = sb('ones_sb', [128, 128], F32)
    ident_sb = sb('ident_sb', [128, 128], F32)
    afac = sb('afac', [128, NKB], F32)
    apld = sb('apld', [128, NKB], F32)
    cv_sb = sb('cv_sb', [128, 8], F32)
    ssrep = sb('ssrep', [128, D], F32)
    sl_sb = sb('sl_sb', [64, D_SLICE * GRID], F32)
    ss_sb = sb('ss_sb', [64, D_SLICE], F32)
    iot64 = sb('iot64', [128, 64], F32)
    cidx = sb('cidx', [128, 64], F32)
    iot8 = sb('iot8', [128, 8], F32)
    iot64m = sb('iot64m', [128, 64], F32)
    iot8m = sb('iot8m', [128, 8], F32)
    scor = sb('scor', [128, 64], F32)
    msk = sb('msk', [128, 64], F32)
    ohtb = sb('ohtb', [128, 64], F32)
    xsl_sb = sb('xsl_sb', [128, 32], F32)
    mmtb = sb('mmtb', [128, 64], F32)
    minacc = sb('minacc', [64, 4], F32)
    redsc = sb('redsc', [64, 128], F32)
    pst = stack.enter_context(nc.psum_tensor('pst', [128, 128], F32, allow_name_mangling=True))
    psb = stack.enter_context(nc.psum_tensor('psb', [128, 512], F32, allow_name_mangling=True))
    ppy = PP(em, stack, nc, 'psy', [128, 512])
    psm = stack.enter_context(nc.psum_tensor('psm', [64, 128], F32, allow_name_mangling=True))

    for s in ['ldc', 'lda0', 'lda1', 'ldx', 'ldw', 'ldq', 'ldb', 'ldss',
              'outx', 'outo', 'outs', 'outm', 'outss', 'sc', 've', 'te']:
        em.sem(s)

    em.setup_consts([EPS, -0.5], cv_sb, CVALS)
    em.dma('sync', ones_sb[:], ONES, inc='ldc')
    em.dma('sync', ident_sb[:], IDENT, inc='ldc')
    em.dma('sync', iot64[:], TAB[0:1, :].partition_broadcast(128).squeeze(1),
           inc='ldc')
    em.dma('sync', cidx[:], TAB[1:2, :].partition_broadcast(128).squeeze(1),
           inc='ldc')
    em.dma('sync', iot8[:], TAB[2:3, 0:8].partition_broadcast(128).squeeze(1),
           inc='ldc')
    em.dma('sync', sl_sb[:], SLOPES, inc='ldc')
    em.wait('vector', 'ldc')
    em.wait('scalar', 'ldc')
    em.wait('tensor', 'ldc')
    em.op('vector', lambda e: e.tensor_scalar(iot64m[:], iot64[:], -BIGC,
                                              None, ALU.add))
    em.op('vector', lambda e: e.tensor_scalar(iot8m[:], iot8[:], -BIGC,
                                              None, ALU.add))
    em.op('vector', lambda e: e.memset(minacc[:, 0:1], BIGC))
    em.op('vector', lambda e: e.memset(minacc[:, 1:2], -BIGC))

    if 'asum' in io:
        for kb in range(NKB):
            em.dma('sync', apld[:, kb:kb + 1],
                   io['asum'][0:1, kb * 128:(kb + 1) * 128].rearrange(
                       'a p -> p a'), inc='lda0')
        em.wait('vector', 'lda0')
    else:
      for r in range(NCORES):
        dst = apld if r == 0 else afac
        sem_r = 'lda0' if r % 2 == 0 else 'lda1'
        for kb in range(NKB):
            em.dma('sync', dst[:, kb:kb + 1],
                   APARTS[r:r + 1, kb * 128:(kb + 1) * 128].rearrange(
                       'a p -> p a'), inc=sem_r)
        em.wait('vector', sem_r)
        if r > 0:
            em.op('vector', lambda e: e.tensor_tensor(
                apld[:], apld[:], afac[:], ALU.add))
        em.wait('sync', 've')
    cA = em.op('vector', lambda e: e.tensor_scalar(
        apld[:], apld[:], 0.5, 0.5, ALU.mult, ALU.add), inc='ve')

    cb = em.dma('sync', t1tb[0:1, :], BOUT, inc='ldb')
    em.wait('tensor', 'ldb', cb)
    for ncx in range(4):
        em.wait('tensor', 'sc')
        cm = em.op('tensor', lambda e, ncx=ncx: e.matmul(
            psb[:], ones_sb[0:1, :], t1tb[0:1, ncx * 512:(ncx + 1) * 512],
            start=True, stop=True), inc='te')
        em.wait('scalar', 'te', cm)
        em.op('scalar', lambda e, ncx=ncx: e.activation(
            brtb[:, ncx * 512:(ncx + 1) * 512], psb[:], AF.Copy),
            inc='sc')

    em.op('vector', lambda e: e.tensor_reduce(
        ss_sb[:], sl_sb[:].rearrange('p (j g) -> p j g', g=GRID),
        AX.X, ALU.add))
    em.wait('scalar', 've')
    cs = em.op('scalar', lambda e: e.activation(
        ss_sb[:], ss_sb[:], AF.Sign), inc='sc')
    em.wait('gpsimd', 'sc', cs)
    cd = em.op('gpsimd', lambda e: e.dma_start(out=SS, in_=ss_sb[:]),
               inc='outss', inc_by=16)
    em.wait('sync', 'outss', cd)
    cr = em.dma('sync', ssrep[:],
                SS.rearrange('a b -> (a b)').unsqueeze(0)
                .partition_broadcast(128).squeeze(1), inc='ldss')

    em.wait('scalar', 've', cA)
    for kb in range(NKB):
        cq = em.dma('sync', wld[:], QT[kb * 128:(kb + 1) * 128, :],
                    inc='ldq')
        em.wait('scalar', 'ldq', cq)
        c = em.op('scalar', lambda e, kb=kb: e.activation(
            ybig[:, kb * TPC:(kb + 1) * TPC], wld[:], AF.Identity,
            scale=apld[:, kb:kb + 1]), inc='sc')
        em.wait('sync', 'sc', c)

    em.wait('scalar', 'te')
    for kb in range(NKB):
        cx = em.dma('sync', wld[:], WOUT[kb * 128:(kb + 1) * 128, :],
                    inc='ldw')
        em.wait('scalar', 'ldw', cx)
        c = em.op('scalar', lambda e, kb=kb: e.activation(
            wbig[:, kb * D:(kb + 1) * D], wld[:], AF.Copy), inc='sc')
        em.wait('sync', 'sc', c)
    em.wait('tensor', 'sc')

    for tb in range(NTB):
        cx = em.dma('sync', xtb[:], X0[tb * 128:(tb + 1) * 128, :],
                    inc='ldx')
        em.wait('scalar', 'ldx', cx)
        em.wait('vector', 'ldx', cx)
        emit_plain_ln(em, xtb[:], t1tb[:], stt, sqtb[:])
        em.wait('vector', 'sc')
        for ncx in range(4):
            parts = [(ybig[:, kb * TPC + tb * 128:kb * TPC + (tb + 1) * 128],
                      wbig[:, kb * D + ncx * 512:kb * D + (ncx + 1) * 512])
                     for kb in range(NKB)]
            psy = ppy.nxt(em)
            cm = mm_accum(em, psy[:], parts, inc='te')
            em.wait('vector', 'te', cm)
            em.op('vector', lambda e, ncx=ncx, psy=psy: e.tensor_tensor(
                t2tb[:, ncx * 512:(ncx + 1) * 512], psy[:],
                brtb[:, ncx * 512:(ncx + 1) * 512], ALU.add))
            cv = em.op('vector', lambda e, ncx=ncx: e.tensor_tensor(
                t2tb[:, ncx * 512:(ncx + 1) * 512],
                t2tb[:, ncx * 512:(ncx + 1) * 512],
                t1tb[:, ncx * 512:(ncx + 1) * 512], ALU.add), inc='ve')
            ppy.done('ve', cv)
        emit_plain_ln(em, t2tb[:], t2tb[:], stt, sqtb[:])
        em.wait('vector', 'sc')
        cv = em.op('vector', lambda e: e.tensor_tensor(
            xtb[:], xtb[:], t2tb[:], ALU.add), inc='ve')
        em.wait('gpsimd', 've', cv)
        cdx = em.op('gpsimd', lambda e, tb=tb: e.dma_start(
            out=X1[tb * 128:(tb + 1) * 128, :], in_=xtb[:]),
            inc='outx', inc_by=16)
        em.wait('sync', 'outx', cdx)

    em.wait('scalar', 've')
    em.wait('tensor', 've')
    for i in range(2):
        dst = g2rep if i == 0 else b2rep
        cb = em.dma('sync', t1tb[0:1, :], GB2[i:i + 1, :], inc='ldb')
        em.wait('tensor', 'ldb', cb)
        for ncx in range(4):
            em.wait('tensor', 'sc')
            cm = em.op('tensor', lambda e, ncx=ncx: e.matmul(
                psb[:], ones_sb[0:1, :],
                t1tb[0:1, ncx * 512:(ncx + 1) * 512],
                start=True, stop=True), inc='te')
            em.wait('scalar', 'te', cm)
            em.op('scalar', lambda e, dst=dst, ncx=ncx: e.activation(
                dst[:, ncx * 512:(ncx + 1) * 512], psb[:], AF.Copy),
                inc='sc')
        em.wait('sync', 'sc')

    em.wait('vector', 'ldss', cr)

    for tb in range(NTB):
        em.wait('sync', 'sc')
        em.wait('sync', 've')
        cx = em.dma('sync', xtb[:], X1[tb * 128:(tb + 1) * 128, :],
                    inc='ldx')
        em.wait('scalar', 'ldx', cx)
        em.wait('vector', 'ldx', cx)
        emit_plain_ln(em, xtb[:], t1tb[:], stt, sqtb[:])
        c3 = emit_plain_ln(em, t1tb[:], t2tb[:], stt, sqtb[:])
        em.wait('vector', 'sc', c3)
        em.op('vector', lambda e: e.tensor_tensor(
            t2tb[:], t2tb[:], g2rep[:], ALU.mult))
        cxn = em.op('vector', lambda e: e.tensor_tensor(
            t2tb[:], t2tb[:], b2rep[:], ALU.add), inc='ve')
        em.wait('scalar', 've', cxn)
        em.op('scalar', lambda e: e.activation(t1tb[:], t2tb[:], AF.Sign),
              inc='sc')
        em.wait('vector', 'sc')
        em.op('vector', lambda e: e.tensor_tensor(
            sqtb[:], t1tb[:], ssrep[:], ALU.mult))
        em.op('vector', lambda e: e.tensor_reduce(
            scor[:], sqtb[:].rearrange('p (s i) -> p s i', i=D_SLICE),
            AX.X, ALU.add))
        em.op('vector', lambda e: e.tensor_reduce(
            stt[:, 8:16], scor[:].rearrange('p (c i) -> p c i', i=8),
            AX.X, ALU.add))
        em.op('vector', lambda e: e.tensor_reduce(
            stt[:, 7:8], stt[:, 8:16], AX.X, ALU.max))
        em.op('vector', lambda e: e.tensor_scalar(
            msk[:, 0:8], stt[:, 8:16], stt[:, 7:8], None, ALU.is_ge))
        em.op('vector', lambda e: e.tensor_tensor(
            msk[:, 8:16], msk[:, 0:8], iot8m[:], ALU.mult))
        em.op('vector', lambda e: e.tensor_scalar(
            msk[:, 8:16], msk[:, 8:16], BIGC, None, ALU.add))
        em.op('vector', lambda e: e.tensor_reduce(
            stt[:, 0:1], msk[:, 8:16], AX.X, ALU.min))
        em.op('vector', lambda e: e.tensor_scalar(
            msk[:], cidx[:], stt[:, 0:1], None, ALU.is_equal))
        em.op('vector', lambda e: e.tensor_scalar(
            mmtb[:], scor[:], BIGC, None, ALU.add))
        em.op('vector', lambda e: e.tensor_tensor(
            mmtb[:], mmtb[:], msk[:], ALU.mult))
        em.op('vector', lambda e: e.tensor_scalar(
            mmtb[:], mmtb[:], -BIGC, None, ALU.add))
        em.op('vector', lambda e: e.tensor_reduce(
            stt[:, 1:2], mmtb[:], AX.X, ALU.max))
        em.op('vector', lambda e: e.tensor_scalar(
            scor[:], mmtb[:], stt[:, 1:2], None, ALU.is_ge))
        em.op('vector', lambda e: e.tensor_tensor(
            scor[:], scor[:], iot64m[:], ALU.mult))
        em.op('vector', lambda e: e.tensor_scalar(
            scor[:], scor[:], BIGC, None, ALU.add))
        em.op('vector', lambda e: e.tensor_reduce(
            stt[:, 2:3], scor[:], AX.X, ALU.min))
        cohv = em.op('vector', lambda e: e.tensor_scalar(
            ohtb[:], iot64[:], stt[:, 2:3], None, ALU.is_equal), inc='ve')
        em.wait('gpsimd', 've', cohv)
        cdo = em.op('gpsimd', lambda e, tb=tb: e.dma_start(
            out=OH[tb * 128:(tb + 1) * 128, :], in_=ohtb[:]),
            inc='outo', inc_by=16)
        em.op('vector', lambda e: e.memset(xsl_sb[:], 0.0))
        for i in range(NUM_TILES):
            em.op('vector', lambda e, i=i: e.scalar_tensor_tensor(
                xsl_sb[:], t2tb[:, i * D_SLICE:(i + 1) * D_SLICE],
                ohtb[:, i:i + 1], xsl_sb[:], ALU.mult, ALU.add))
        cxs = em.op('vector', lambda e: e.tensor_copy(
            xsl_sb[:], xsl_sb[:]), inc='ve')
        em.wait('gpsimd', 've', cxs)
        cds = em.op('gpsimd', lambda e, tb=tb: e.dma_start(
            out=XSL[tb * 128:(tb + 1) * 128, :], in_=xsl_sb[:]),
            inc='outs', inc_by=16)
        em.op('vector', lambda e: e.tensor_reduce(
            stt[:, 3:4], xsl_sb[:], AX.X, ALU.min))
        em.op('vector', lambda e: e.tensor_reduce(
            stt[:, 4:5], xsl_sb[:], AX.X, ALU.max))
        em.op('vector', lambda e: e.tensor_scalar(
            stt[:, 5:6], stt[:, 3:4], -BIGC, None, ALU.add))
        em.op('vector', lambda e: e.tensor_scalar(
            stt[:, 6:7], stt[:, 4:5], BIGC, None, ALU.add))
        cv1 = em.op('vector', lambda e: e.tensor_scalar(
            mmtb[:], ohtb[:], stt[:, 5:6], BIGC, ALU.mult, ALU.add),
            inc='ve')
        em.wait('tensor', 've', cv1)
        em.wait('tensor', 'sc')
        ct = em.op('tensor', lambda e: e.transpose(
            psm[:], mmtb[:, 0:64], ident_sb[:]), inc='te')
        em.wait('scalar', 'te', ct)
        cev = em.op('scalar', lambda e: e.activation(
            redsc[:], psm[:], AF.Copy), inc='sc')
        em.wait('vector', 'sc', cev)
        em.op('vector', lambda e: e.tensor_reduce(
            minacc[:, 2:3], redsc[:], AX.X, ALU.min))
        cv2 = em.op('vector', lambda e: e.tensor_tensor(
            minacc[:, 0:1], minacc[:, 0:1], minacc[:, 2:3], ALU.min),
            inc='ve')
        cv3 = em.op('vector', lambda e: e.tensor_scalar(
            mmtb[:], ohtb[:], stt[:, 6:7], -BIGC, ALU.mult, ALU.add),
            inc='ve')
        em.wait('tensor', 've', cv3)
        em.wait('tensor', 'sc')
        ct = em.op('tensor', lambda e: e.transpose(
            psm[:], mmtb[:, 0:64], ident_sb[:]), inc='te')
        em.wait('scalar', 'te', ct)
        cev = em.op('scalar', lambda e: e.activation(
            redsc[:], psm[:], AF.Copy), inc='sc')
        em.wait('vector', 'sc', cev)
        em.op('vector', lambda e: e.tensor_reduce(
            minacc[:, 3:4], redsc[:], AX.X, ALU.max))
        em.op('vector', lambda e: e.tensor_tensor(
            minacc[:, 1:2], minacc[:, 1:2], minacc[:, 3:4], ALU.max))
        em.wait('vector', 'outo', cdo)
        em.wait('vector', 'outs', cds)

    cfin = em.op('vector', lambda e: e.tensor_copy(
        minacc[:, 0:2], minacc[:, 0:2]), inc='ve')
    em.wait('gpsimd', 've', cfin)
    em.dma('gpsimd', MINP, minacc[:, 0:1], inc='outm')
    em.dma('gpsimd', MAXP, minacc[:, 1:2], inc='outm')
    em.wait('gpsimd', 'outm')
    em.wait('gpsimd', 'outx')



def build_launchB():
    nc = bass.Bass('TRN2', target_bir_lowering=False)
    io = declare_io_B(nc)
    stack = contextlib.ExitStack()
    with stack:
        em = Em(nc, stack)
        emit_B(nc, em, stack, io)
        em.finalize()
    return nc


def host_prep_launchB(inputs, qts, aparts):
    X = np.asarray(inputs['L0_attn_X'])[0]
    W_out = np.ascontiguousarray(np.asarray(inputs['L0_attn_W_out']))
    b_out = np.asarray(inputs['L0_attn_b_out']).reshape(1, D)
    gb2 = np.stack([np.asarray(inputs['L0_ffn_ln_gamma']),
                    np.asarray(inputs['L0_ffn_ln_beta'])], axis=0)
    slopes = np.ascontiguousarray(
        np.asarray(inputs['L0_ffn_spline_slopes']).reshape(NUM_TILES,
                                                           D_SLICE * GRID))
    ones = np.ones((128, 128), np.float32)
    ident = np.eye(128, dtype=np.float32)
    cvals = np.array([[EPS, -0.5, 0, 0, 0, 0, 0, 0]], np.float32)
    tab = np.zeros((3, 64), np.float32)
    tab[0] = np.arange(64)
    tab[1] = np.arange(64) // 8
    tab[2, 0:8] = np.arange(8)
    ap = np.stack(aparts, axis=0).reshape(NCORES, D)
    in_maps = []
    for c in range(NCORES):
        in_maps.append(dict(
            x0=np.ascontiguousarray(X[c * TPC:(c + 1) * TPC]),
            qt=qts[c], aparts=ap, wout=W_out, bout=b_out, gb2=gb2,
            slopes=slopes, ones=ones, ident=ident, cvals=cvals, tab=tab))
    return in_maps


def declare_io_C(nc, internal=()):
    io = {}
    io['x1'] = nc.dram_tensor('x1', [TPC, D], F32, kind=None if 'x1' in internal else 'ExternalInput').ap()
    io['oh'] = nc.dram_tensor('oh', [TPC, 64], F32, kind=None if 'oh' in internal else 'ExternalInput').ap()
    io['xsl'] = nc.dram_tensor('xsl', [TPC, 32], F32, kind=None if 'xsl' in internal else 'ExternalInput').ap()
    io['mins'] = nc.dram_tensor('mins', [64, NCORES], F32, kind=None if 'mins' in internal else 'ExternalInput').ap()
    io['maxs'] = nc.dram_tensor('maxs', [64, NCORES], F32, kind=None if 'maxs' in internal else 'ExternalInput').ap()
    io['bases'] = nc.dram_tensor('bases', [64, 512], F32, kind=None if 'bases' in internal else 'ExternalInput').ap()
    io['slopes'] = nc.dram_tensor('slopes', [64, 512], F32, kind=None if 'slopes' in internal else 'ExternalInput').ap()
    io['osc'] = nc.dram_tensor('osc', [64, 1], F32, kind=None if 'osc' in internal else 'ExternalInput').ap()
    io['gb1'] = nc.dram_tensor('gb1', [2, D], F32, kind=None if 'gb1' in internal else 'ExternalInput').ap()
    io['ident'] = nc.dram_tensor('ident', [128, 128], F32, kind=None if 'ident' in internal else 'ExternalInput').ap()
    io['cvals'] = nc.dram_tensor('cvals', [1, 8], F32, kind=None if 'cvals' in internal else 'ExternalInput').ap()
    io['x2'] = nc.dram_tensor('x2', [TPC, D], F32, kind=None if 'x2' in internal else 'ExternalOutput').ap()
    io['ut'] = nc.dram_tensor('ut', [D, TPC], BF16, kind=None if 'ut' in internal else 'ExternalOutput').ap()
    return io


def emit_C(nc, em, stack, io):
    X1 = io['x1']
    OH = io['oh']
    XSL = io['xsl']
    MINS = io.get('mins')
    MAXS = io.get('maxs')
    BASES = io['bases']
    SLOPES = io['slopes']
    OSC = io['osc']
    GB1 = io['gb1']
    IDENT = io['ident']
    CVALS = io['cvals']
    X2 = io['x2']
    UT = io['ut']
    sb = lambda name, shp, dt: stack.enter_context(
        nc.sbuf_tensor(name, shp, dt, allow_name_mangling=True))
    xtb = sb('xtb', [128, D], F32)
    t1tb = sb('t1tb', [128, D], F32)
    sqtb = sb('sqtb', [128, D], F32)
    stt = sb('stt', [128, 16], F32)
    ident_sb = sb('ident_sb', [128, 128], F32)
    cv_sb = sb('cv_sb', [128, 8], F32)
    gb_sb = sb('gb_sb', [128, 2 * NKB], F32)
    rhs = sb('rhs', [64, 1048], F32)
    ohtb = sb('ohtb', [128, 64], F32)
    ohT = sb('ohT', [64, 128], F32)
    xsl_sb = sb('xsl_sb', [128, 32], F32)
    btok = sb('btok', [128, 512], F32)
    stok = sb('stok', [128, 512], F32)
    mm3 = sb('mm3', [128, 3], F32)
    u_sb = sb('u_sb', [128, 32], F32)
    gi_sb = sb('gi_sb', [128, 32], F32)
    xl_sb = sb('xl_sb', [128, 32], F32)
    f_sb = sb('f_sb', [128, 512], F32)
    sel_sb = sb('sel_sb', [128, 32], F32)
    tmp32 = sb('tmp32', [128, 32], F32)
    dsb = sb('dsb', [128, 32], F32)
    utst = [sb('utst0', [128, 128], BF16), sb('utst1', [128, 128], BF16)]
    utgate = [0, 0]
    psm = stack.enter_context(nc.psum_tensor('psm', [64, 128], F32, allow_name_mangling=True))
    psg = stack.enter_context(nc.psum_tensor('psg', [128, 512], F32, allow_name_mangling=True))
    psh = stack.enter_context(nc.psum_tensor('psh', [128, 512], F32, allow_name_mangling=True))
    ps3 = stack.enter_context(nc.psum_tensor('ps3', [128, 3], F32, allow_name_mangling=True))
    ppt = PP(em, stack, nc, 'pst', [128, 128])

    for s in ['ldc', 'ldx', 'ldo', 'lds', 'outx', 'outu', 'sc', 've',
              'te']:
        em.sem(s)

    em.setup_consts([EPS, -0.5], cv_sb, CVALS)
    em.dma('sync', ident_sb[:], IDENT, inc='ldc')
    for i in range(2):
        for kb in range(NKB):
            em.dma('sync', gb_sb[:, i * NKB + kb:i * NKB + kb + 1],
                   GB1[i:i + 1, kb * 128:(kb + 1) * 128].rearrange(
                       'a p -> p a'), inc='ldc')
    em.dma('sync', rhs[:, 0:512], BASES, inc='ldc')
    em.dma('sync', rhs[:, 512:1024], SLOPES, inc='ldc')
    em.dma('sync', rhs[:, 1024:1025], OSC, inc='ldc')
    if 'minr' in io:
        em.dma('sync', rhs[:, 1025:1026], io['minr'], inc='ldc')
        em.dma('sync', rhs[:, 1026:1027], io['maxr'], inc='ldc')
        em.wait('vector', 'ldc')
        em.wait('scalar', 'ldc')
        em.wait('tensor', 'ldc')
        cmm = em.op('vector', lambda e: e.tensor_copy(
            rhs[:, 1025:1027], rhs[:, 1025:1027]), inc='ve')
    else:
        em.dma('sync', rhs[:, 1027:1027 + NCORES], MINS, inc='ldc')
        em.dma('sync', rhs[:, 1027 + NCORES:1027 + 2 * NCORES], MAXS,
               inc='ldc')
        em.wait('vector', 'ldc')
        em.wait('scalar', 'ldc')
        em.wait('tensor', 'ldc')
        em.op('vector', lambda e: e.tensor_reduce(
            rhs[:, 1025:1026], rhs[:, 1027:1027 + NCORES], AX.X, ALU.min))
        cmm = em.op('vector', lambda e: e.tensor_reduce(
            rhs[:, 1026:1027], rhs[:, 1027 + NCORES:1027 + 2 * NCORES],
            AX.X, ALU.max), inc='ve')
    em.wait('tensor', 've', cmm)

    for tb in range(NTB):
        r0, r1 = tb * 128, (tb + 1) * 128
        em.wait('sync', 'sc')
        em.wait('sync', 've')
        cx = em.dma('sync', xtb[:], X1[r0:r1, :], inc='ldx')
        co = em.dma('sync', ohtb[:], OH[r0:r1, :], inc='ldo')
        cs = em.dma('sync', xsl_sb[:], XSL[r0:r1, :], inc='lds')
        em.wait('scalar', 'ldx', cx)
        em.wait('vector', 'ldx', cx)
        emit_plain_ln(em, xtb[:], t1tb[:], stt, sqtb[:])
        em.wait('tensor', 'ldo', co)
        em.wait('tensor', 'sc')
        ct = em.op('tensor', lambda e: e.transpose(
            psm[:], ohtb[:, 0:64], ident_sb[:]), inc='te')
        em.wait('scalar', 'te', ct)
        cev = em.op('scalar', lambda e: e.activation(
            ohT[:], psm[:], AF.Copy), inc='sc')
        em.wait('tensor', 'sc', cev)
        cm1 = em.op('tensor', lambda e: e.matmul(
            psg[:], ohT[:], rhs[:, 0:512], start=True, stop=True),
            inc='te')
        cm2 = em.op('tensor', lambda e: e.matmul(
            psh[:], ohT[:], rhs[:, 512:1024], start=True, stop=True),
            inc='te')
        cm3 = em.op('tensor', lambda e: e.matmul(
            ps3[:], ohT[:], rhs[:, 1024:1027], start=True, stop=True),
            inc='te')
        em.wait('scalar', 'te', cm3)
        em.op('scalar', lambda e: e.activation(btok[:], psg[:], AF.Copy))
        em.op('scalar', lambda e: e.activation(stok[:], psh[:], AF.Copy))
        cev = em.op('scalar', lambda e: e.activation(
            mm3[:], ps3[:], AF.Copy), inc='sc')
        em.wait('vector', 'sc', cev)
        em.wait('vector', 'lds', cs)
        em.op('vector', lambda e: e.tensor_scalar(
            stt[:, 8:9], mm3[:, 1:2], -1.0, None, ALU.mult))
        em.op('vector', lambda e: e.scalar_tensor_tensor(
            stt[:, 9:10], mm3[:, 1:2], -1.0, mm3[:, 2:3], ALU.mult,
            ALU.add))
        em.op('vector', lambda e: e.tensor_scalar(
            stt[:, 9:10], stt[:, 9:10], 1e-8, None, ALU.add))
        em.op('vector', lambda e: e.reciprocal(
            stt[:, 10:11], stt[:, 9:10]))
        em.op('vector', lambda e: e.tensor_scalar(
            u_sb[:], xsl_sb[:], stt[:, 8:9], stt[:, 10:11], ALU.add,
            ALU.mult))
        em.op('vector', lambda e: e.tensor_scalar(
            u_sb[:], u_sb[:], 0.0, 1.0 - 1e-06, ALU.max, ALU.min))
        em.op('vector', lambda e: e.tensor_scalar(
            u_sb[:], u_sb[:], float(GRID), None, ALU.mult))
        em.op('vector', lambda e: e.tensor_scalar(
            gi_sb[:], u_sb[:], 1.0, None, ALU.is_ge))
        for kk in range(2, GRID):
            em.op('vector', lambda e, kk=kk: e.scalar_tensor_tensor(
                gi_sb[:], u_sb[:], float(kk), gi_sb[:], ALU.is_ge,
                ALU.add))
        em.op('vector', lambda e: e.tensor_tensor(
            xl_sb[:], u_sb[:], gi_sb[:], ALU.subtract))
        xlb = xl_sb[:].unsqueeze(2).broadcast_to([128, 32, GRID])
        em.op('vector', lambda e, xlb=xlb: e.tensor_tensor(
            f_sb[:].rearrange('p (j g) -> p j g', g=GRID),
            stok[:].rearrange('p (j g) -> p j g', g=GRID), xlb, ALU.mult))
        em.op('vector', lambda e: e.tensor_tensor(
            f_sb[:], f_sb[:], btok[:], ALU.add))
        em.op('vector', lambda e: e.memset(dsb[:], 0.0))
        for g in range(GRID):
            em.op('vector', lambda e, g=g: e.tensor_scalar(
                sel_sb[:], gi_sb[:], float(g), None, ALU.is_equal))
            fsl = f_sb[:].rearrange('p (j g) -> p g j', g=GRID)[:, g, :]
            em.op('vector', lambda e, fsl=fsl: e.tensor_tensor(
                tmp32[:], sel_sb[:], fsl, ALU.mult))
            em.op('vector', lambda e: e.tensor_tensor(
                dsb[:], dsb[:], tmp32[:], ALU.add))
        em.op('vector', lambda e: e.tensor_scalar(
            dsb[:], dsb[:], mm3[:, 0:1], None, ALU.mult))
        em.wait('vector', 'ldo', co)
        em.wait('vector', 'sc')
        for i in range(NUM_TILES):
            em.op('vector', lambda e, i=i: e.scalar_tensor_tensor(
                t1tb[:, i * D_SLICE:(i + 1) * D_SLICE], dsb[:],
                ohtb[:, i:i + 1], t1tb[:, i * D_SLICE:(i + 1) * D_SLICE],
                ALU.mult, ALU.add))
        emit_plain_ln(em, t1tb[:], t1tb[:], stt, sqtb[:])
        em.wait('vector', 'sc')
        cv = em.op('vector', lambda e: e.tensor_tensor(
            xtb[:], xtb[:], t1tb[:], ALU.add), inc='ve')
        em.wait('gpsimd', 've', cv)
        cdx = em.op('gpsimd', lambda e, r0=r0, r1=r1: e.dma_start(
            out=X2[r0:r1, :], in_=xtb[:]), inc='outx', inc_by=16)
        em.wait('scalar', 've', cv)
        c5 = emit_plain_ln(em, xtb[:], t1tb[:], stt, sqtb[:])
        em.wait('tensor', 'sc', c5)
        for kb in range(NKB):
            pstk = ppt.nxt(em)
            ct = em.op('tensor', lambda e, kb=kb, pstk=pstk: e.transpose(
                pstk[:], t1tb[:, kb * 128:(kb + 1) * 128], ident_sb[:]),
                inc='te')
            em.wait('scalar', 'te', ct)
            kp = kb % 2
            em.wait('scalar', 'outu', utgate[kp])
            cu = em.op('scalar', lambda e, kb=kb, pstk=pstk, kp=kp:
                       e.activation(
                utst[kp][:], pstk[:], AF.Identity,
                scale=gb_sb[:, kb:kb + 1],
                bias=gb_sb[:, NKB + kb:NKB + kb + 1]), inc='sc')
            ppt.done('sc', cu)
            em.wait('gpsimd', 'sc', cu)
            cdu = em.op('gpsimd', lambda e, kb=kb, r0=r0, kp=kp: e.dma_start(
                out=UT[kb * 128:(kb + 1) * 128, r0:r0 + 128],
                in_=utst[kp][:]), inc='outu', inc_by=16)
            utgate[kp] = cdu
        em.wait('sync', 'outx', cdx)

    em.wait('gpsimd', 'outx')
    em.wait('gpsimd', 'outu')



def build_launchC():
    nc = bass.Bass('TRN2', target_bir_lowering=False)
    io = declare_io_C(nc)
    stack = contextlib.ExitStack()
    with stack:
        em = Em(nc, stack)
        emit_C(nc, em, stack, io)
        em.finalize()
    return nc


def host_prep_launchC(inputs, x1s, ohs, xsls, minps, maxps):
    bases = np.ascontiguousarray(
        np.asarray(inputs['L0_ffn_spline_bases']).reshape(64, 512))
    slopes = np.ascontiguousarray(
        np.asarray(inputs['L0_ffn_spline_slopes']).reshape(64, 512))
    osc = np.asarray(inputs['L0_ffn_output_scale']).reshape(64, 1)
    gb1 = np.stack([np.asarray(inputs['L1_attn_ln1_w']),
                    np.asarray(inputs['L1_attn_ln1_b'])], axis=0)
    mins = np.ascontiguousarray(np.stack([m[:, 0] for m in minps], axis=1))
    maxs = np.ascontiguousarray(np.stack([m[:, 0] for m in maxps], axis=1))
    ident = np.eye(128, dtype=np.float32)
    cvals = np.array([[EPS, -0.5, 0, 0, 0, 0, 0, 0]], np.float32)
    in_maps = []
    for c in range(NCORES):
        in_maps.append(dict(
            x1=x1s[c], oh=ohs[c], xsl=xsls[c], mins=mins, maxs=maxs,
            bases=bases, slopes=slopes, osc=osc, gb1=gb1, ident=ident,
            cvals=cvals))
    return in_maps


TCH = 1024
NCH = T // TCH

def declare_io_D(nc, internal=()):
    io = {}
    io['utall'] = nc.dram_tensor('utall', [NCORES, D, TPC], BF16, kind=None if 'utall' in internal else 'ExternalInput').ap()
    io['wxmt'] = nc.dram_tensor('wxmt', [D, DPC], F32, kind=None if 'wxmt' in internal else 'ExternalInput').ap()
    io['wzt'] = nc.dram_tensor('wzt', [D, DPC], F32, kind=None if 'wzt' in internal else 'ExternalInput').ap()
    io['cw'] = nc.dram_tensor('cw', [4, 128, D_CONV], F32, kind=None if 'cw' in internal else 'ExternalInput').ap()
    io['cb'] = nc.dram_tensor('cb', [4, 128], F32, kind=None if 'cb' in internal else 'ExternalInput').ap()
    io['xpt'] = nc.dram_tensor('xpt', [DPC, 160], F32, kind=None if 'xpt' in internal else 'ExternalInput').ap()
    io['dtpt'] = nc.dram_tensor('dtpt', [DT_RANK, DPC], F32, kind=None if 'dtpt' in internal else 'ExternalInput').ap()
    io['dtb'] = nc.dram_tensor('dtb', [4, 128], F32, kind=None if 'dtb' in internal else 'ExternalInput').ap()
    io['alog'] = nc.dram_tensor('alog', [4, 128, D_STATE], F32, kind=None if 'alog' in internal else 'ExternalInput').ap()
    io['dsk'] = nc.dram_tensor('dsk', [4, 128], F32, kind=None if 'dsk' in internal else 'ExternalInput').ap()
    io['cvals'] = nc.dram_tensor('cvals', [1, 8], F32, kind=None if 'cvals' in internal else 'ExternalInput').ap()
    io['g'] = nc.dram_tensor('g', [NCORES, DPC, TPC], BF16, kind=None if 'g' in internal else 'ExternalOutput').ap()
    io['xcst'] = nc.dram_tensor('xcst', [DPC, T], F32).ap()
    io['zst'] = nc.dram_tensor('zst', [DPC, T], F32).ap()
    io['xdblp'] = nc.dram_tensor('xdblp', [160, T], F32).ap()
    io['xdbl'] = nc.dram_tensor('xdbl', [160, T], F32).ap()
    return io


def emit_D(nc, em, stack, io):
    XC = io['xcst']
    Z = io['zst']
    UTALL = io['utall']
    WXMT = io['wxmt']
    WZT = io['wzt']
    CW = io['cw']
    CB = io['cb']
    XPT = io['xpt']
    DTPT = io['dtpt']
    DTB = io['dtb']
    ALOG = io['alog']
    DSK = io['dsk']
    CVALS = io['cvals']
    G = io['g']
    XCST = io['xcst']
    ZST = io['zst']
    XDBLP = io['xdblp']
    XDBL = io['xdbl']
    sb = lambda name, shp, dt: stack.enter_context(
        nc.sbuf_tensor(name, shp, dt, allow_name_mangling=True))
    wxm = sb('wxm', [128, NKB * DPC], BF16)
    wz = sb('wz', [128, NKB * DPC], BF16)
    xpt_bf = sb('xpt_bf', [128, 4 * 160], BF16)
    dtpt_bf = sb('dtpt_bf', [128, DPC], BF16)
    wld = sb('wld', [128, DPC], F32)
    utch = sb('utch', [128, NKB * 512], BF16)
    ext = sb('ext', [128, 4 * 515], F32)
    xcsb = sb('xcsb', [128, 512], F32)
    xcbf = sb('xcbf', [128, 512], BF16)
    zsb = sb('zsb', [128, 512], F32)
    xdsb0 = sb('xdsb0', [128, 512], F32)
    xdsb1 = sb('xdsb1', [32, 512], F32)
    cw_sb = sb('cw_sb', [128, 16], F32)
    cb_sb = sb('cb_sb', [128, 4], F32)
    dtb_sb = sb('dtb_sb', [128, 4], F32)
    dsk_sb = sb('dsk_sb', [128, 4], F32)
    a_sb4 = sb('a_sb4', [128, 4 * D_STATE], F32)
    st_sb = sb('st_sb', [128, 4 * D_STATE], F32)
    cv_sb = sb('cv_sb', [128, 8], F32)
    dt_sb = sb('dt_sb', [128, 4 * TCH], F32)
    dtx_sb = sb('dtx_sb', [128, 4 * TCH], F32)
    xcch = sb('xcch', [128, 4 * TCH], F32)
    yacc = sb('yacc', [128, 4 * TCH], F32)
    asb = [sb('asb0', [128, TCH], F32), sb('asb1', [128, TCH], F32)]
    usb = sb('usb', [128, TCH], F32)
    wsb = [sb('wsb0', [128, TCH], F32), sb('wsb1', [128, TCH], F32)]
    hsb = sb('hsb', [128, TCH], F32)
    brB = [sb(f'brB{i}', [128, TCH], F32) for i in range(2)]
    brC = [sb(f'brC{i}', [128, TCH], F32) for i in range(2)]
    xdbb = sb('xdbb', [128, TCH], BF16)
    zch = sb('zch', [128, TCH], F32)
    gb16 = sb('gb16', [128, TCH], BF16)
    sp1 = sb('sp1', [128, 512], F32)
    sp2 = sb('sp2', [128, 512], F32)
    ppx = PP(em, stack, nc, 'psx', [128, 512])
    ppz = PP(em, stack, nc, 'psz', [128, 512])
    psd0 = stack.enter_context(nc.psum_tensor('psd0', [128, 512], F32, allow_name_mangling=True))
    psd1 = stack.enter_context(nc.psum_tensor('psd1', [32, 512], F32, allow_name_mangling=True))
    psdt = stack.enter_context(nc.psum_tensor('psdt', [128, 512], F32, allow_name_mangling=True))

    for s in ['ldc', 'ldw', 'ldu', 'ldxd', 'ldxc', 'ldz', 'ldbr0',
              'ldbr1', 'outxc', 'outz', 'outxd', 'outg', 'cc', 'bar',
              'po', 'sc', 've', 'te']:
        em.sem(s)

    em.setup_consts([EPS, -0.5, 1.0, -1.0], cv_sb, CVALS)
    for pb in range(4):
        for t_, srcap in [(cw_sb, None), ]:
            pass
    for pb in range(4):
        em.dma('sync', cw_sb[:, pb * 4:(pb + 1) * 4], CW[pb], inc='ldc')
        em.dma('sync', cb_sb[:, pb:pb + 1],
               CB[pb:pb + 1, :].rearrange('a p -> p a'), inc='ldc')
        em.dma('sync', dtb_sb[:, pb:pb + 1],
               DTB[pb:pb + 1, :].rearrange('a p -> p a'), inc='ldc')
        em.dma('sync', dsk_sb[:, pb:pb + 1],
               DSK[pb:pb + 1, :].rearrange('a p -> p a'), inc='ldc')
        em.dma('sync', a_sb4[:, pb * D_STATE:(pb + 1) * D_STATE],
               ALOG[pb], inc='ldc')
    em.wait('scalar', 'ldc')
    em.wait('vector', 'ldc')
    em.op('scalar', lambda e: e.activation(a_sb4[:], a_sb4[:], AF.Exp))
    em.wait('vector', 'sc')
    em.op('vector', lambda e: e.tensor_scalar(a_sb4[:], a_sb4[:], -1.0,
                                              None, ALU.mult))
    em.op('vector', lambda e: e.memset(st_sb[:], 0.0))
    em.op('vector', lambda e: e.memset(ext[:], 0.0))
    def loadw(dst, W, ncols):
        for kb in range(W.shape[0] // 128):
            c = em.dma('sync', wld[:, 0:ncols], W[kb * 128:(kb + 1) * 128, :],
                       inc='ldw')
            em.wait('scalar', 'ldw', c)
            c2 = em.op('scalar', lambda e, kb=kb, dst=dst, ncols=ncols:
                       e.activation(dst[:, kb * ncols:(kb + 1) * ncols],
                                    wld[:, 0:ncols], AF.Copy), inc='sc')
            em.wait('sync', 'sc', c2)
    loadw(wxm, WXMT, DPC)
    loadw(wz, WZT, DPC)
    loadw(xpt_bf, XPT, 160)
    loadw(dtpt_bf, DTPT, DPC)
    em.wait('tensor', 'sc')

    em.wait('scalar', 've')
    em.wait('tensor', 've')
    for tch in range(T // 512):
        j, w0 = tch // 4, (tch % 4) * 512
        for kb in range(NKB):
            em.dma('sync', utch[:, kb * 512:(kb + 1) * 512],
                   UTALL[j, kb * 128:(kb + 1) * 128, w0:w0 + 512],
                   inc='ldu')
        cu = em.cnt['ldu']
        em.wait('tensor', 'ldu', cu)
        for pb in range(4):
            parts = [(wxm[:, kb * DPC + pb * 128:kb * DPC + (pb + 1) * 128],
                      utch[:, kb * 512:(kb + 1) * 512])
                     for kb in range(NKB)]
            psx = ppx.nxt(em)
            cm = mm_accum(em, psx[:], parts, inc='te')
            em.wait('scalar', 'te', cm)
            cev = em.op('scalar', lambda e, pb=pb, psx=psx: e.activation(
                ext[:, pb * 515 + 3:(pb + 1) * 515], psx[:], AF.Copy),
                inc='sc')
            ppx.done('sc', cev)
            em.wait('vector', 'sc', cev)
            em.op('vector', lambda e, pb=pb: e.tensor_scalar(
                xcsb[:], ext[:, pb * 515:pb * 515 + 512],
                cw_sb[:, pb * 4:pb * 4 + 1], None, ALU.mult))
            for k in range(1, 4):
                em.op('vector', lambda e, pb=pb, k=k: e.scalar_tensor_tensor(
                    xcsb[:], ext[:, pb * 515 + k:pb * 515 + k + 512],
                    cw_sb[:, pb * 4 + k:pb * 4 + k + 1], xcsb[:],
                    ALU.mult, ALU.add))
            cvt = em.op('vector', lambda e, pb=pb: e.tensor_copy(
                ext[:, pb * 515:pb * 515 + 3],
                ext[:, pb * 515 + 512:pb * 515 + 515]), inc='ve')
            em.wait('scalar', 've', cvt)
            csl = em.op('scalar', lambda e, pb=pb: e.activation(
                xcsb[:], xcsb[:], AF.Silu, bias=cb_sb[:, pb:pb + 1]),
                inc='sc')
            em.wait('gpsimd', 'sc', csl)
            cdx = em.op('gpsimd', lambda e, pb=pb, tch=tch: e.dma_start(
                out=XC[pb * 128:(pb + 1) * 128, tch * 512:(tch + 1) * 512],
                in_=xcsb[:]), inc='outxc', inc_by=16)
            cbf = em.op('scalar', lambda e: e.activation(
                xcbf[:], xcsb[:], AF.Copy), inc='sc')
            em.wait('tensor', 'sc', cbf)
            em.op('tensor', lambda e, pb=pb: e.matmul(
                psd0[:], xpt_bf[:, pb * 160:pb * 160 + 128], xcbf[:],
                start=(pb == 0), stop=(pb == 3), skip_group_check=True))
            cm2 = em.op('tensor', lambda e, pb=pb: e.matmul(
                psd1[:], xpt_bf[:, pb * 160 + 128:pb * 160 + 160], xcbf[:],
                start=(pb == 0), stop=(pb == 3), skip_group_check=True),
                inc='te')
            em.wait('scalar', 'outxc', cdx)
            em.wait('vector', 'outxc', cdx)
            parts = [(wz[:, kb * DPC + pb * 128:kb * DPC + (pb + 1) * 128],
                      utch[:, kb * 512:(kb + 1) * 512])
                     for kb in range(NKB)]
            psz = ppz.nxt(em)
            cmz = mm_accum(em, psz[:], parts, inc='te')
            em.wait('scalar', 'te', cmz)
            cez = em.op('scalar', lambda e, psz=psz: e.activation(
                zsb[:], psz[:], AF.Copy), inc='sc')
            ppz.done('sc', cez)
            em.wait('gpsimd', 'sc', cez)
            cdz = em.op('gpsimd', lambda e, pb=pb, tch=tch: e.dma_start(
                out=Z[pb * 128:(pb + 1) * 128, tch * 512:(tch + 1) * 512],
                in_=zsb[:]), inc='outz', inc_by=16)
            em.wait('scalar', 'outz', cdz)
        em.wait('scalar', 'te', cm2)
        em.op('scalar', lambda e: e.activation(
            xdsb0[:], psd0[:], AF.Copy))
        cxd = em.op('scalar', lambda e: e.activation(
            xdsb1[:], psd1[:], AF.Copy), inc='sc')
        em.wait('tensor', 'sc', cxd)
        em.wait('gpsimd', 'sc', cxd)
        em.op('gpsimd', lambda e, tch=tch: e.dma_start(
            out=XDBLP[0:128, tch * 512:(tch + 1) * 512], in_=xdsb0[:]),
            inc='outxd', inc_by=16)
        cdd = em.op('gpsimd', lambda e, tch=tch: e.dma_start(
            out=XDBLP[128:160, tch * 512:(tch + 1) * 512], in_=xdsb1[:]),
            inc='outxd', inc_by=16)
        em.wait('scalar', 'outxd', cdd)
        em.wait('sync', 'te')

    em.wait('gpsimd', 'outxd')
    em.coll('AllReduce', ALU.add, [XDBLP.opt()], [XDBL.opt()], 'cc')
    em.wait('gpsimd', 'cc', 1)
    em.op('gpsimd', lambda e: e.engine_nop(), inc='bar', inc_by=1)
    cgate = em.cnt['bar']
    em.wait('sync', 'bar', cgate)
    em.wait('scalar', 'bar', cgate)
    em.wait('vector', 'bar', cgate)
    em.wait('tensor', 'bar', cgate)

    for ch in range(NCH):
        c0 = ch * TCH
        cxd = em.dma('sync', sp1[:, 0:512], XDBL[0:128, c0:c0 + 512],
                     inc='ldxd')
        for half in range(2):
            if half == 1:
                cxd = em.dma('sync', sp1[:, 0:512],
                             XDBL[0:128, c0 + 512:c0 + 1024], inc='ldxd')
            em.wait('scalar', 'ldxd', cxd)
            ch2 = em.op('scalar', lambda e, half=half: e.activation(
                xdbb[:, half * 512:(half + 1) * 512], sp1[:, 0:512],
                AF.Copy), inc='sc')
            em.wait('sync', 'sc', ch2)
        em.wait('tensor', 'sc')
        em.wait('sync', 've')
        for pb in range(4):
            em.dma('sync', xcch[:, pb * TCH:(pb + 1) * TCH],
                   XC[pb * 128:(pb + 1) * 128, c0:c0 + TCH], inc='ldxc')
        cxcall = em.cnt['ldxc']
        for pb in range(4):
            for tsub in range(TCH // 512):
                cm = em.op('tensor', lambda e, pb=pb, tsub=tsub: e.matmul(
                    psdt[:], dtpt_bf[:, pb * 128:(pb + 1) * 128],
                    xdbb[:, tsub * 512:(tsub + 1) * 512],
                    start=True, stop=True), inc='te')
                em.wait('scalar', 'te', cm)
                em.wait('scalar', 've')
                em.op('scalar', lambda e, pb=pb: e.activation(
                    sp1[:, 0:512], psdt[:], AF.Identity,
                    bias=dtb_sb[:, pb:pb + 1]))
                em.op('scalar', lambda e: e.activation(
                    sp2[:, 0:512], sp1[:, 0:512], AF.Abs))
                em.op('scalar', lambda e: e.activation(
                    sp2[:, 0:512], sp2[:, 0:512], AF.Exp, scale=-1.0))
                em.op('scalar', lambda e: e.activation(
                    sp2[:, 0:512], sp2[:, 0:512], AF.Ln, bias=1.0))
                csr = em.op('scalar', lambda e: e.activation(
                    sp1[:, 0:512], sp1[:, 0:512], AF.Relu), inc='sc')
                em.wait('tensor', 'sc', csr)
                em.wait('vector', 'sc', csr)
                cdt = em.op('vector', lambda e, pb=pb, tsub=tsub:
                            e.tensor_tensor(
                    dt_sb[:, pb * TCH + tsub * 512:pb * TCH + (tsub + 1) * 512],
                    sp1[:, 0:512], sp2[:, 0:512], ALU.add), inc='ve')
            em.wait('vector', 'ldxc', cxcall)
            em.op('vector', lambda e, pb=pb: e.tensor_tensor(
                dtx_sb[:, pb * TCH:(pb + 1) * TCH],
                dt_sb[:, pb * TCH:(pb + 1) * TCH],
                xcch[:, pb * TCH:(pb + 1) * TCH], ALU.mult))
        em.wait('scalar', 've')
        asb_gate = [0, 0]
        brep_gate = [0, 0]
        wgate2 = [0, 0]
        for s in range(D_STATE):
            par = s % 2
            em.wait('sync', 've', brep_gate[par])
            cbr = em.dma('sync', brB[par][:],
                         XDBL[128 + s:129 + s, c0:c0 + TCH]
                         .partition_broadcast(128).squeeze(1),
                         inc=f'ldbr{par}')
            ccr = em.dma('sync', brC[par][:],
                         XDBL[144 + s:145 + s, c0:c0 + TCH]
                         .partition_broadcast(128).squeeze(1),
                         inc=f'ldbr{par}')
            em.wait('vector', f'ldbr{par}', ccr)
            for pb in range(4):
                col = pb * D_STATE + s
                it = s * 4 + pb
                ap2 = it % 2
                em.wait('scalar', 've', asb_gate[ap2])
                ca = em.op('scalar', lambda e, pb=pb, col=col, ap2=ap2:
                           e.activation(
                    asb[ap2][:], dt_sb[:, pb * TCH:(pb + 1) * TCH], AF.Exp,
                    scale=a_sb4[:, col:col + 1]), inc='sc')
                em.op('vector', lambda e, pb=pb, par=par: e.tensor_tensor(
                    usb[:], dtx_sb[:, pb * TCH:(pb + 1) * TCH],
                    brB[par][:], ALU.mult))
                em.wait('vector', 'sc', ca)
                em.op('vector', lambda e, col=col, ap2=ap2:
                      e.tensor_tensor_scan(
                    hsb[:], asb[ap2][:], usb[:], st_sb[:, col:col + 1],
                    ALU.mult, ALU.add))
                asb_gate[ap2] = em.cnt['ve']
                em.op('vector', lambda e, col=col: e.tensor_copy(
                    st_sb[:, col:col + 1], hsb[:, TCH - 1:TCH]))
                em.op('vector', lambda e, par=par: e.tensor_tensor(
                    usb[:], hsb[:], brC[par][:], ALU.mult))
                sfirst = (s == 0)
                if sfirst:
                    cy = em.op('vector', lambda e, pb=pb: e.tensor_copy(
                        yacc[:, pb * TCH:(pb + 1) * TCH], usb[:]),
                        inc='ve')
                else:
                    cy = em.op('vector', lambda e, pb=pb: e.tensor_tensor(
                        yacc[:, pb * TCH:(pb + 1) * TCH],
                        yacc[:, pb * TCH:(pb + 1) * TCH], usb[:], ALU.add),
                        inc='ve')
            brep_gate[par] = em.cnt['ve']
        for pb in range(4):
            cz = em.dma('sync', zch[:], Z[pb * 128:(pb + 1) * 128,
                                          c0:c0 + TCH], inc='ldz')
            em.wait('scalar', 'ldz', cz)
            csz = em.op('scalar', lambda e: e.activation(
                zch[:], zch[:], AF.Silu), inc='sc')
            em.wait('vector', 'sc', csz)
            em.op('vector', lambda e, pb=pb: e.tensor_scalar(
                usb[:], xcch[:, pb * TCH:(pb + 1) * TCH],
                dsk_sb[:, pb:pb + 1], None, ALU.mult))
            em.op('vector', lambda e, pb=pb: e.tensor_tensor(
                usb[:], usb[:], yacc[:, pb * TCH:(pb + 1) * TCH], ALU.add))
            cg = em.op('vector', lambda e: e.tensor_tensor(
                usb[:], usb[:], zch[:], ALU.mult), inc='ve')
            em.wait('scalar', 've', cg)
            cgb = em.op('scalar', lambda e: e.activation(
                gb16[:], usb[:], AF.Copy), inc='sc')
            em.wait('gpsimd', 'sc', cgb)
            cdg = em.op('gpsimd', lambda e, ch=ch, pb=pb: e.dma_start(
                out=G[ch // 2, pb * 128:(pb + 1) * 128,
                      (ch % 2) * TCH:(ch % 2 + 1) * TCH], in_=gb16[:]),
                inc='outg', inc_by=16)
            em.wait('scalar', 'outg', cdg)
            em.wait('sync', 've')
    em.wait('gpsimd', 'outg')



def build_launchD():
    nc = bass.Bass('TRN2', target_bir_lowering=False)
    io = declare_io_D(nc)
    stack = contextlib.ExitStack()
    with stack:
        em = Em(nc, stack)
        emit_D(nc, em, stack, io)
        em.finalize()
    return nc


def host_prep_launchD(inputs, uts):
    utall = np.ascontiguousarray(np.stack(uts, axis=0))
    ipw = np.asarray(inputs['L1_attn_in_proj_w'])
    cw = np.asarray(inputs['L1_attn_conv_w'])[:, 0, :]
    cb = np.asarray(inputs['L1_attn_conv_b'])
    xpw = np.asarray(inputs['L1_attn_x_proj_w'])
    dpw = np.asarray(inputs['L1_attn_dt_proj_w'])
    dpb = np.asarray(inputs['L1_attn_dt_proj_b'])
    alog = np.asarray(inputs['L1_attn_A_log'])
    dsk = np.asarray(inputs['L1_attn_D_skip'])
    cvals = np.array([[EPS, -0.5, 1.0, -1.0, 0, 0, 0, 0]], np.float32)
    in_maps = []
    for c in range(NCORES):
        s0, s1 = c * DPC, (c + 1) * DPC
        in_maps.append(dict(
            utall=utall,
            wxmt=np.ascontiguousarray(ipw[s0:s1, :].T),
            wzt=np.ascontiguousarray(ipw[4096 + s0:4096 + s1, :].T),
            cw=np.ascontiguousarray(cw[s0:s1].reshape(4, 128, D_CONV)),
            cb=np.ascontiguousarray(cb[s0:s1].reshape(4, 128)),
            xpt=np.ascontiguousarray(xpw[:, s0:s1].T),
            dtpt=np.ascontiguousarray(dpw[s0:s1, :].T),
            dtb=np.ascontiguousarray(dpb[s0:s1].reshape(4, 128)),
            alog=np.ascontiguousarray(alog[s0:s1].reshape(4, 128, D_STATE)),
            dsk=np.ascontiguousarray(dsk[s0:s1].reshape(4, 128)),
            cvals=cvals))
    return in_maps


def declare_io_E(nc, internal=()):
    io = {}
    io['gfull'] = nc.dram_tensor('gfull', [D_INNER, TPC], BF16, kind=None if 'gfull' in internal else 'ExternalInput').ap()
    io['x2'] = nc.dram_tensor('x2', [TPC, D], F32, kind=None if 'x2' in internal else 'ExternalInput').ap()
    io['opt'] = nc.dram_tensor('opt', [D_INNER, D], F32, kind=None if 'opt' in internal else 'ExternalInput').ap()
    io['gb2l'] = nc.dram_tensor('gb2l', [2, D], F32, kind=None if 'gb2l' in internal else 'ExternalInput').ap()
    io['fpt'] = nc.dram_tensor('fpt', [D, HIDDEN], F32, kind=None if 'fpt' in internal else 'ExternalInput').ap()
    io['fot'] = nc.dram_tensor('fot', [D_INNER, D], F32, kind=None if 'fot' in internal else 'ExternalInput').ap()
    io['ident'] = nc.dram_tensor('ident', [128, 128], F32, kind=None if 'ident' in internal else 'ExternalInput').ap()
    io['cvals'] = nc.dram_tensor('cvals', [1, 8], F32, kind=None if 'cvals' in internal else 'ExternalInput').ap()
    io['x3'] = nc.dram_tensor('x3', [TPC, D], F32, kind=None if 'x3' in internal else 'ExternalOutput').ap()
    io['mix'] = nc.dram_tensor('mix', [TPC, D], F32).ap()
    io['ares'] = nc.dram_tensor('ares', [TPC, D], F32).ap()
    io['ffa'] = nc.dram_tensor('ffa', [D_INNER, TPC], BF16).ap()
    io['t6'] = nc.dram_tensor('t6', [TPC, D], F32).ap()

    return io


def emit_E(nc, em, stack, io):
    NH = 8
    WCH = D // NH
    GFULL = io['gfull']
    X2 = io['x2']
    OPT = io['opt']
    GB2L = io['gb2l']
    FPT = io['fpt']
    FOT = io['fot']
    IDENT = io['ident']
    CVALS = io['cvals']
    X3 = io['x3']
    MIX = io['mix']
    ARES = io['ares']
    FFA = io['ffa']
    T6 = io['t6']
    sb = lambda name, shp, dt: stack.enter_context(
        nc.sbuf_tensor(name, shp, dt, allow_name_mangling=True))
    big = sb('big', [128, 32 * TPC], BF16)
    warena = [sb('warena0', [128, 32 * WCH], BF16),
              sb('warena1', [128, 32 * WCH], BF16)]
    fa_bf = [sb('fa_bf0', [128, NKB * 128], BF16)] * 2
    fg_bf = [sb('fg_bf0', [128, NKB * 128], BF16)] * 2
    wlda = sb('wlda', [128, 16 * WCH], F32)
    wldg = sb('wldg', [128, 16 * WCH], F32)
    t1tb = wldg[:, 0:D]
    sq2 = wlda[:, 0:D]
    xtb = wlda[:, D:2 * D] if 16 * WCH >= 2 * D else None
    assert 16 * WCH >= 2 * D
    stt = sb('stt', [128, 16], F32)
    ident_sb = sb('ident_sb', [128, 128], F32)
    cv_sb = sb('cv_sb', [128, 8], F32)
    gb_sb = sb('gb_sb', [128, 2 * NKB], F32)
    sg_sb = sb('sg_sb', [128, 512], F32)
    ffat = sb('ffat', [128, 512], BF16)
    mixt = sg_sb[:, 0:WCH]
    arch = sg_sb[:, WCH:2 * WCH]
    ppy = PP(em, stack, nc, 'psy', [128, WCH])
    ppa = PP(em, stack, nc, 'psa', [128, 512])
    ppg = PP(em, stack, nc, 'psg', [128, 512])
    ppt = PP(em, stack, nc, 'pst', [128, 128])

    for s in ['ldc', 'ldg', 'ldw', 'ldwg', 'ldx', 'ldm', 'lda', 'ldf',
              'outm', 'outa', 'outf', 'outt', 'outx', 'sc', 've', 'te']:
        em.sem(s)

    em.setup_consts([EPS, -0.5], cv_sb, CVALS)
    em.dma('sync', ident_sb[:], IDENT, inc='ldc')
    for i in range(2):
        for kb in range(NKB):
            em.dma('sync', gb_sb[:, i * NKB + kb:i * NKB + kb + 1],
                   GB2L[i:i + 1, kb * 128:(kb + 1) * 128].rearrange(
                       'a p -> p a'), inc='ldc')
    em.wait('scalar', 'ldc')
    em.wait('vector', 'ldc')
    em.wait('tensor', 'ldc')

    for kb in range(32):
        em.dma('sync', big[:, kb * TPC:(kb + 1) * TPC],
               GFULL[kb * 128:(kb + 1) * 128, :], inc='ldg')
    cg = em.cnt['ldg']
    em.wait('tensor', 'ldg', cg)

    wgate = [0, 0]
    def load_chunk(W, nh, par):
        em.wait('scalar', 'te', wgate[par])
        srcw = W[:, nh * WCH:(nh + 1) * WCH].rearrange(
            '(kb p) c -> p kb c', p=128)
        half = 16 * WCH
        cwa = em.dma('sync',
                     wlda[:].rearrange('p (kb c) -> p kb c', kb=16),
                     srcw[:, 0:16, :], inc='ldw')
        cwg = em.dma('sync',
                     wldg[:].rearrange('p (kb c) -> p kb c', kb=16),
                     srcw[:, 16:32, :], inc='ldwg')
        em.wait('scalar', 'ldw', cwa)
        em.op('scalar', lambda e, par=par: e.activation(
            warena[par][:, 0:half], wlda[:], AF.Copy), inc='sc')
        em.wait('scalar', 'ldwg', cwg)
        c2 = em.op('scalar', lambda e, par=par, half=half: e.activation(
            warena[par][:, half:2 * half], wldg[:], AF.Copy), inc='sc')
        em.wait('sync', 'sc', c2)
        return em.cnt['sc']

    for nh in range(NH):
        par = nh % 2
        cwl = load_chunk(OPT, nh, par)
        em.wait('tensor', 'sc', cwl)
        for tb in range(NTB):
            parts = [(big[:, kb * TPC + tb * 128:kb * TPC + (tb + 1) * 128],
                      warena[par][:, kb * WCH:(kb + 1) * WCH])
                     for kb in range(32)]
            psy = ppy.nxt(em)
            cm = mm_accum(em, psy[:], parts, inc='te')
            if tb == NTB - 1:
                wgate[par] = cm
            em.wait('scalar', 'te', cm)
            ce = em.op('scalar', lambda e, psy=psy: e.activation(
                mixt[:], psy[:], AF.Copy), inc='sc')
            ppy.done('sc', ce)
            em.wait('gpsimd', 'sc', ce)
            cd = em.op('gpsimd', lambda e, tb=tb, nh=nh: e.dma_start(
                out=MIX[tb * 128:(tb + 1) * 128, nh * WCH:(nh + 1) * WCH],
                in_=mixt[:]), inc='outm', inc_by=16)
            em.wait('scalar', 'outm', cd)

    em.wait('scalar', 'te')
    em.wait('sync', 'te')
    for tb in range(NTB):
        cx = em.dma('sync', xtb[:], X2[tb * 128:(tb + 1) * 128, :],
                    inc='ldx')
        cmx = em.dma('sync', t1tb[:], MIX[tb * 128:(tb + 1) * 128, :],
                     inc='ldm')
        em.wait('scalar', 'ldx', cx)
        em.wait('vector', 'ldx', cx)
        emit_plain_ln(em, xtb[:], xtb[:], stt, sq2[:])
        em.wait('vector', 'sc')
        em.wait('vector', 'ldm', cmx)
        cv = em.op('vector', lambda e: e.tensor_tensor(
            xtb[:], xtb[:], t1tb[:], ALU.add), inc='ve')
        em.wait('gpsimd', 've', cv)
        cda = em.op('gpsimd', lambda e, tb=tb: e.dma_start(
            out=ARES[tb * 128:(tb + 1) * 128, :], in_=xtb[:]),
            inc='outa', inc_by=16)
        em.wait('scalar', 've', cv)
        c2 = emit_plain_ln(em, xtb[:], t1tb[:], stt, sq2[:])
        em.wait('tensor', 'sc', c2)
        for kb in range(NKB):
            pstk = ppt.nxt(em)
            ct = em.op('tensor', lambda e, kb=kb, pstk=pstk: e.transpose(
                pstk[:], t1tb[:, kb * 128:(kb + 1) * 128], ident_sb[:]),
                inc='te')
            em.wait('scalar', 'te', ct)
            ce2 = em.op('scalar', lambda e, kb=kb, tb=tb, pstk=pstk:
                        e.activation(
                big[:, kb * TPC + tb * 128:kb * TPC + (tb + 1) * 128],
                pstk[:], AF.Identity, scale=gb_sb[:, kb:kb + 1],
                bias=gb_sb[:, NKB + kb:NKB + kb + 1]), inc='sc')
            ppt.done('sc', ce2)
        em.wait('sync', 'sc')
        em.wait('sync', 'outa', cda)

    em.wait('sync', 'sc')
    for cb in range(32):
        jp = 0
        em.wait('scalar', 'te')
        src_a = FPT[:, cb * 128:(cb + 1) * 128].rearrange(
            '(kb p) c -> p kb c', p=128)
        src_g = FPT[:, 4096 + cb * 128:4096 + (cb + 1) * 128].rearrange(
            '(kb p) c -> p kb c', p=128)
        cwa = em.dma('sync',
                     wlda[:, 0:NKB * 128].rearrange('p (kb c) -> p kb c',
                                                    kb=NKB), src_a, inc='ldw')
        cwg = em.dma('sync',
                     wldg[:, 0:NKB * 128].rearrange('p (kb c) -> p kb c',
                                                    kb=NKB), src_g,
                     inc='ldwg')
        em.wait('scalar', 'ldw', cwa)
        em.op('scalar', lambda e, jp=jp: e.activation(
            fa_bf[jp][:], wlda[:, 0:NKB * 128], AF.Copy), inc='sc')
        em.wait('scalar', 'ldwg', cwg)
        c2 = em.op('scalar', lambda e, jp=jp: e.activation(
            fg_bf[jp][:], wldg[:, 0:NKB * 128], AF.Copy), inc='sc')
        em.wait('sync', 'sc', c2)
        em.wait('tensor', 'sc', c2)
        for tch in range(4):
            parts = [(fa_bf[jp][:, kb * 128:(kb + 1) * 128],
                      big[:, kb * TPC + tch * 512:kb * TPC + (tch + 1) * 512])
                     for kb in range(NKB)]
            psa = ppa.nxt(em)
            mm_accum(em, psa[:], parts, inc='te')
            parts = [(fg_bf[jp][:, kb * 128:(kb + 1) * 128],
                      big[:, kb * TPC + tch * 512:kb * TPC + (tch + 1) * 512])
                     for kb in range(NKB)]
            psg = ppg.nxt(em)
            cm = mm_accum(em, psg[:], parts, inc='te')
            if tch == 3:
                wgate[jp] = cm
            em.wait('scalar', 'te', cm)
            cs = em.op('scalar', lambda e, psg=psg: e.activation(
                sg_sb[:], psg[:], AF.Silu), inc='sc')
            ppg.done('sc', cs)
            em.wait('vector', 'sc', cs)
            cv = em.op('vector', lambda e, psa=psa: e.tensor_tensor(
                ffat[:], sg_sb[:], psa[:], ALU.mult), inc='ve')
            ppa.done('ve', cv)
            em.wait('gpsimd', 've', cv)
            cd = em.op('gpsimd', lambda e, cb=cb, tch=tch: e.dma_start(
                out=FFA[cb * 128:(cb + 1) * 128,
                        tch * 512:(tch + 1) * 512], in_=ffat[:]),
                inc='outf', inc_by=16)
            em.wait('vector', 'outf', cd)

    em.wait('sync', 'te')
    em.wait('sync', 'outf')
    for kb in range(32):
        em.dma('sync', big[:, kb * TPC:(kb + 1) * TPC],
               FFA[kb * 128:(kb + 1) * 128, :], inc='ldf')
    cf = em.cnt['ldf']
    em.wait('tensor', 'ldf', cf)
    for nh in range(NH):
        par = nh % 2
        cwl = load_chunk(FOT, nh, par)
        em.wait('tensor', 'sc', cwl)
        for tb in range(NTB):
            parts = [(big[:, kb * TPC + tb * 128:kb * TPC + (tb + 1) * 128],
                      warena[par][:, kb * WCH:(kb + 1) * WCH])
                     for kb in range(32)]
            psy = ppy.nxt(em)
            cm = mm_accum(em, psy[:], parts, inc='te')
            if tb == NTB - 1:
                wgate[par] = cm
            ca = em.dma('sync', arch[:],
                        ARES[tb * 128:(tb + 1) * 128,
                             nh * WCH:(nh + 1) * WCH], inc='lda')
            em.wait('scalar', 'te', cm)
            ce = em.op('scalar', lambda e, psy=psy: e.activation(
                mixt[:], psy[:], AF.Copy), inc='sc')
            ppy.done('sc', ce)
            em.wait('vector', 'sc', ce)
            em.wait('vector', 'lda', ca)
            cv = em.op('vector', lambda e: e.tensor_tensor(
                mixt[:], mixt[:], arch[:], ALU.add), inc='ve')
            em.wait('gpsimd', 've', cv)
            cd = em.op('gpsimd', lambda e, tb=tb, nh=nh: e.dma_start(
                out=T6[tb * 128:(tb + 1) * 128, nh * WCH:(nh + 1) * WCH],
                in_=mixt[:]), inc='outt', inc_by=16)
            em.wait('scalar', 'outt', cd)
            em.wait('vector', 'outt', cd)
            em.wait('sync', 've', cv)

    for tb in range(NTB):
        em.wait('sync', 'sc')
        em.wait('sync', 've')
        cx = em.dma('sync', xtb[:], X2[tb * 128:(tb + 1) * 128, :],
                    inc='ldx')
        ct6 = em.dma('sync', t1tb[:], T6[tb * 128:(tb + 1) * 128, :],
                     inc='ldm')
        em.wait('scalar', 'ldm', ct6)
        em.wait('vector', 'ldm', ct6)
        emit_plain_ln(em, t1tb[:], t1tb[:], stt, sq2[:])
        em.wait('vector', 'sc')
        em.wait('vector', 'ldx', cx)
        cv = em.op('vector', lambda e: e.tensor_tensor(
            xtb[:], xtb[:], t1tb[:], ALU.add), inc='ve')
        em.wait('gpsimd', 've', cv)
        cd = em.op('gpsimd', lambda e, tb=tb: e.dma_start(
            out=X3[tb * 128:(tb + 1) * 128, :], in_=xtb[:]),
            inc='outx', inc_by=16)
        em.wait('sync', 'outx', cd)
    em.wait('gpsimd', 'outx')



def build_launchE():
    nc = bass.Bass('TRN2', target_bir_lowering=False)
    io = declare_io_E(nc)
    stack = contextlib.ExitStack()
    with stack:
        em = Em(nc, stack)
        emit_E(nc, em, stack, io)
        em.finalize()
    return nc


def host_prep_launchE(inputs, gs, x2s):
    gfulls = []
    for j in range(NCORES):
        gfulls.append(np.ascontiguousarray(
            np.concatenate([gs[c][j] for c in range(NCORES)], axis=0)))
    opt = np.ascontiguousarray(np.asarray(inputs['L1_attn_out_proj_w']).T)
    gb2l = np.stack([np.asarray(inputs['L1_attn_ln2_w']),
                     np.asarray(inputs['L1_attn_ln2_b'])], axis=0)
    fpt = np.ascontiguousarray(np.asarray(inputs['L1_attn_ff_proj_w']).T)
    fot = np.ascontiguousarray(np.asarray(inputs['L1_attn_ff_out_w']).T)
    ident = np.eye(128, dtype=np.float32)
    cvals = np.array([[EPS, -0.5, 0, 0, 0, 0, 0, 0]], np.float32)
    in_maps = []
    for c in range(NCORES):
        in_maps.append(dict(gfull=gfulls[c], x2=x2s[c], opt=opt, gb2l=gb2l,
                            fpt=fpt, fot=fot, ident=ident, cvals=cvals))
    return in_maps


def declare_io_F(nc, internal=()):
    io = {}
    io['x3'] = nc.dram_tensor('x3', [TPC, D], F32, kind=None if 'x3' in internal else 'ExternalInput').ap()
    io['rw9'] = nc.dram_tensor('rw9', [D, 9], F32, kind=None if 'rw9' in internal else 'ExternalInput').ap()
    io['gupt'] = nc.dram_tensor('gupt', [E_MOE, D, 2 * I_MOE], F32, kind=None if 'gupt' in internal else 'ExternalInput').ap()
    io['dpt'] = nc.dram_tensor('dpt', [E_MOE, I_MOE, D], F32, kind=None if 'dpt' in internal else 'ExternalInput').ap()
    io['sgt'] = nc.dram_tensor('sgt', [D, I_SH], F32, kind=None if 'sgt' in internal else 'ExternalInput').ap()
    io['sut'] = nc.dram_tensor('sut', [D, I_SH], F32, kind=None if 'sut' in internal else 'ExternalInput').ap()
    io['sdt'] = nc.dram_tensor('sdt', [I_SH, D], F32, kind=None if 'sdt' in internal else 'ExternalInput').ap()
    io['ident'] = nc.dram_tensor('ident', [128, 128], F32, kind=None if 'ident' in internal else 'ExternalInput').ap()
    io['cvals'] = nc.dram_tensor('cvals', [1, 8], F32, kind=None if 'cvals' in internal else 'ExternalInput').ap()
    io['tab'] = nc.dram_tensor('tab', [2, 8], F32, kind=None if 'tab' in internal else 'ExternalInput').ap()
    io['x4'] = nc.dram_tensor('x4', [TPC, D], F32, kind=None if 'x4' in internal else 'ExternalOutput').ap()
    io['oacc'] = nc.dram_tensor('oacc', [TPC, D], F32).ap()
    io['shact'] = nc.dram_tensor('shact', [I_SH, TPC], BF16).ap()
    io['combt'] = nc.dram_tensor('combt', [E_MOE, TPC], F32).ap()

    return io


def emit_F(nc, em, stack, io):
    NJ = I_MOE // 128
    NJS = I_SH // 128
    X3 = io['x3']
    RW9 = io['rw9']
    GUPT = io['gupt']
    DPT = io['dpt']
    SGT = io['sgt']
    SUT = io['sut']
    SDT = io['sdt']
    IDENT = io['ident']
    CVALS = io['cvals']
    TAB = io['tab']
    X4 = io['x4']
    OACC = io['oacc']
    SHACT = io['shact']
    COMBT = io['combt']
    sb = lambda name, shp, dt: stack.enter_context(
        nc.sbuf_tensor(name, shp, dt, allow_name_mangling=True))
    big = sb('big', [128, NKB * TPC], BF16)
    act = sb('act', [128, NJ * TPC], BF16)
    fa_bf = [sb('fa_bf0', [128, NKB * 128], BF16),
             sb('fa_bf1', [128, NKB * 128], BF16)]
    fg_bf = [sb('fg_bf0', [128, NKB * 128], BF16),
             sb('fg_bf1', [128, NKB * 128], BF16)]
    shtb = sb('shtb', [128, NJS * 128], BF16)
    wld = sb('wld', [128, 512], F32)
    wldd = sb('wldd', [128, 6 * 512], F32)
    xtb = sb('xtb', [128, D], F32)
    t1tb = sb('t1tb', [128, D], F32)
    sq2 = sb('sq2', [128, D], F32)
    wlda = sq2
    wldg = t1tb
    stt = sb('stt', [128, 16], F32)
    ident_sb = sb('ident_sb', [128, 128], F32)
    cv_sb = sb('cv_sb', [128, 8], F32)
    rw_bf = sb('rw_bf', [128, NKB * 9], BF16)
    lg = sb('lg', [128, 16], F32)
    iot8 = sb('iot8', [128, 8], F32)
    iot8m = sb('iot8m', [128, 8], F32)
    comb = sb('comb', [128, 8], F32)
    combT = sb('combT', [8, TPC], F32)
    sg_all = sb('sg_all', [128, NTB], F32)
    cre = sb('cre', [128, 4 * 512], F32)
    sg_sb = sb('sg_sb', [128, 512], F32)
    au_sb = sb('au_sb', [128, 512], F32)
    act16 = sb('act16', [128, 512], BF16)
    otile = [sb('otile0', [128, 512], F32), sb('otile1', [128, 512], F32)]
    psr = stack.enter_context(nc.psum_tensor('psr', [128, 9], F32, allow_name_mangling=True))
    ppa = PP(em, stack, nc, 'psa', [128, 512])
    ppg = PP(em, stack, nc, 'psg', [128, 512])
    ppd = PP(em, stack, nc, 'psd', [128, 512])
    ppt = PP(em, stack, nc, 'pst', None,
             banks=[ppg.ps[0][:, 0:128], ppa.ps[0][:, 0:128]])
    pst = ppg.ps[1]

    for s in ['ldc', 'ldx', 'ldw', 'ldwg', 'ldacc', 'ldsh', 'ldcre',
              'outacc', 'outsh', 'outx', 'outct', 'sc', 've', 'te']:
        em.sem(s)

    em.setup_consts([EPS, -0.5], cv_sb, CVALS)
    em.dma('sync', ident_sb[:], IDENT, inc='ldc')
    em.dma('sync', iot8[:], TAB[0:1, :].partition_broadcast(128).squeeze(1),
           inc='ldc')
    em.dma('sync', iot8m[:], TAB[1:2, :].partition_broadcast(128).squeeze(1),
           inc='ldc')
    for kb in range(NKB):
        cw = em.dma('sync', wld[:, 0:9], RW9[kb * 128:(kb + 1) * 128, :],
                    inc='ldw')
        em.wait('scalar', 'ldw', cw)
        c2 = em.op('scalar', lambda e, kb=kb: e.activation(
            rw_bf[:, kb * 9:(kb + 1) * 9], wld[:, 0:9], AF.Copy),
            inc='sc')
        em.wait('sync', 'sc', c2)
    em.wait('scalar', 'ldc')
    em.wait('vector', 'ldc')
    em.wait('tensor', 'ldc')
    em.wait('tensor', 'sc')

    for tb in range(NTB):
        cx = em.dma('sync', xtb[:], X3[tb * 128:(tb + 1) * 128, :],
                    inc='ldx')
        em.wait('scalar', 'ldx', cx)
        em.wait('vector', 'ldx', cx)
        c7 = emit_plain_ln(em, xtb[:], t1tb[:], stt, sq2[:])
        em.wait('tensor', 'sc', c7)
        for kb in range(NKB):
            pstk = ppt.nxt(em)
            ct = em.op('tensor', lambda e, kb=kb, pstk=pstk: e.transpose(
                pstk[:], t1tb[:, kb * 128:(kb + 1) * 128], ident_sb[:]),
                inc='te')
            em.wait('scalar', 'te', ct)
            ce2 = em.op('scalar', lambda e, kb=kb, tb=tb, pstk=pstk:
                        e.activation(
                big[:, kb * TPC + tb * 128:kb * TPC + (tb + 1) * 128],
                pstk[:], AF.Copy), inc='sc')
            ppt.done('sc', ce2)
        em.wait('tensor', 'sc')
        parts = [(big[:, kb * TPC + tb * 128:kb * TPC + (tb + 1) * 128],
                  rw_bf[:, kb * 9:(kb + 1) * 9]) for kb in range(NKB)]
        cm = mm_accum(em, psr[:], parts, inc='te')
        em.wait('scalar', 'te', cm)
        ce = em.op('scalar', lambda e: e.activation(
            lg[:, 0:9], psr[:], AF.Copy), inc='sc')
        em.wait('tensor', 'sc', ce)
        em.wait('vector', 'sc', ce)
        em.op('vector', lambda e: e.tensor_reduce(
            stt[:, 8:9], lg[:, 0:8], AX.X, ALU.max, negate=True))
        cs = em.op('vector', lambda e: e.tensor_copy(
            stt[:, 8:9], stt[:, 8:9]), inc='ve')
        em.wait('scalar', 've', cs)
        cex = em.op('scalar', lambda e: e.activation(
            lg[:, 0:8], lg[:, 0:8], AF.Exp, bias=stt[:, 8:9],
            accum_out=stt[:, 9:10]), inc='sc')
        em.wait('vector', 'sc', cex)
        em.op('vector', lambda e: e.reciprocal(stt[:, 10:11],
                                               stt[:, 9:10]))
        em.op('vector', lambda e: e.tensor_scalar(
            lg[:, 0:8], lg[:, 0:8], stt[:, 10:11], None, ALU.mult))
        em.op('vector', lambda e: e.tensor_reduce(
            stt[:, 11:12], lg[:, 0:8], AX.X, ALU.max))
        em.op('vector', lambda e: e.tensor_scalar(
            comb[:], lg[:, 0:8], stt[:, 11:12], None, ALU.is_ge))
        em.op('vector', lambda e: e.tensor_tensor(
            comb[:], comb[:], iot8m[:], ALU.mult))
        em.op('vector', lambda e: e.tensor_scalar(
            comb[:], comb[:], BIGC, None, ALU.add))
        em.op('vector', lambda e: e.tensor_reduce(
            stt[:, 12:13], comb[:], AX.X, ALU.min))
        em.op('vector', lambda e: e.tensor_scalar(
            comb[:], iot8[:], stt[:, 12:13], None, ALU.is_equal))
        em.op('vector', lambda e: e.tensor_tensor(
            sq2[:, 16:24], lg[:, 0:8], comb[:], ALU.mult))
        em.op('vector', lambda e: e.tensor_tensor(
            sq2[:, 16:24], lg[:, 0:8], sq2[:, 16:24], ALU.subtract))
        em.op('vector', lambda e: e.tensor_reduce(
            stt[:, 13:14], sq2[:, 16:24], AX.X, ALU.max))
        em.op('vector', lambda e: e.tensor_scalar(
            sq2[:, 0:8], sq2[:, 16:24], stt[:, 13:14], None, ALU.is_ge))
        em.op('vector', lambda e: e.tensor_tensor(
            sq2[:, 0:8], sq2[:, 0:8], iot8m[:], ALU.mult))
        em.op('vector', lambda e: e.tensor_scalar(
            sq2[:, 0:8], sq2[:, 0:8], BIGC, None, ALU.add))
        em.op('vector', lambda e: e.tensor_reduce(
            stt[:, 14:15], sq2[:, 0:8], AX.X, ALU.min))
        em.op('vector', lambda e: e.tensor_tensor(
            stt[:, 15:16], stt[:, 11:12], stt[:, 13:14], ALU.add))
        em.op('vector', lambda e: e.reciprocal(stt[:, 15:16],
                                               stt[:, 15:16]))
        em.op('vector', lambda e: e.tensor_tensor(
            stt[:, 11:12], stt[:, 11:12], stt[:, 15:16], ALU.mult))
        em.op('vector', lambda e: e.tensor_tensor(
            stt[:, 13:14], stt[:, 13:14], stt[:, 15:16], ALU.mult))
        em.op('vector', lambda e: e.tensor_scalar(
            comb[:], comb[:], stt[:, 11:12], None, ALU.mult))
        em.op('vector', lambda e: e.tensor_scalar(
            sq2[:, 8:16], iot8[:], stt[:, 14:15], None, ALU.is_equal))
        cv = em.op('vector', lambda e: e.scalar_tensor_tensor(
            comb[:], sq2[:, 8:16], stt[:, 13:14], comb[:], ALU.mult,
            ALU.add), inc='ve')
        em.wait('tensor', 've', cv)
        em.wait('tensor', 'sc')
        ct = em.op('tensor', lambda e: e.transpose(
            pst[0:8, 0:128], comb[:], ident_sb[:]), inc='te')
        em.wait('scalar', 'te', ct)
        em.op('scalar', lambda e, tb=tb: e.activation(
            combT[:, tb * 128:(tb + 1) * 128], pst[0:8, 0:128], AF.Copy),
            inc='sc')
        csg = em.op('scalar', lambda e, tb=tb: e.activation(
            sg_all[:, tb:tb + 1], lg[:, 8:9], AF.Sigmoid), inc='sc')
        em.wait('sync', 'sc', csg)
        em.wait('sync', 've')
    em.wait('gpsimd', 'sc')
    cct = em.dma('gpsimd', COMBT, combT[:], inc='outct')
    em.wait('sync', 'outct', cct)

    fggate = [0, 0]
    ogate = [0] * (4 * NTB)
    pgate = [0, 0]
    def load_pair(Wsrc, coloff, cb):
        pass
    for e in range(E_MOE):
        em.wait('sync', 've')
        ccre = em.dma('sync', cre[:],
                      COMBT[e:e + 1, :].partition_broadcast(128)
                      .squeeze(1), inc='ldcre')
        em.wait('vector', 'ldcre', ccre)
        for j in range(NJ):
            jp = j % 2
            em.wait('scalar', 'te', fggate[jp])
            src_a = GUPT[e, :, j * 128:(j + 1) * 128].rearrange(
                '(kb p) c -> p kb c', p=128)
            src_g = GUPT[e, :, I_MOE + j * 128:I_MOE + (j + 1) * 128].rearrange('(kb p) c -> p kb c', p=128)
            cwa = em.dma('sync', wlda[:].rearrange('p (kb c) -> p kb c',
                                                   kb=NKB), src_a,
                         inc='ldw')
            cwg = em.dma('sync', wldg[:].rearrange('p (kb c) -> p kb c',
                                                   kb=NKB), src_g,
                         inc='ldwg')
            em.wait('scalar', 'ldw', cwa)
            em.op('scalar', lambda e_, jp=jp: e_.activation(
                fa_bf[jp][:], wlda[:], AF.Copy), inc='sc')
            em.wait('scalar', 'ldwg', cwg)
            c2 = em.op('scalar', lambda e_, jp=jp: e_.activation(
                fg_bf[jp][:], wldg[:], AF.Copy), inc='sc')
            em.wait('sync', 'sc', c2)
            em.wait('tensor', 'sc', c2)
            for tch in range(4):
                parts = [(fa_bf[jp][:, kb * 128:(kb + 1) * 128],
                          big[:, kb * TPC + tch * 512:
                              kb * TPC + (tch + 1) * 512])
                         for kb in range(NKB)]
                psg = ppg.nxt(em)
                mm_accum(em, psg[:], parts, inc='te')
                parts = [(fg_bf[jp][:, kb * 128:(kb + 1) * 128],
                          big[:, kb * TPC + tch * 512:
                              kb * TPC + (tch + 1) * 512])
                         for kb in range(NKB)]
                psa = ppa.nxt(em)
                cm = mm_accum(em, psa[:], parts, inc='te')
                if tch == 3:
                    fggate[jp] = cm
                em.wait('scalar', 'te', cm)
                cs = em.op('scalar', lambda e_, psg=psg: e_.activation(
                    sg_sb[:], psg[:], AF.Silu), inc='sc')
                ppg.done('sc', cs)
                em.wait('vector', 'sc', cs)
                cv0 = em.op('vector', lambda e_, psa=psa: e_.tensor_tensor(
                    au_sb[:], sg_sb[:], psa[:], ALU.mult), inc='ve')
                ppa.done('ve', cv0)
                cv = em.op('vector', lambda e_, tch=tch: e_.tensor_tensor(
                    au_sb[:], au_sb[:], cre[:, tch * 512:(tch + 1) * 512],
                    ALU.mult), inc='ve')
                em.wait('scalar', 've', cv)
                ca = em.op('scalar', lambda e_, j=j, tch=tch: e_.activation(
                    act[:, j * TPC + tch * 512:j * TPC + (tch + 1) * 512],
                    au_sb[:], AF.Copy), inc='sc')
                em.wait('vector', 'sc', ca)
        em.wait('tensor', 'sc')
        for nh in range(4):
            em.wait('scalar', 'te')
            src_d = DPT[e, :, nh * 512:(nh + 1) * 512].rearrange(
                '(j p) c -> p j c', p=128)
            for j0 in range(0, NJ, 6):
                nj = min(6, NJ - j0)
                cw = em.dma('sync',
                            wldd[:, 0:nj * 512].rearrange('p (j c) -> p j c',
                                                          j=nj),
                            src_d[:, j0:j0 + nj, :], inc='ldw')
                em.wait('scalar', 'ldw', cw)
                c2 = em.op('scalar', lambda e_, j0=j0, nj=nj: e_.activation(
                    shtb[:, j0 * 512:(j0 + nj) * 512], wldd[:, 0:nj * 512],
                    AF.Copy), inc='sc')
                em.wait('sync', 'sc', c2)
            em.wait('tensor', 'sc')
            for tb in range(NTB):
                parts = [(act[:, j * TPC + tb * 128:j * TPC + (tb + 1) * 128],
                          shtb[:, j * 512:(j + 1) * 512])
                         for j in range(NJ)]
                psd = ppd.nxt(em)
                cm = mm_accum(em, psd[:], parts, inc='te')
                em.wait('scalar', 'te', cm)
                op = otile[tb % 2]
                em.wait('scalar', 'outacc', max(ogate[nh * NTB + tb],
                                                pgate[tb % 2]))
                ce = em.op('scalar', lambda e_, psd=psd, op=op: e_.activation(
                    op[:], psd[:], AF.Copy), inc='sc')
                ppd.done('sc', ce)
                em.wait('gpsimd', 'sc', ce)
                if e == 0:
                    cd = em.op('gpsimd', lambda e_, tb=tb, nh=nh, op=op:
                               e_.dma_start(
                        out=OACC[tb * 128:(tb + 1) * 128,
                                 nh * 512:(nh + 1) * 512], in_=op[:]),
                        inc='outacc', inc_by=16)
                else:
                    cd = em.op('gpsimd', lambda e_, tb=tb, nh=nh, op=op:
                               e_.dma_start(
                        out=OACC[tb * 128:(tb + 1) * 128,
                                 nh * 512:(nh + 1) * 512], in_=op[:],
                        accum_op=ALU.add), inc='outacc', inc_by=16)
                ogate[nh * NTB + tb] = cd
                pgate[tb % 2] = cd

    for cb in range(NJS):
        jp = cb % 2
        em.wait('scalar', 'te', fggate[jp])
        src_a = SGT[:, cb * 128:(cb + 1) * 128].rearrange(
            '(kb p) c -> p kb c', p=128)
        src_g = SUT[:, cb * 128:(cb + 1) * 128].rearrange(
            '(kb p) c -> p kb c', p=128)
        cwa = em.dma('sync', wlda[:].rearrange('p (kb c) -> p kb c',
                                               kb=NKB), src_a, inc='ldw')
        cwg = em.dma('sync', wldg[:].rearrange('p (kb c) -> p kb c',
                                               kb=NKB), src_g, inc='ldwg')
        em.wait('scalar', 'ldw', cwa)
        em.op('scalar', lambda e_, jp=jp: e_.activation(
            fa_bf[jp][:], wlda[:], AF.Copy), inc='sc')
        em.wait('scalar', 'ldwg', cwg)
        c2 = em.op('scalar', lambda e_, jp=jp: e_.activation(
            fg_bf[jp][:], wldg[:], AF.Copy), inc='sc')
        em.wait('sync', 'sc', c2)
        em.wait('tensor', 'sc', c2)
        for tch in range(4):
            parts = [(fa_bf[jp][:, kb * 128:(kb + 1) * 128],
                      big[:, kb * TPC + tch * 512:kb * TPC + (tch + 1) * 512])
                     for kb in range(NKB)]
            psg = ppg.nxt(em)
            mm_accum(em, psg[:], parts, inc='te')
            parts = [(fg_bf[jp][:, kb * 128:(kb + 1) * 128],
                      big[:, kb * TPC + tch * 512:kb * TPC + (tch + 1) * 512])
                     for kb in range(NKB)]
            psa = ppa.nxt(em)
            cm = mm_accum(em, psa[:], parts, inc='te')
            if tch == 3:
                fggate[jp] = cm
            em.wait('scalar', 'te', cm)
            cs = em.op('scalar', lambda e_, psg=psg: e_.activation(
                sg_sb[:], psg[:], AF.Silu), inc='sc')
            ppg.done('sc', cs)
            em.wait('vector', 'sc', cs)
            cv = em.op('vector', lambda e_, psa=psa: e_.tensor_tensor(
                act16[:], sg_sb[:], psa[:], ALU.mult), inc='ve')
            ppa.done('ve', cv)
            em.wait('gpsimd', 've', cv)
            cd = em.op('gpsimd', lambda e_, cb=cb, tch=tch: e_.dma_start(
                out=SHACT[cb * 128:(cb + 1) * 128,
                          tch * 512:(tch + 1) * 512], in_=act16[:]),
                inc='outsh', inc_by=16)
            em.wait('vector', 'outsh', cd)

    em.wait('sync', 'te')
    em.wait('sync', 'outsh')
    for nh in range(4):
        em.wait('scalar', 'te')
        src_d = SDT[:, nh * 512:(nh + 1) * 512].rearrange(
            '(j p) c -> p j c', p=128)
        for j0 in range(0, NJS, 6):
            nj = min(6, NJS - j0)
            cw = em.dma('sync',
                        wldd[:, 0:nj * 512].rearrange('p (j c) -> p j c',
                                                      j=nj),
                        src_d[:, j0:j0 + nj, :], inc='ldw')
            em.wait('scalar', 'ldw', cw)
            c2 = em.op('scalar', lambda e_, j0=j0, nj=nj: e_.activation(
                act[:, j0 * 512:(j0 + nj) * 512], wldd[:, 0:nj * 512],
                AF.Copy), inc='sc')
            em.wait('sync', 'sc', c2)
        em.wait('tensor', 'sc')
        for tb in range(NTB):
            em.wait('sync', 'te')
            csh = em.dma('sync',
                         shtb[:].rearrange('p (j c) -> p j c', j=NJS),
                         SHACT[:, tb * 128:(tb + 1) * 128].rearrange(
                             '(j p) c -> p j c', p=128), inc='ldsh')
            em.wait('tensor', 'ldsh', csh)
            parts = [(shtb[:, j * 128:(j + 1) * 128],
                      act[:, j * 512:(j + 1) * 512])
                     for j in range(NJS)]
            psd = ppd.nxt(em)
            cm = mm_accum(em, psd[:], parts, inc='te')
            em.wait('scalar', 'te', cm)
            op = otile[tb % 2]
            em.wait('scalar', 'outacc', max(ogate[nh * NTB + tb],
                                            pgate[tb % 2]))
            ce = em.op('scalar', lambda e_, tb=tb, psd=psd, op=op:
                       e_.activation(
                op[:], psd[:], AF.Identity,
                scale=sg_all[:, tb:tb + 1]), inc='sc')
            ppd.done('sc', ce)
            em.wait('gpsimd', 'sc', ce)
            cd = em.op('gpsimd', lambda e_, tb=tb, nh=nh, op=op: e_.dma_start(
                out=OACC[tb * 128:(tb + 1) * 128,
                         nh * 512:(nh + 1) * 512], in_=op[:],
                accum_op=ALU.add), inc='outacc', inc_by=16)
            ogate[nh * NTB + tb] = cd
            pgate[tb % 2] = cd

    em.wait('sync', 'outacc')
    for tb in range(NTB):
        em.wait('sync', 'sc')
        em.wait('sync', 've')
        cx = em.dma('sync', xtb[:], X3[tb * 128:(tb + 1) * 128, :],
                    inc='ldx')
        ct8 = em.dma('sync', t1tb[:], OACC[tb * 128:(tb + 1) * 128, :],
                     inc='ldacc')
        em.wait('scalar', 'ldacc', ct8)
        em.wait('vector', 'ldacc', ct8)
        emit_plain_ln(em, t1tb[:], t1tb[:], stt, sq2[:])
        em.wait('vector', 'sc')
        em.wait('vector', 'ldx', cx)
        cv = em.op('vector', lambda e_: e_.tensor_tensor(
            xtb[:], xtb[:], t1tb[:], ALU.add), inc='ve')
        em.wait('gpsimd', 've', cv)
        cd = em.op('gpsimd', lambda e_, tb=tb: e_.dma_start(
            out=X4[tb * 128:(tb + 1) * 128, :], in_=xtb[:]),
            inc='outx', inc_by=16)
        em.wait('sync', 'outx', cd)
    em.wait('gpsimd', 'outx')



def build_launchF():
    nc = bass.Bass('TRN2', target_bir_lowering=False)
    io = declare_io_F(nc)
    stack = contextlib.ExitStack()
    with stack:
        em = Em(nc, stack)
        emit_F(nc, em, stack, io)
        em.finalize()
    return nc


def host_prep_launchF(inputs, x3s):
    rw9 = np.ascontiguousarray(np.concatenate(
        [np.asarray(inputs['L1_ffn_router_weight']).T,
         np.asarray(inputs['L1_ffn_shared_gate_w']).T], axis=1))
    gupt = np.ascontiguousarray(
        np.asarray(inputs['L1_ffn_gate_up_proj']).transpose(0, 2, 1))
    dpt = np.ascontiguousarray(
        np.asarray(inputs['L1_ffn_down_proj']).transpose(0, 2, 1))
    sgt = np.ascontiguousarray(np.asarray(inputs['L1_ffn_sh_gate_w']).T)
    sut = np.ascontiguousarray(np.asarray(inputs['L1_ffn_sh_up_w']).T)
    sdt = np.ascontiguousarray(np.asarray(inputs['L1_ffn_sh_down_w']).T)
    ident = np.eye(128, dtype=np.float32)
    cvals = np.array([[EPS, -0.5, 0, 0, 0, 0, 0, 0]], np.float32)
    tab = np.zeros((2, 8), np.float32)
    tab[0] = np.arange(8)
    tab[1] = np.arange(8) - BIGC
    in_maps = []
    for c in range(NCORES):
        in_maps.append(dict(x3=x3s[c], rw9=rw9, gupt=gupt, dpt=dpt, sgt=sgt,
                            sut=sut, sdt=sdt, ident=ident, cvals=cvals,
                            tab=tab))
    return in_maps


def build_fused():
    nc = bass.Bass('TRN2', target_bir_lowering=False)
    IO = {}
    def ext(nm, shp, dt):
        IO[nm] = nc.dram_tensor(nm, shp, dt, kind='ExternalInput').ap()
    def outt(nm, shp, dt):
        IO[nm] = nc.dram_tensor(nm, shp, dt, kind='ExternalOutput').ap()
    def itn(nm, shp, dt):
        IO[nm] = nc.dram_tensor(nm, shp, dt).ap()

    ext('x0', [TPC, D], F32)
    ext('wq', [D, D], F32); ext('wk', [D, D], F32); ext('wv', [D, D], F32)
    ext('bqkv', [3, D], F32); ext('gb', [2, D], F32)
    ext('cos', [TPC, ROT // 2], F32); ext('sin', [TPC, ROT // 2], F32)
    ext('ones', [128, 128], F32); ext('ident', [128, 128], F32)
    ext('cvals', [1, 8], F32)
    ext('wout', [D, D], F32); ext('bout', [1, D], F32); ext('gb2', [2, D], F32)
    ext('slopes', [NUM_TILES, D_SLICE * GRID], F32); ext('tab', [3, 64], F32)
    ext('bases', [64, 512], F32); ext('osc', [64, 1], F32); ext('gb1', [2, D], F32)
    ext('wxmt', [D, DPC], F32); ext('wzt', [D, DPC], F32)
    ext('cw', [4, 128, D_CONV], F32); ext('cb', [4, 128], F32)
    ext('xpt', [DPC, 160], F32); ext('dtpt', [DT_RANK, DPC], F32)
    ext('dtb', [4, 128], F32); ext('alog', [4, 128, D_STATE], F32)
    ext('dsk', [4, 128], F32)
    ext('opt', [D_INNER, D], F32); ext('gb2l', [2, D], F32)
    ext('fpt', [D, HIDDEN], F32); ext('fot', [D_INNER, D], F32)
    ext('rw9', [D, 9], F32)
    ext('gupt', [E_MOE, D, 2 * I_MOE], F32); ext('dpt', [E_MOE, I_MOE, D], F32)
    ext('sgt', [D, I_SH], F32); ext('sut', [D, I_SH], F32)
    ext('sdt', [I_SH, D], F32); ext('tab8', [2, 8], F32)
    outt('x4', [TPC, D], F32)
    itn('qt', [D, TPC], F32); itn('apart', [1, D], F32); itn('asum', [1, D], F32)
    itn('kst', [TPC, D], F32)
    itn('x1', [TPC, D], F32); itn('oh', [TPC, 64], F32); itn('xsl', [TPC, 32], F32)
    itn('minp', [64, 1], F32); itn('maxp', [64, 1], F32)
    itn('minr', [64, 1], F32); itn('maxr', [64, 1], F32)
    itn('ss', [NUM_TILES, D_SLICE], F32)
    itn('x2', [TPC, D], F32); itn('ut', [D, TPC], BF16)
    itn('utallf', [NCORES * D, TPC], BF16)
    itn('xcst', [DPC, T], F32); itn('zst', [DPC, T], F32)
    itn('xdblp', [160, T], F32); itn('xdbl', [160, T], F32)
    itn('gf2', [NCORES * DPC, TPC], BF16); itn('gout', [NCORES * DPC, TPC], BF16)
    itn('x3', [TPC, D], F32)
    itn('mix', [TPC, D], F32); itn('ares', [TPC, D], F32)
    itn('ffa', [D_INNER, TPC], BF16); itn('t6', [TPC, D], F32)
    itn('oacc', [TPC, D], F32); itn('shact', [I_SH, TPC], BF16)
    itn('combt', [E_MOE, TPC], F32)

    gstack = contextlib.ExitStack()
    with gstack:
        em = Em(nc, gstack)
        em.setup_consts([EPS, -0.5, 1.0, -1.0], None, IO['cvals'])

        def coll(name, kind, op, in_ap, out_ap):
            h = em.sem(name)
            def fn(e, h=h, in_ap=in_ap, out_ap=out_ap):
                with allow_ici_all_to_all():
                    e.collective_compute(
                        kind, op, replica_groups=[list(range(NCORES))],
                        ins=[in_ap.opt()], outs=[out_ap.opt()]).then_inc(h)
            em.q['gpsimd'].append(fn)
            for eng in Em.ENGINES:
                em.wait(eng, name, 1)

        with contextlib.ExitStack() as st:
            emit_A(nc, em, st, IO)
            em.barrier()
        coll('ccA', 'AllReduce', ALU.add, IO['apart'], IO['asum'])
        ioB = dict(IO)
        with contextlib.ExitStack() as st:
            emit_B(nc, em, st, ioB)
            em.barrier()
        coll('ccMn', 'AllReduce', ALU.min, IO['minp'], IO['minr'])
        coll('ccMx', 'AllReduce', ALU.max, IO['maxp'], IO['maxr'])
        with contextlib.ExitStack() as st:
            emit_C(nc, em, st, IO)
            em.barrier()
        coll('ccU', 'AllGather', ALU.bypass, IO['ut'], IO['utallf'])
        ioD = dict(IO)
        ioD['utall'] = IO['utallf'].rearrange('(j d) t -> j d t', d=D)
        ioD['g'] = IO['gf2'].rearrange('(j d) t -> j d t', d=DPC)
        with contextlib.ExitStack() as st:
            emit_D(nc, em, st, ioD)
            em.barrier()
        coll('ccG', 'AllToAll', ALU.bypass, IO['gf2'], IO['gout'])
        ioE = dict(IO)
        ioE['gfull'] = IO['gout']
        with contextlib.ExitStack() as st:
            emit_E(nc, em, st, ioE)
            em.barrier()
        ioF = dict(IO)
        ioF['tab'] = IO['tab8']
        with contextlib.ExitStack() as st:
            emit_F(nc, em, st, ioF)
            em.barrier()
        em.finalize()
    return nc


def host_prep_fused(inputs):
    import ml_dtypes
    dummy = np.zeros((1, 1), np.float32)
    d8 = [dummy] * NCORES
    hA = host_prep_launchA(inputs)
    hB = host_prep_launchB(inputs, d8, [np.zeros((1, D), np.float32)] * NCORES)
    hC = host_prep_launchC(inputs, d8, d8, d8,
                           [np.zeros((64, 1), np.float32)] * NCORES,
                           [np.zeros((64, 1), np.float32)] * NCORES)
    hD = host_prep_launchD(inputs, [np.zeros((1, 1), ml_dtypes.bfloat16)] * NCORES)
    hE = host_prep_launchE(inputs, [[np.zeros((1, 1), ml_dtypes.bfloat16)] * NCORES] * NCORES, d8)
    hF = host_prep_launchF(inputs, d8)
    in_maps = []
    for c in range(NCORES):
        m = {}
        m.update(hA[c])
        for k in ['wout', 'bout', 'gb2', 'slopes', 'tab']:
            m[k] = hB[c][k]
        for k in ['bases', 'osc', 'gb1']:
            m[k] = hC[c][k]
        for k in ['wxmt', 'wzt', 'cw', 'cb', 'xpt', 'dtpt', 'dtb', 'alog',
                  'dsk']:
            m[k] = hD[c][k]
        for k in ['opt', 'gb2l', 'fpt', 'fot']:
            m[k] = hE[c][k]
        for k in ['rw9', 'gupt', 'dpt', 'sgt', 'sut', 'sdt']:
            m[k] = hF[c][k]
        m['tab8'] = hF[c]['tab']
        m['cvals'] = np.array([[EPS, -0.5, 1.0, -1.0, 0, 0, 0, 0]], np.float32)
        in_maps.append(m)
    return in_maps


def _rotary_tables(t0, t1):
    j = np.arange(ROT // 2, dtype=np.float64)
    inv = 10000.0 ** (-2.0 * j / ROT)
    pos = np.arange(t0, t1, dtype=np.float64)[:, None]
    fr = pos * inv[None, :]
    return (np.cos(fr).astype(np.float32), np.sin(fr).astype(np.float32))


def host_prep_launchA(inputs):
    X = np.asarray(inputs['L0_attn_X'])[0]
    W_in = np.asarray(inputs['L0_attn_W_in'])
    b_in = np.asarray(inputs['L0_attn_b_in'])
    gamma = np.asarray(inputs['L0_attn_gamma'])
    beta = np.asarray(inputs['L0_attn_beta'])
    wq = np.ascontiguousarray(W_in[:, 0:D])
    wk = np.ascontiguousarray(W_in[:, D:2 * D])
    wv = np.ascontiguousarray(W_in[:, 2 * D:3 * D])
    bqkv = np.ascontiguousarray(b_in.reshape(3, D))
    gb = np.stack([gamma, beta], axis=0)
    ones = np.ones((128, 128), np.float32)
    ident = np.eye(128, dtype=np.float32)
    in_maps = []
    for c in range(NCORES):
        cos, sin = _rotary_tables(c * TPC, (c + 1) * TPC)
        in_maps.append(dict(
            x0=np.ascontiguousarray(X[c * TPC:(c + 1) * TPC]),
            wq=wq, wk=wk, wv=wv, bqkv=bqkv, gb=gb,
            cos=cos, sin=sin, ones=ones, ident=ident,
            cvals=np.array([[EPS, -0.5, 0, 0, 0, 0, 0, 0]], np.float32)))
    return in_maps


def kernel(**inputs):
    cores = list(range(NCORES))
    nc = build_fused()
    in_maps = host_prep_fused(inputs)
    with allow_ici_all_to_all():
        res = run_bass_kernel_spmd(nc, in_maps, core_ids=cores)
    x4 = np.concatenate([res.results[c]['x4'] for c in range(NCORES)], axis=0)
    return x4.reshape(1, T, D).astype(np.float32)


def kernel_multilaunch(**inputs):

    import ml_dtypes

    def as_bf16(a):
        a = np.asarray(a)
        if a.itemsize == 2 and a.dtype != ml_dtypes.bfloat16:
            return a.view(ml_dtypes.bfloat16)
        return a

    cores = list(range(NCORES))
    with allow_ici_all_to_all():
        resA = run_bass_kernel_spmd(build_launchA(), host_prep_launchA(inputs),
                                    core_ids=cores)
        qts = [resA.results[c]['qt'] for c in range(NCORES)]
        aparts = [resA.results[c]['apart'] for c in range(NCORES)]
        resB = run_bass_kernel_spmd(build_launchB(),
                                    host_prep_launchB(inputs, qts, aparts),
                                    core_ids=cores)
        x1s = [resB.results[c]['x1'] for c in range(NCORES)]
        ohs = [resB.results[c]['oh'] for c in range(NCORES)]
        xsls = [resB.results[c]['xsl'] for c in range(NCORES)]
        minps = [resB.results[c]['minp'] for c in range(NCORES)]
        maxps = [resB.results[c]['maxp'] for c in range(NCORES)]
        resC = run_bass_kernel_spmd(
            build_launchC(),
            host_prep_launchC(inputs, x1s, ohs, xsls, minps, maxps),
            core_ids=cores)
        x2s = [resC.results[c]['x2'] for c in range(NCORES)]
        uts = [as_bf16(resC.results[c]['ut']) for c in range(NCORES)]
        resD = run_bass_kernel_spmd(build_launchD(),
                                    host_prep_launchD(inputs, uts),
                                    core_ids=cores)
        gs = [as_bf16(resD.results[c]['g']) for c in range(NCORES)]
        resE = run_bass_kernel_spmd(build_launchE(),
                                    host_prep_launchE(inputs, gs, x2s),
                                    core_ids=cores)
        x3s = [resE.results[c]['x3'] for c in range(NCORES)]
        resF = run_bass_kernel_spmd(build_launchF(),
                                    host_prep_launchF(inputs, x3s),
                                    core_ids=cores)
        x4 = np.concatenate([resF.results[c]['x4'] for c in range(NCORES)],
                            axis=0)
    return x4.reshape(1, T, D).astype(np.float32)


# version 11 (final)
# speedup vs baseline: 1.7707x; 1.0081x over previous
import contextlib
import numpy as np

import concourse.bass as bass
import concourse.mybir as mybir
from concourse.bass_utils import run_bass_kernel_spmd
from concourse.aot_env import allow_ici_all_to_all

F32 = mybir.dt.float32
BF16 = mybir.dt.bfloat16
AF = mybir.ActivationFunctionType
ALU = mybir.AluOpType
AX = mybir.AxisListType

NCORES = 8
T = 16384
D = 2048
TPC = T // NCORES
NTB = TPC // 128
NKB = D // 128
ROT = 1024
EPS = 1e-5

D_INNER = 4096
D_STATE = 16
D_CONV = 4
DT_RANK = 128
HIDDEN = 8192
DPC = D_INNER // NCORES

E_MOE = 8
I_MOE = 1408
I_SH = 5632

NUM_TILES = 64
D_SLICE = 32
GRID = 16


class Em:
    ENGINES = ('sync', 'tensor', 'scalar', 'vector', 'gpsimd')

    def __init__(self, nc, stack):
        self.nc = nc
        self.stack = stack
        self.q = {e: [] for e in self.ENGINES}
        self.sems = {}
        self.cnt = {}

    def sem(self, name):
        if name in self.sems:
            return self.sems[name]
        h = self.stack.enter_context(self.nc.semaphore(name))
        self.sems[name] = h
        self.cnt[name] = 0
        return h

    def barrier(self):
        for eng in self.ENGINES:
            for name in list(self.sems):
                if self.cnt[name] > 0:
                    self.wait(eng, name)

    def wait(self, eng, name, val=None):
        if val is None:
            val = self.cnt[name]
        h = self.sems[name]
        self.q[eng].append(lambda e, h=h, v=val: e.wait_ge(h, v))

    AUTO = {'scalar': 'sc', 'vector': 've'}

    def op(self, eng, fn, inc=None, inc_by=1):
        own = self.AUTO.get(eng)
        if own is not None:
            self.wait(eng, own)
            if inc is None:
                inc, inc_by = own, 1
            assert inc == own, f'{eng} ops must inc {own}'
        if inc is None:
            self.q[eng].append(fn)
            return None
        h = self.sems[inc]
        self.cnt[inc] += inc_by
        tgt = self.cnt[inc]
        def wrapped(e, fn=fn, h=h, k=inc_by):
            fn(e).then_inc(h, k)
        self.q[eng].append(wrapped)
        return tgt

    def dma(self, eng, dst, src, inc=None):
        return self.op(eng, lambda e: e.dma_start(out=dst, in_=src), inc=inc,
                       inc_by=16)

    def setup_consts(self, values, cv_sb, cv_dram):
        if not hasattr(self, 'cv_global'):
            self.cv_global = self.stack.enter_context(
                self.nc.sbuf_tensor('cv_global', [128, 8], F32,
                                    allow_name_mangling=True))
            self.sem('ldcv')
            self.dma('sync', self.cv_global[:],
                     cv_dram.partition_broadcast(128).squeeze(1), inc='ldcv')
            for eng in ('scalar', 'vector', 'tensor', 'gpsimd'):
                self.wait(eng, 'ldcv')
            self._consts_done = set()
        for i, v in enumerate(values):
            if float(v) not in self._consts_done:
                self._consts_done.add(float(v))
                self.nc.const_aps.register(F32, float(v),
                                           self.cv_global[:, i:i + 1])

    def coll(self, kind, op, ins, outs, inc):
        h = self.sems[inc]
        self.cnt[inc] += 1
        tgt = 1
        rg = [list(range(NCORES))]
        def wrapped(e, h=h):
            with allow_ici_all_to_all():
                e.collective_compute(kind, op, replica_groups=rg, ins=ins,
                                     outs=outs).then_inc(h)
        self.q['gpsimd'].append(wrapped)
        return tgt

    def finalize(self):
        with self.nc.Block() as block:
            for eng in self.ENGINES:
                ops = self.q[eng]
                if not ops:
                    continue
                dec = getattr(block, eng)
                def body(e, ops=ops):
                    for fn in ops:
                        fn(e)
                dec(body)


class PP:
    def __init__(self, em, stack, nc, name, shape, n=2, banks=None):
        if banks is not None:
            self.ps = banks
            n = len(banks)
        else:
            self.ps = [stack.enter_context(nc.psum_tensor(
                f'{name}{i}', shape, F32, allow_name_mangling=True))
                for i in range(n)]
        self.gate = [None] * n
        self.i = 0

    def nxt(self, em):
        k = self.i % len(self.ps)
        if self.gate[k] is not None:
            em.wait('tensor', self.gate[k][0], self.gate[k][1])
        self.i += 1
        return self.ps[k]

    def done(self, sem, cnt):
        k = (self.i - 1) % len(self.ps)
        self.gate[k] = (sem, cnt)


def mm_accum(em, ps_ap, parts, inc='te', skip_group_check=False):
    n = len(parts)
    last = None
    for i, (lh, rh) in enumerate(parts):
        fn = lambda e, lh=lh, rh=rh, i=i: e.matmul(
            ps_ap, lh, rh, start=(i == 0), stop=(i == n - 1),
            skip_group_check=skip_group_check)
        if i == n - 1:
            last = em.op('tensor', fn, inc=inc)
        else:
            em.op('tensor', fn)
    return last


def emit_plain_ln(em, src, dst, stt, sq, n=D, eps=EPS):
    s_sum = stt[:, 0:1]
    s_sumsq = stt[:, 1:2]
    s_mean = stt[:, 2:3]
    s_m2 = stt[:, 3:4]
    s_var = stt[:, 4:5]
    s_rstd = stt[:, 5:6]
    s_negmr = stt[:, 6:7]
    inv_n = 1.0 / n
    em.wait('vector', 'sc')
    em.wait('scalar', 've')
    c_sq = em.op('scalar', lambda e: e.activation(sq, src, AF.Square,
                                                  accum_out=s_sumsq),
                 inc='sc')
    em.op('vector', lambda e: e.reduce_sum(s_sum, src, AX.X))
    em.wait('vector', 'sc', c_sq)
    em.op('vector', lambda e: e.tensor_scalar(s_mean, s_sum, inv_n, None,
                                              ALU.mult))
    em.op('vector', lambda e: e.tensor_tensor(s_m2, s_mean, s_mean, ALU.mult))
    c_var = em.op('vector', lambda e: e.scalar_tensor_tensor(
        s_var, s_sumsq, inv_n, s_m2, ALU.mult, ALU.subtract), inc='ve')
    em.wait('scalar', 've', c_var)
    em.op('scalar', lambda e: e.activation(s_rstd, s_var, AF.Ln, bias=eps))
    c_r = em.op('scalar', lambda e: e.activation(s_rstd, s_rstd, AF.Exp,
                                                 scale=-0.5), inc='sc')
    em.wait('vector', 'sc', c_r)
    c_nm = em.op('vector', lambda e: e.scalar_tensor_tensor(
        s_negmr, s_mean, -1.0, s_rstd, ALU.mult, ALU.mult), inc='ve')
    em.wait('scalar', 've', c_nm)
    return em.op('scalar', lambda e: e.activation(dst, src, AF.Identity,
                                                  scale=s_rstd,
                                                  bias=s_negmr), inc='sc')


def declare_io_A(nc, internal=()):
    io = {}
    io['x0'] = nc.dram_tensor('x0', [TPC, D], F32, kind=None if 'x0' in internal else 'ExternalInput').ap()
    io['wq'] = nc.dram_tensor('wq', [D, D], F32, kind=None if 'wq' in internal else 'ExternalInput').ap()
    io['wk'] = nc.dram_tensor('wk', [D, D], F32, kind=None if 'wk' in internal else 'ExternalInput').ap()
    io['wv'] = nc.dram_tensor('wv', [D, D], F32, kind=None if 'wv' in internal else 'ExternalInput').ap()
    io['bqkv'] = nc.dram_tensor('bqkv', [3, D], F32, kind=None if 'bqkv' in internal else 'ExternalInput').ap()
    io['gb'] = nc.dram_tensor('gb', [2, D], F32, kind=None if 'gb' in internal else 'ExternalInput').ap()
    io['cos'] = nc.dram_tensor('cos', [TPC, ROT // 2], F32, kind=None if 'cos' in internal else 'ExternalInput').ap()
    io['sin'] = nc.dram_tensor('sin', [TPC, ROT // 2], F32, kind=None if 'sin' in internal else 'ExternalInput').ap()
    io['ones'] = nc.dram_tensor('ones', [128, 128], F32, kind=None if 'ones' in internal else 'ExternalInput').ap()
    io['ident'] = nc.dram_tensor('ident', [128, 128], F32, kind=None if 'ident' in internal else 'ExternalInput').ap()
    io['cvals'] = nc.dram_tensor('cvals', [1, 8], F32, kind=None if 'cvals' in internal else 'ExternalInput').ap()
    io['qt'] = nc.dram_tensor('qt', [D, TPC], F32, kind=None if 'qt' in internal else 'ExternalOutput').ap()
    io['apart'] = nc.dram_tensor('apart', [1, D], F32, kind=None if 'apart' in internal else 'ExternalOutput').ap()
    io['kst'] = nc.dram_tensor('kst', [TPC, D], F32).ap()
    return io


def emit_A(nc, em, stack, io):
    X0 = io['x0']
    WQ = io['wq']
    WK = io['wk']
    WV = io['wv']
    BQKV = io['bqkv']
    GB = io['gb']
    COS = io['cos']
    SIN = io['sin']
    ONES = io['ones']
    IDENT = io['ident']
    CVALS = io['cvals']
    QT = io['qt']
    APART = io['apart']
    KST = io['kst']
    sb = lambda name, shp, dt: stack.enter_context(
        nc.sbuf_tensor(name, shp, dt, allow_name_mangling=True))
    xnT = sb('xnT', [128, NKB * TPC], BF16)
    wbig = sb('wbig', [128, NKB * D], BF16)
    xtb = sb('xtb', [128, D], F32)
    t1tb = sb('t1tb', [128, D], F32)
    sqtb = sb('sqtb', [128, D], F32)
    brtb = sqtb
    ktile = sb('ktile', [128, D], F32)
    vtile = sb('vtile', [128, D], F32)
    rot_a = sb('rot_a', [128, ROT // 2], F32)
    rot_b = sb('rot_b', [128, ROT // 2], F32)
    rot_c = sb('rot_c', [128, ROT // 2], F32)
    costb = sb('costb', [128, ROT // 2], F32)
    sintb = sb('sintb', [128, ROT // 2], F32)
    wld = sb('wld', [128, D], F32)
    stt = sb('stt', [128, 16], F32)
    ones_sb = sb('ones_sb', [128, 128], F32)
    ident_sb = sb('ident_sb', [128, 128], F32)
    gb_sb = sb('gb_sb', [128, 2 * NKB], F32)
    bq_sb = sb('bq_sb', [128, NKB], F32)
    brow = t1tb[0:1, :]
    qout = sb('qout', [128, 512], F32)
    arow = ktile[0:1, :]
    cv_sb = sb('cv_sb', [128, 8], F32)
    ppq = PP(em, stack, nc, 'psq', [128, 512], n=4)
    ppk = ppq
    ppt = PP(em, stack, nc, 'pst', [128, 128])
    psb = stack.enter_context(nc.psum_tensor('psb', [128, 512], F32, allow_name_mangling=True))
    psa1 = stack.enter_context(nc.psum_tensor('psa1', [1, 512], F32,
                                              allow_name_mangling=True))
    aacc = sb('aacc', [1, D], F32)

    for s in ['ldc', 'ldx', 'ldw', 'ldb', 'ldcos', 'ldsin', 'ldk',
              'outq', 'outk', 'outa', 'sc', 've', 'te']:
        em.sem(s)

    em.setup_consts([EPS, -0.5], cv_sb, CVALS)
    em.dma('sync', ones_sb[:], ONES, inc='ldc')
    em.dma('sync', ident_sb[:], IDENT, inc='ldc')
    for i in range(2):
        for kb in range(NKB):
            em.dma('sync', gb_sb[:, i * NKB + kb:i * NKB + kb + 1],
                   GB[i:i + 1, kb * 128:(kb + 1) * 128].rearrange(
                       'a p -> p a'), inc='ldc')
    for kb in range(NKB):
        em.dma('sync', bq_sb[:, kb:kb + 1],
               BQKV[0:1, kb * 128:(kb + 1) * 128].rearrange('a p -> p a'),
               inc='ldc')
    em.wait('scalar', 'ldc')
    em.wait('vector', 'ldc')
    em.wait('tensor', 'ldc')

    em.op('vector', lambda e: e.memset(aacc[:], 0.0))
    for tb in range(NTB):
        cx = em.dma('sync', xtb[:], X0[tb * 128:(tb + 1) * 128, :],
                    inc='ldx')
        em.wait('scalar', 'ldx', cx)
        em.wait('vector', 'ldx', cx)
        emit_plain_ln(em, xtb[:], t1tb[:], stt, sqtb[:])
        c = emit_plain_ln(em, t1tb[:], xtb[:], stt, sqtb[:])
        em.wait('tensor', 'sc', c)
        for kb in range(NKB):
            pstk = ppt.nxt(em)
            ct = em.op('tensor', lambda e, kb=kb, pstk=pstk: e.transpose(
                pstk[:], xtb[:, kb * 128:(kb + 1) * 128], ident_sb[:]),
                inc='te')
            em.wait('scalar', 'te', ct)
            ce2 = em.op('scalar', lambda e, kb=kb, tb=tb, pstk=pstk:
                        e.activation(
                xnT[:, kb * TPC + tb * 128: kb * TPC + (tb + 1) * 128],
                pstk[:], AF.Identity, scale=gb_sb[:, kb:kb + 1],
                bias=gb_sb[:, NKB + kb:NKB + kb + 1]), inc='sc')
            ppt.done('sc', ce2)
        em.wait('sync', 'sc')

    def load_w(WSRC):
        em.wait('scalar', 'te')
        for kb in range(NKB):
            cx = em.dma('sync', wld[:], WSRC[kb * 128:(kb + 1) * 128, :],
                        inc='ldw')
            em.wait('scalar', 'ldw', cx)
            c = em.op('scalar', lambda e, kb=kb: e.activation(
                wbig[:, kb * D:(kb + 1) * D], wld[:], AF.Copy), inc='sc')
            em.wait('sync', 'sc', c)
        return em.cnt['sc']

    cw = load_w(WQ)
    em.wait('tensor', 'sc', cw)
    for cb in range(NKB):
        for tch in range(TPC // 512):
            parts = [(wbig[:, kb * D + cb * 128:kb * D + (cb + 1) * 128],
                      xnT[:, kb * TPC + tch * 512:kb * TPC + (tch + 1) * 512])
                     for kb in range(NKB)]
            psq = ppq.nxt(em)
            cm = mm_accum(em, psq[:], parts, inc='te')
            em.wait('scalar', 'te', cm)
            ce = em.op('scalar', lambda e, cb=cb, psq=psq: e.activation(
                qout[:], psq[:], AF.Identity, bias=bq_sb[:, cb:cb + 1]),
                inc='sc')
            ppq.done('sc', ce)
            em.wait('gpsimd', 'sc', ce)
            cd = em.op('gpsimd', lambda e, cb=cb, tch=tch: e.dma_start(
                out=QT[cb * 128:(cb + 1) * 128, tch * 512:(tch + 1) * 512],
                in_=qout[:]), inc='outq', inc_by=16)
            em.wait('scalar', 'outq', cd)

    def kv_pass(which):
        WSRC = WK if which == 0 else WV
        tile = ktile if which == 0 else vtile
        em.wait('sync', 've')
        em.wait('sync', 'outk')
        cbr = em.dma('sync', brow, BQKV[1 + which:2 + which, :],
                     inc='ldb')
        cw = load_w(WSRC)
        em.wait('tensor', 'sc', cw)
        em.wait('tensor', 'ldb', cbr)
        em.wait('scalar', 've')
        for ncx in range(4):
            em.wait('tensor', 'sc')
            cm = em.op('tensor', lambda e, ncx=ncx: e.matmul(
                psb[:], ones_sb[0:1, :],
                brow[:, ncx * 512:(ncx + 1) * 512], start=True, stop=True),
                inc='te')
            em.wait('scalar', 'te', cm)
            em.op('scalar', lambda e, ncx=ncx: e.activation(
                brtb[:, ncx * 512:(ncx + 1) * 512], psb[:], AF.Copy),
                inc='sc')
        em.wait('tensor', 'sc')
        em.wait('vector', 'sc')

        for tb in range(NTB):
            if tb > 0:
                em.wait('sync', 've')
            c_cos = em.dma('sync', costb[:],
                           COS[tb * 128:(tb + 1) * 128, :], inc='ldcos')
            c_sin = em.dma('sync', sintb[:],
                           SIN[tb * 128:(tb + 1) * 128, :], inc='ldsin')
            if which == 1:
                c_krb = em.dma('sync', ktile[:],
                               KST[tb * 128:(tb + 1) * 128, :], inc='ldk')
            for ncx in range(4):
                parts = [(xnT[:, kb * TPC + tb * 128:kb * TPC + (tb + 1) * 128],
                          wbig[:, kb * D + ncx * 512:kb * D + (ncx + 1) * 512])
                         for kb in range(NKB)]
                psk = ppk.nxt(em)
                cm = mm_accum(em, psk[:], parts, inc='te')
                em.wait('vector', 'te', cm)
                cv = em.op('vector', lambda e, ncx=ncx, psk=psk:
                           e.tensor_tensor(
                    tile[:, ncx * 512:(ncx + 1) * 512], psk[:],
                    brtb[:, ncx * 512:(ncx + 1) * 512], ALU.add),
                    inc='ve')
                ppk.done('ve', cv)
            em.wait('vector', 'ldcos', c_cos)
            em.wait('vector', 'ldsin', c_sin)
            r3 = tile[:, 0:ROT].rearrange('p (j two) -> p two j', two=2)
            ev = r3[:, 0, :]
            od = r3[:, 1, :]
            em.op('vector', lambda e, ev=ev: e.tensor_tensor(
                rot_a[:], ev, costb[:], ALU.mult))
            em.op('vector', lambda e, od=od: e.tensor_tensor(
                rot_b[:], od, sintb[:], ALU.mult))
            em.op('vector', lambda e: e.tensor_tensor(
                rot_a[:], rot_a[:], rot_b[:], ALU.subtract))
            em.op('vector', lambda e, od=od: e.tensor_tensor(
                rot_b[:], od, costb[:], ALU.mult))
            em.op('vector', lambda e, ev=ev: e.tensor_tensor(
                rot_c[:], ev, sintb[:], ALU.mult))
            em.op('vector', lambda e, od=od: e.tensor_tensor(
                od, rot_b[:], rot_c[:], ALU.add))
            cv = em.op('vector', lambda e, ev=ev: e.tensor_copy(
                ev, rot_a[:]), inc='ve')
            if which == 0:
                em.wait('gpsimd', 've', cv)
                cd = em.op('gpsimd', lambda e, tb=tb: e.dma_start(
                    out=KST[tb * 128:(tb + 1) * 128, :], in_=ktile[:]),
                    inc='outk', inc_by=16)
                em.wait('vector', 'outk', cd)
            else:
                em.wait('scalar', 've', cv)
                em.wait('scalar', 'ldk', c_krb)
                em.op('scalar', lambda e: e.activation(
                    t1tb[:], ktile[:], AF.Square, accum_out=stt[:, 8:9]))
                em.op('scalar', lambda e: e.activation(
                    t1tb[:], vtile[:], AF.Square, accum_out=stt[:, 9:10]))
                em.op('scalar', lambda e: e.activation(
                    stt[:, 10:11], stt[:, 8:9], AF.Ln))
                cs = em.op('scalar', lambda e: e.activation(
                    stt[:, 11:12], stt[:, 9:10], AF.Ln), inc='sc')
                em.wait('vector', 'sc', cs)
                cv2 = em.op('vector', lambda e: e.tensor_tensor(
                    stt[:, 13:14], stt[:, 10:11], stt[:, 11:12], ALU.add),
                    inc='ve')
                em.wait('scalar', 've', cv2)
                cs2 = em.op('scalar', lambda e: e.activation(
                    stt[:, 14:15], stt[:, 13:14], AF.Exp, scale=-0.5),
                    inc='sc')
                em.wait('vector', 'sc', cs2)
                em.op('vector', lambda e: e.tensor_tensor(
                    t1tb[:], ktile[:], vtile[:], ALU.mult))
                cv3 = em.op('vector', lambda e: e.tensor_scalar(
                    t1tb[:], t1tb[:], stt[:, 14:15], None, ALU.mult),
                    inc='ve')
                em.wait('tensor', 've', cv3)
                for ncx in range(4):
                    cm = em.op('tensor', lambda e, ncx=ncx: e.matmul(
                        psa1[:], ones_sb[:, 0:1],
                        t1tb[:, ncx * 512:(ncx + 1) * 512],
                        start=True, stop=True), inc='te')
                    em.wait('vector', 'te', cm)
                    cva = em.op('vector', lambda e, ncx=ncx: e.tensor_tensor(
                        aacc[:, ncx * 512:(ncx + 1) * 512],
                        aacc[:, ncx * 512:(ncx + 1) * 512], psa1[:], ALU.add),
                        inc='ve')
                    em.wait('tensor', 've', cva)

    kv_pass(0)
    kv_pass(1)
    em.wait('gpsimd', 've')
    em.dma('gpsimd', APART, aacc[:], inc='outa')
    em.wait('gpsimd', 'outa')
    em.wait('gpsimd', 'outq')



def build_launchA():
    nc = bass.Bass('TRN2', target_bir_lowering=False)
    io = declare_io_A(nc)
    stack = contextlib.ExitStack()
    with stack:
        em = Em(nc, stack)
        emit_A(nc, em, stack, io)
        em.finalize()
    return nc


BIGC = 65536.0

def declare_io_B(nc, internal=()):
    io = {}
    io['x0'] = nc.dram_tensor('x0', [TPC, D], F32, kind=None if 'x0' in internal else 'ExternalInput').ap()
    io['qt'] = nc.dram_tensor('qt', [D, TPC], F32, kind=None if 'qt' in internal else 'ExternalInput').ap()
    io['aparts'] = nc.dram_tensor('aparts', [NCORES, D], F32, kind=None if 'aparts' in internal else 'ExternalInput').ap()
    io['wout'] = nc.dram_tensor('wout', [D, D], F32, kind=None if 'wout' in internal else 'ExternalInput').ap()
    io['bout'] = nc.dram_tensor('bout', [1, D], F32, kind=None if 'bout' in internal else 'ExternalInput').ap()
    io['gb2'] = nc.dram_tensor('gb2', [2, D], F32, kind=None if 'gb2' in internal else 'ExternalInput').ap()
    io['slopes'] = nc.dram_tensor('slopes', [NUM_TILES, D_SLICE * GRID], F32, kind=None if 'slopes' in internal else 'ExternalInput').ap()
    io['ones'] = nc.dram_tensor('ones', [128, 128], F32, kind=None if 'ones' in internal else 'ExternalInput').ap()
    io['ident'] = nc.dram_tensor('ident', [128, 128], F32, kind=None if 'ident' in internal else 'ExternalInput').ap()
    io['cvals'] = nc.dram_tensor('cvals', [1, 8], F32, kind=None if 'cvals' in internal else 'ExternalInput').ap()
    io['tab'] = nc.dram_tensor('tab', [3, 64], F32, kind=None if 'tab' in internal else 'ExternalInput').ap()
    io['x1'] = nc.dram_tensor('x1', [TPC, D], F32, kind=None if 'x1' in internal else 'ExternalOutput').ap()
    io['oh'] = nc.dram_tensor('oh', [TPC, 64], F32, kind=None if 'oh' in internal else 'ExternalOutput').ap()
    io['xsl'] = nc.dram_tensor('xsl', [TPC, 32], F32, kind=None if 'xsl' in internal else 'ExternalOutput').ap()
    io['minp'] = nc.dram_tensor('minp', [64, 1], F32, kind=None if 'minp' in internal else 'ExternalOutput').ap()
    io['maxp'] = nc.dram_tensor('maxp', [64, 1], F32, kind=None if 'maxp' in internal else 'ExternalOutput').ap()
    io['ss'] = nc.dram_tensor('ss', [NUM_TILES, D_SLICE], F32).ap()
    return io


def emit_B(nc, em, stack, io):
    X0 = io['x0']
    QT = io['qt']
    APARTS = io.get('aparts')
    WOUT = io['wout']
    BOUT = io['bout']
    GB2 = io['gb2']
    SLOPES = io['slopes']
    ONES = io['ones']
    IDENT = io['ident']
    CVALS = io['cvals']
    TAB = io['tab']
    X1 = io['x1']
    OH = io['oh']
    XSL = io['xsl']
    MINP = io['minp']
    MAXP = io['maxp']
    SS = io['ss']
    sb = lambda name, shp, dt: stack.enter_context(
        nc.sbuf_tensor(name, shp, dt, allow_name_mangling=True))
    ybig = sb('ybig', [128, NKB * TPC], BF16)
    wbig = sb('wbig', [128, NKB * D], BF16)
    xtb = sb('xtb', [128, D], F32)
    t1tb = sb('t1tb', [128, D], F32)
    t2tb = sb('t2tb', [128, D], F32)
    sqtb = sb('sqtb', [128, D], F32)
    wld = sb('wld', [128, D], F32)
    brtb = sb('brtb', [128, D], F32)
    g2rep = brtb
    b2rep = wld
    stt = sb('stt', [128, 16], F32)
    ones_sb = sb('ones_sb', [128, 128], F32)
    ident_sb = sb('ident_sb', [128, 128], F32)
    afac = sb('afac', [128, NKB], F32)
    apld = sb('apld', [128, NKB], F32)
    cv_sb = sb('cv_sb', [128, 8], F32)
    ssrep = sb('ssrep', [128, D], F32)
    sl_sb = sb('sl_sb', [64, D_SLICE * GRID], F32)
    ss_sb = sb('ss_sb', [64, D_SLICE], F32)
    iot64 = sb('iot64', [128, 64], F32)
    cidx = sb('cidx', [128, 64], F32)
    iot8 = sb('iot8', [128, 8], F32)
    iot64m = sb('iot64m', [128, 64], F32)
    iot8m = sb('iot8m', [128, 8], F32)
    scor = sb('scor', [128, 64], F32)
    msk = sb('msk', [128, 64], F32)
    ohtb = sb('ohtb', [128, 64], F32)
    xsl_sb = sb('xsl_sb', [128, 32], F32)
    mmtb = sb('mmtb', [128, 64], F32)
    minacc = sb('minacc', [64, 4], F32)
    redsc = sb('redsc', [64, 128], F32)
    pst = stack.enter_context(nc.psum_tensor('pst', [128, 128], F32, allow_name_mangling=True))
    psb = stack.enter_context(nc.psum_tensor('psb', [128, 512], F32, allow_name_mangling=True))
    ppy = PP(em, stack, nc, 'psy', [128, 512], n=4)
    psm = stack.enter_context(nc.psum_tensor('psm', [64, 128], F32, allow_name_mangling=True))

    for s in ['ldc', 'lda0', 'lda1', 'ldx', 'ldw', 'ldq', 'ldb', 'ldss',
              'outx', 'outo', 'outs', 'outm', 'outss', 'sc', 've', 'te']:
        em.sem(s)

    em.setup_consts([EPS, -0.5], cv_sb, CVALS)
    em.dma('sync', ones_sb[:], ONES, inc='ldc')
    em.dma('sync', ident_sb[:], IDENT, inc='ldc')
    em.dma('sync', iot64[:], TAB[0:1, :].partition_broadcast(128).squeeze(1),
           inc='ldc')
    em.dma('sync', cidx[:], TAB[1:2, :].partition_broadcast(128).squeeze(1),
           inc='ldc')
    em.dma('sync', iot8[:], TAB[2:3, 0:8].partition_broadcast(128).squeeze(1),
           inc='ldc')
    em.dma('sync', sl_sb[:], SLOPES, inc='ldc')
    em.wait('vector', 'ldc')
    em.wait('scalar', 'ldc')
    em.wait('tensor', 'ldc')
    em.op('vector', lambda e: e.tensor_scalar(iot64m[:], iot64[:], -BIGC,
                                              None, ALU.add))
    em.op('vector', lambda e: e.tensor_scalar(iot8m[:], iot8[:], -BIGC,
                                              None, ALU.add))
    em.op('vector', lambda e: e.memset(minacc[:, 0:1], BIGC))
    em.op('vector', lambda e: e.memset(minacc[:, 1:2], -BIGC))

    if 'asum' in io:
        for kb in range(NKB):
            em.dma('sync', apld[:, kb:kb + 1],
                   io['asum'][0:1, kb * 128:(kb + 1) * 128].rearrange(
                       'a p -> p a'), inc='lda0')
        em.wait('vector', 'lda0')
    else:
      for r in range(NCORES):
        dst = apld if r == 0 else afac
        sem_r = 'lda0' if r % 2 == 0 else 'lda1'
        for kb in range(NKB):
            em.dma('sync', dst[:, kb:kb + 1],
                   APARTS[r:r + 1, kb * 128:(kb + 1) * 128].rearrange(
                       'a p -> p a'), inc=sem_r)
        em.wait('vector', sem_r)
        if r > 0:
            em.op('vector', lambda e: e.tensor_tensor(
                apld[:], apld[:], afac[:], ALU.add))
        em.wait('sync', 've')
    cA = em.op('vector', lambda e: e.tensor_scalar(
        apld[:], apld[:], 0.5, 0.5, ALU.mult, ALU.add), inc='ve')

    cb = em.dma('sync', t1tb[0:1, :], BOUT, inc='ldb')
    em.wait('tensor', 'ldb', cb)
    for ncx in range(4):
        em.wait('tensor', 'sc')
        cm = em.op('tensor', lambda e, ncx=ncx: e.matmul(
            psb[:], ones_sb[0:1, :], t1tb[0:1, ncx * 512:(ncx + 1) * 512],
            start=True, stop=True), inc='te')
        em.wait('scalar', 'te', cm)
        em.op('scalar', lambda e, ncx=ncx: e.activation(
            brtb[:, ncx * 512:(ncx + 1) * 512], psb[:], AF.Copy),
            inc='sc')

    em.op('vector', lambda e: e.tensor_reduce(
        ss_sb[:], sl_sb[:].rearrange('p (j g) -> p j g', g=GRID),
        AX.X, ALU.add))
    em.wait('scalar', 've')
    cs = em.op('scalar', lambda e: e.activation(
        ss_sb[:], ss_sb[:], AF.Sign), inc='sc')
    em.wait('gpsimd', 'sc', cs)
    cd = em.op('gpsimd', lambda e: e.dma_start(out=SS, in_=ss_sb[:]),
               inc='outss', inc_by=16)
    em.wait('sync', 'outss', cd)
    cr = em.dma('sync', ssrep[:],
                SS.rearrange('a b -> (a b)').unsqueeze(0)
                .partition_broadcast(128).squeeze(1), inc='ldss')

    em.wait('scalar', 've', cA)
    for kb in range(NKB):
        cq = em.dma('sync', wld[:], QT[kb * 128:(kb + 1) * 128, :],
                    inc='ldq')
        em.wait('scalar', 'ldq', cq)
        c = em.op('scalar', lambda e, kb=kb: e.activation(
            ybig[:, kb * TPC:(kb + 1) * TPC], wld[:], AF.Identity,
            scale=apld[:, kb:kb + 1]), inc='sc')
        em.wait('sync', 'sc', c)

    em.wait('scalar', 'te')
    for kb in range(NKB):
        cx = em.dma('sync', wld[:], WOUT[kb * 128:(kb + 1) * 128, :],
                    inc='ldw')
        em.wait('scalar', 'ldw', cx)
        c = em.op('scalar', lambda e, kb=kb: e.activation(
            wbig[:, kb * D:(kb + 1) * D], wld[:], AF.Copy), inc='sc')
        em.wait('sync', 'sc', c)
    em.wait('tensor', 'sc')

    for tb in range(NTB):
        cx = em.dma('sync', xtb[:], X0[tb * 128:(tb + 1) * 128, :],
                    inc='ldx')
        em.wait('scalar', 'ldx', cx)
        em.wait('vector', 'ldx', cx)
        emit_plain_ln(em, xtb[:], t1tb[:], stt, sqtb[:])
        em.wait('vector', 'sc')
        for ncx in range(4):
            parts = [(ybig[:, kb * TPC + tb * 128:kb * TPC + (tb + 1) * 128],
                      wbig[:, kb * D + ncx * 512:kb * D + (ncx + 1) * 512])
                     for kb in range(NKB)]
            psy = ppy.nxt(em)
            cm = mm_accum(em, psy[:], parts, inc='te')
            em.wait('vector', 'te', cm)
            em.op('vector', lambda e, ncx=ncx, psy=psy: e.tensor_tensor(
                t2tb[:, ncx * 512:(ncx + 1) * 512], psy[:],
                brtb[:, ncx * 512:(ncx + 1) * 512], ALU.add))
            cv = em.op('vector', lambda e, ncx=ncx: e.tensor_tensor(
                t2tb[:, ncx * 512:(ncx + 1) * 512],
                t2tb[:, ncx * 512:(ncx + 1) * 512],
                t1tb[:, ncx * 512:(ncx + 1) * 512], ALU.add), inc='ve')
            ppy.done('ve', cv)
        emit_plain_ln(em, t2tb[:], t2tb[:], stt, sqtb[:])
        em.wait('vector', 'sc')
        cv = em.op('vector', lambda e: e.tensor_tensor(
            xtb[:], xtb[:], t2tb[:], ALU.add), inc='ve')
        em.wait('gpsimd', 've', cv)
        cdx = em.op('gpsimd', lambda e, tb=tb: e.dma_start(
            out=X1[tb * 128:(tb + 1) * 128, :], in_=xtb[:]),
            inc='outx', inc_by=16)
        em.wait('sync', 'outx', cdx)

    em.wait('scalar', 've')
    em.wait('tensor', 've')
    for i in range(2):
        dst = g2rep if i == 0 else b2rep
        cb = em.dma('sync', t1tb[0:1, :], GB2[i:i + 1, :], inc='ldb')
        em.wait('tensor', 'ldb', cb)
        for ncx in range(4):
            em.wait('tensor', 'sc')
            cm = em.op('tensor', lambda e, ncx=ncx: e.matmul(
                psb[:], ones_sb[0:1, :],
                t1tb[0:1, ncx * 512:(ncx + 1) * 512],
                start=True, stop=True), inc='te')
            em.wait('scalar', 'te', cm)
            em.op('scalar', lambda e, dst=dst, ncx=ncx: e.activation(
                dst[:, ncx * 512:(ncx + 1) * 512], psb[:], AF.Copy),
                inc='sc')
        em.wait('sync', 'sc')

    em.wait('vector', 'ldss', cr)

    for tb in range(NTB):
        em.wait('sync', 'sc')
        em.wait('sync', 've')
        cx = em.dma('sync', xtb[:], X1[tb * 128:(tb + 1) * 128, :],
                    inc='ldx')
        em.wait('scalar', 'ldx', cx)
        em.wait('vector', 'ldx', cx)
        emit_plain_ln(em, xtb[:], t1tb[:], stt, sqtb[:])
        c3 = emit_plain_ln(em, t1tb[:], t2tb[:], stt, sqtb[:])
        em.wait('vector', 'sc', c3)
        em.op('vector', lambda e: e.tensor_tensor(
            t2tb[:], t2tb[:], g2rep[:], ALU.mult))
        cxn = em.op('vector', lambda e: e.tensor_tensor(
            t2tb[:], t2tb[:], b2rep[:], ALU.add), inc='ve')
        em.wait('scalar', 've', cxn)
        em.op('scalar', lambda e: e.activation(t1tb[:], t2tb[:], AF.Sign),
              inc='sc')
        em.wait('vector', 'sc')
        em.op('vector', lambda e: e.tensor_tensor(
            sqtb[:], t1tb[:], ssrep[:], ALU.mult))
        em.op('vector', lambda e: e.tensor_reduce(
            scor[:], sqtb[:].rearrange('p (s i) -> p s i', i=D_SLICE),
            AX.X, ALU.add))
        em.op('vector', lambda e: e.tensor_reduce(
            stt[:, 8:16], scor[:].rearrange('p (c i) -> p c i', i=8),
            AX.X, ALU.add))
        em.op('vector', lambda e: e.tensor_reduce(
            stt[:, 7:8], stt[:, 8:16], AX.X, ALU.max))
        em.op('vector', lambda e: e.tensor_scalar(
            msk[:, 0:8], stt[:, 8:16], stt[:, 7:8], None, ALU.is_ge))
        em.op('vector', lambda e: e.tensor_tensor(
            msk[:, 8:16], msk[:, 0:8], iot8m[:], ALU.mult))
        em.op('vector', lambda e: e.tensor_scalar(
            msk[:, 8:16], msk[:, 8:16], BIGC, None, ALU.add))
        em.op('vector', lambda e: e.tensor_reduce(
            stt[:, 0:1], msk[:, 8:16], AX.X, ALU.min))
        em.op('vector', lambda e: e.tensor_scalar(
            msk[:], cidx[:], stt[:, 0:1], None, ALU.is_equal))
        em.op('vector', lambda e: e.tensor_scalar(
            mmtb[:], scor[:], BIGC, None, ALU.add))
        em.op('vector', lambda e: e.tensor_tensor(
            mmtb[:], mmtb[:], msk[:], ALU.mult))
        em.op('vector', lambda e: e.tensor_scalar(
            mmtb[:], mmtb[:], -BIGC, None, ALU.add))
        em.op('vector', lambda e: e.tensor_reduce(
            stt[:, 1:2], mmtb[:], AX.X, ALU.max))
        em.op('vector', lambda e: e.tensor_scalar(
            scor[:], mmtb[:], stt[:, 1:2], None, ALU.is_ge))
        em.op('vector', lambda e: e.tensor_tensor(
            scor[:], scor[:], iot64m[:], ALU.mult))
        em.op('vector', lambda e: e.tensor_scalar(
            scor[:], scor[:], BIGC, None, ALU.add))
        em.op('vector', lambda e: e.tensor_reduce(
            stt[:, 2:3], scor[:], AX.X, ALU.min))
        cohv = em.op('vector', lambda e: e.tensor_scalar(
            ohtb[:], iot64[:], stt[:, 2:3], None, ALU.is_equal), inc='ve')
        em.wait('gpsimd', 've', cohv)
        cdo = em.op('gpsimd', lambda e, tb=tb: e.dma_start(
            out=OH[tb * 128:(tb + 1) * 128, :], in_=ohtb[:]),
            inc='outo', inc_by=16)
        em.op('vector', lambda e: e.memset(xsl_sb[:], 0.0))
        for i in range(NUM_TILES):
            em.op('vector', lambda e, i=i: e.scalar_tensor_tensor(
                xsl_sb[:], t2tb[:, i * D_SLICE:(i + 1) * D_SLICE],
                ohtb[:, i:i + 1], xsl_sb[:], ALU.mult, ALU.add))
        cxs = em.op('vector', lambda e: e.tensor_copy(
            xsl_sb[:], xsl_sb[:]), inc='ve')
        em.wait('gpsimd', 've', cxs)
        cds = em.op('gpsimd', lambda e, tb=tb: e.dma_start(
            out=XSL[tb * 128:(tb + 1) * 128, :], in_=xsl_sb[:]),
            inc='outs', inc_by=16)
        em.op('vector', lambda e: e.tensor_reduce(
            stt[:, 3:4], xsl_sb[:], AX.X, ALU.min))
        em.op('vector', lambda e: e.tensor_reduce(
            stt[:, 4:5], xsl_sb[:], AX.X, ALU.max))
        em.op('vector', lambda e: e.tensor_scalar(
            stt[:, 5:6], stt[:, 3:4], -BIGC, None, ALU.add))
        em.op('vector', lambda e: e.tensor_scalar(
            stt[:, 6:7], stt[:, 4:5], BIGC, None, ALU.add))
        cv1 = em.op('vector', lambda e: e.tensor_scalar(
            mmtb[:], ohtb[:], stt[:, 5:6], BIGC, ALU.mult, ALU.add),
            inc='ve')
        em.wait('tensor', 've', cv1)
        em.wait('tensor', 'sc')
        ct = em.op('tensor', lambda e: e.transpose(
            psm[:], mmtb[:, 0:64], ident_sb[:]), inc='te')
        em.wait('scalar', 'te', ct)
        cev = em.op('scalar', lambda e: e.activation(
            redsc[:], psm[:], AF.Copy), inc='sc')
        em.wait('vector', 'sc', cev)
        em.op('vector', lambda e: e.tensor_reduce(
            minacc[:, 2:3], redsc[:], AX.X, ALU.min))
        cv2 = em.op('vector', lambda e: e.tensor_tensor(
            minacc[:, 0:1], minacc[:, 0:1], minacc[:, 2:3], ALU.min),
            inc='ve')
        cv3 = em.op('vector', lambda e: e.tensor_scalar(
            mmtb[:], ohtb[:], stt[:, 6:7], -BIGC, ALU.mult, ALU.add),
            inc='ve')
        em.wait('tensor', 've', cv3)
        em.wait('tensor', 'sc')
        ct = em.op('tensor', lambda e: e.transpose(
            psm[:], mmtb[:, 0:64], ident_sb[:]), inc='te')
        em.wait('scalar', 'te', ct)
        cev = em.op('scalar', lambda e: e.activation(
            redsc[:], psm[:], AF.Copy), inc='sc')
        em.wait('vector', 'sc', cev)
        em.op('vector', lambda e: e.tensor_reduce(
            minacc[:, 3:4], redsc[:], AX.X, ALU.max))
        em.op('vector', lambda e: e.tensor_tensor(
            minacc[:, 1:2], minacc[:, 1:2], minacc[:, 3:4], ALU.max))
        em.wait('vector', 'outo', cdo)
        em.wait('vector', 'outs', cds)

    cfin = em.op('vector', lambda e: e.tensor_copy(
        minacc[:, 0:2], minacc[:, 0:2]), inc='ve')
    em.wait('gpsimd', 've', cfin)
    em.dma('gpsimd', MINP, minacc[:, 0:1], inc='outm')
    em.dma('gpsimd', MAXP, minacc[:, 1:2], inc='outm')
    em.wait('gpsimd', 'outm')
    em.wait('gpsimd', 'outx')



def build_launchB():
    nc = bass.Bass('TRN2', target_bir_lowering=False)
    io = declare_io_B(nc)
    stack = contextlib.ExitStack()
    with stack:
        em = Em(nc, stack)
        emit_B(nc, em, stack, io)
        em.finalize()
    return nc


def host_prep_launchB(inputs, qts, aparts):
    X = np.asarray(inputs['L0_attn_X'])[0]
    W_out = np.ascontiguousarray(np.asarray(inputs['L0_attn_W_out']))
    b_out = np.asarray(inputs['L0_attn_b_out']).reshape(1, D)
    gb2 = np.stack([np.asarray(inputs['L0_ffn_ln_gamma']),
                    np.asarray(inputs['L0_ffn_ln_beta'])], axis=0)
    slopes = np.ascontiguousarray(
        np.asarray(inputs['L0_ffn_spline_slopes']).reshape(NUM_TILES,
                                                           D_SLICE * GRID))
    ones = np.ones((128, 128), np.float32)
    ident = np.eye(128, dtype=np.float32)
    cvals = np.array([[EPS, -0.5, 0, 0, 0, 0, 0, 0]], np.float32)
    tab = np.zeros((3, 64), np.float32)
    tab[0] = np.arange(64)
    tab[1] = np.arange(64) // 8
    tab[2, 0:8] = np.arange(8)
    ap = np.stack(aparts, axis=0).reshape(NCORES, D)
    in_maps = []
    for c in range(NCORES):
        in_maps.append(dict(
            x0=np.ascontiguousarray(X[c * TPC:(c + 1) * TPC]),
            qt=qts[c], aparts=ap, wout=W_out, bout=b_out, gb2=gb2,
            slopes=slopes, ones=ones, ident=ident, cvals=cvals, tab=tab))
    return in_maps


def declare_io_C(nc, internal=()):
    io = {}
    io['x1'] = nc.dram_tensor('x1', [TPC, D], F32, kind=None if 'x1' in internal else 'ExternalInput').ap()
    io['oh'] = nc.dram_tensor('oh', [TPC, 64], F32, kind=None if 'oh' in internal else 'ExternalInput').ap()
    io['xsl'] = nc.dram_tensor('xsl', [TPC, 32], F32, kind=None if 'xsl' in internal else 'ExternalInput').ap()
    io['mins'] = nc.dram_tensor('mins', [64, NCORES], F32, kind=None if 'mins' in internal else 'ExternalInput').ap()
    io['maxs'] = nc.dram_tensor('maxs', [64, NCORES], F32, kind=None if 'maxs' in internal else 'ExternalInput').ap()
    io['bases'] = nc.dram_tensor('bases', [64, 512], F32, kind=None if 'bases' in internal else 'ExternalInput').ap()
    io['slopes'] = nc.dram_tensor('slopes', [64, 512], F32, kind=None if 'slopes' in internal else 'ExternalInput').ap()
    io['osc'] = nc.dram_tensor('osc', [64, 1], F32, kind=None if 'osc' in internal else 'ExternalInput').ap()
    io['gb1'] = nc.dram_tensor('gb1', [2, D], F32, kind=None if 'gb1' in internal else 'ExternalInput').ap()
    io['ident'] = nc.dram_tensor('ident', [128, 128], F32, kind=None if 'ident' in internal else 'ExternalInput').ap()
    io['cvals'] = nc.dram_tensor('cvals', [1, 8], F32, kind=None if 'cvals' in internal else 'ExternalInput').ap()
    io['x2'] = nc.dram_tensor('x2', [TPC, D], F32, kind=None if 'x2' in internal else 'ExternalOutput').ap()
    io['ut'] = nc.dram_tensor('ut', [D, TPC], BF16, kind=None if 'ut' in internal else 'ExternalOutput').ap()
    return io


def emit_C(nc, em, stack, io):
    X1 = io['x1']
    OH = io['oh']
    XSL = io['xsl']
    MINS = io.get('mins')
    MAXS = io.get('maxs')
    BASES = io['bases']
    SLOPES = io['slopes']
    OSC = io['osc']
    GB1 = io['gb1']
    IDENT = io['ident']
    CVALS = io['cvals']
    X2 = io['x2']
    UT = io['ut']
    sb = lambda name, shp, dt: stack.enter_context(
        nc.sbuf_tensor(name, shp, dt, allow_name_mangling=True))
    xtb = sb('xtb', [128, D], F32)
    t1tb = sb('t1tb', [128, D], F32)
    sqtb = sb('sqtb', [128, D], F32)
    stt = sb('stt', [128, 16], F32)
    ident_sb = sb('ident_sb', [128, 128], F32)
    cv_sb = sb('cv_sb', [128, 8], F32)
    gb_sb = sb('gb_sb', [128, 2 * NKB], F32)
    rhs = sb('rhs', [64, 1048], F32)
    ohtb = sb('ohtb', [128, 64], F32)
    ohT = sb('ohT', [64, 128], F32)
    xsl_sb = sb('xsl_sb', [128, 32], F32)
    btok = sb('btok', [128, 512], F32)
    stok = sb('stok', [128, 512], F32)
    mm3 = sb('mm3', [128, 3], F32)
    u_sb = sb('u_sb', [128, 32], F32)
    gi_sb = sb('gi_sb', [128, 32], F32)
    xl_sb = sb('xl_sb', [128, 32], F32)
    f_sb = sb('f_sb', [128, 512], F32)
    sel_sb = sb('sel_sb', [128, 32], F32)
    tmp32 = sb('tmp32', [128, 32], F32)
    dsb = sb('dsb', [128, 32], F32)
    utst = [sb('utst0', [128, 128], BF16), sb('utst1', [128, 128], BF16)]
    utgate = [0, 0]
    psm = stack.enter_context(nc.psum_tensor('psm', [64, 128], F32, allow_name_mangling=True))
    psg = stack.enter_context(nc.psum_tensor('psg', [128, 512], F32, allow_name_mangling=True))
    psh = stack.enter_context(nc.psum_tensor('psh', [128, 512], F32, allow_name_mangling=True))
    ps3 = stack.enter_context(nc.psum_tensor('ps3', [128, 3], F32, allow_name_mangling=True))
    ppt = PP(em, stack, nc, 'pst', [128, 128])

    for s in ['ldc', 'ldx', 'ldo', 'lds', 'outx', 'outu', 'sc', 've',
              'te']:
        em.sem(s)

    em.setup_consts([EPS, -0.5], cv_sb, CVALS)
    em.dma('sync', ident_sb[:], IDENT, inc='ldc')
    for i in range(2):
        for kb in range(NKB):
            em.dma('sync', gb_sb[:, i * NKB + kb:i * NKB + kb + 1],
                   GB1[i:i + 1, kb * 128:(kb + 1) * 128].rearrange(
                       'a p -> p a'), inc='ldc')
    em.dma('sync', rhs[:, 0:512], BASES, inc='ldc')
    em.dma('sync', rhs[:, 512:1024], SLOPES, inc='ldc')
    em.dma('sync', rhs[:, 1024:1025], OSC, inc='ldc')
    if 'minr' in io:
        em.dma('sync', rhs[:, 1025:1026], io['minr'], inc='ldc')
        em.dma('sync', rhs[:, 1026:1027], io['maxr'], inc='ldc')
        em.wait('vector', 'ldc')
        em.wait('scalar', 'ldc')
        em.wait('tensor', 'ldc')
        cmm = em.op('vector', lambda e: e.tensor_copy(
            rhs[:, 1025:1027], rhs[:, 1025:1027]), inc='ve')
    else:
        em.dma('sync', rhs[:, 1027:1027 + NCORES], MINS, inc='ldc')
        em.dma('sync', rhs[:, 1027 + NCORES:1027 + 2 * NCORES], MAXS,
               inc='ldc')
        em.wait('vector', 'ldc')
        em.wait('scalar', 'ldc')
        em.wait('tensor', 'ldc')
        em.op('vector', lambda e: e.tensor_reduce(
            rhs[:, 1025:1026], rhs[:, 1027:1027 + NCORES], AX.X, ALU.min))
        cmm = em.op('vector', lambda e: e.tensor_reduce(
            rhs[:, 1026:1027], rhs[:, 1027 + NCORES:1027 + 2 * NCORES],
            AX.X, ALU.max), inc='ve')
    em.wait('tensor', 've', cmm)

    for tb in range(NTB):
        r0, r1 = tb * 128, (tb + 1) * 128
        em.wait('sync', 'sc')
        em.wait('sync', 've')
        cx = em.dma('sync', xtb[:], X1[r0:r1, :], inc='ldx')
        co = em.dma('sync', ohtb[:], OH[r0:r1, :], inc='ldo')
        cs = em.dma('sync', xsl_sb[:], XSL[r0:r1, :], inc='lds')
        em.wait('scalar', 'ldx', cx)
        em.wait('vector', 'ldx', cx)
        emit_plain_ln(em, xtb[:], t1tb[:], stt, sqtb[:])
        em.wait('tensor', 'ldo', co)
        em.wait('tensor', 'sc')
        ct = em.op('tensor', lambda e: e.transpose(
            psm[:], ohtb[:, 0:64], ident_sb[:]), inc='te')
        em.wait('scalar', 'te', ct)
        cev = em.op('scalar', lambda e: e.activation(
            ohT[:], psm[:], AF.Copy), inc='sc')
        em.wait('tensor', 'sc', cev)
        cm1 = em.op('tensor', lambda e: e.matmul(
            psg[:], ohT[:], rhs[:, 0:512], start=True, stop=True),
            inc='te')
        cm2 = em.op('tensor', lambda e: e.matmul(
            psh[:], ohT[:], rhs[:, 512:1024], start=True, stop=True),
            inc='te')
        cm3 = em.op('tensor', lambda e: e.matmul(
            ps3[:], ohT[:], rhs[:, 1024:1027], start=True, stop=True),
            inc='te')
        em.wait('scalar', 'te', cm3)
        em.op('scalar', lambda e: e.activation(btok[:], psg[:], AF.Copy))
        em.op('scalar', lambda e: e.activation(stok[:], psh[:], AF.Copy))
        cev = em.op('scalar', lambda e: e.activation(
            mm3[:], ps3[:], AF.Copy), inc='sc')
        em.wait('vector', 'sc', cev)
        em.wait('vector', 'lds', cs)
        em.op('vector', lambda e: e.tensor_scalar(
            stt[:, 8:9], mm3[:, 1:2], -1.0, None, ALU.mult))
        em.op('vector', lambda e: e.scalar_tensor_tensor(
            stt[:, 9:10], mm3[:, 1:2], -1.0, mm3[:, 2:3], ALU.mult,
            ALU.add))
        em.op('vector', lambda e: e.tensor_scalar(
            stt[:, 9:10], stt[:, 9:10], 1e-8, None, ALU.add))
        em.op('vector', lambda e: e.reciprocal(
            stt[:, 10:11], stt[:, 9:10]))
        em.op('vector', lambda e: e.tensor_scalar(
            u_sb[:], xsl_sb[:], stt[:, 8:9], stt[:, 10:11], ALU.add,
            ALU.mult))
        em.op('vector', lambda e: e.tensor_scalar(
            u_sb[:], u_sb[:], 0.0, 1.0 - 1e-06, ALU.max, ALU.min))
        em.op('vector', lambda e: e.tensor_scalar(
            u_sb[:], u_sb[:], float(GRID), None, ALU.mult))
        em.op('vector', lambda e: e.tensor_scalar(
            gi_sb[:], u_sb[:], 1.0, None, ALU.is_ge))
        for kk in range(2, GRID):
            em.op('vector', lambda e, kk=kk: e.scalar_tensor_tensor(
                gi_sb[:], u_sb[:], float(kk), gi_sb[:], ALU.is_ge,
                ALU.add))
        em.op('vector', lambda e: e.tensor_tensor(
            xl_sb[:], u_sb[:], gi_sb[:], ALU.subtract))
        xlb = xl_sb[:].unsqueeze(2).broadcast_to([128, 32, GRID])
        em.op('vector', lambda e, xlb=xlb: e.tensor_tensor(
            f_sb[:].rearrange('p (j g) -> p j g', g=GRID),
            stok[:].rearrange('p (j g) -> p j g', g=GRID), xlb, ALU.mult))
        em.op('vector', lambda e: e.tensor_tensor(
            f_sb[:], f_sb[:], btok[:], ALU.add))
        em.op('vector', lambda e: e.memset(dsb[:], 0.0))
        for g in range(GRID):
            em.op('vector', lambda e, g=g: e.tensor_scalar(
                sel_sb[:], gi_sb[:], float(g), None, ALU.is_equal))
            fsl = f_sb[:].rearrange('p (j g) -> p g j', g=GRID)[:, g, :]
            em.op('vector', lambda e, fsl=fsl: e.tensor_tensor(
                tmp32[:], sel_sb[:], fsl, ALU.mult))
            em.op('vector', lambda e: e.tensor_tensor(
                dsb[:], dsb[:], tmp32[:], ALU.add))
        em.op('vector', lambda e: e.tensor_scalar(
            dsb[:], dsb[:], mm3[:, 0:1], None, ALU.mult))
        em.wait('vector', 'ldo', co)
        em.wait('vector', 'sc')
        for i in range(NUM_TILES):
            em.op('vector', lambda e, i=i: e.scalar_tensor_tensor(
                t1tb[:, i * D_SLICE:(i + 1) * D_SLICE], dsb[:],
                ohtb[:, i:i + 1], t1tb[:, i * D_SLICE:(i + 1) * D_SLICE],
                ALU.mult, ALU.add))
        emit_plain_ln(em, t1tb[:], t1tb[:], stt, sqtb[:])
        em.wait('vector', 'sc')
        cv = em.op('vector', lambda e: e.tensor_tensor(
            xtb[:], xtb[:], t1tb[:], ALU.add), inc='ve')
        em.wait('gpsimd', 've', cv)
        cdx = em.op('gpsimd', lambda e, r0=r0, r1=r1: e.dma_start(
            out=X2[r0:r1, :], in_=xtb[:]), inc='outx', inc_by=16)
        em.wait('scalar', 've', cv)
        c5 = emit_plain_ln(em, xtb[:], t1tb[:], stt, sqtb[:])
        em.wait('tensor', 'sc', c5)
        for kb in range(NKB):
            pstk = ppt.nxt(em)
            ct = em.op('tensor', lambda e, kb=kb, pstk=pstk: e.transpose(
                pstk[:], t1tb[:, kb * 128:(kb + 1) * 128], ident_sb[:]),
                inc='te')
            em.wait('scalar', 'te', ct)
            kp = kb % 2
            em.wait('scalar', 'outu', utgate[kp])
            cu = em.op('scalar', lambda e, kb=kb, pstk=pstk, kp=kp:
                       e.activation(
                utst[kp][:], pstk[:], AF.Identity,
                scale=gb_sb[:, kb:kb + 1],
                bias=gb_sb[:, NKB + kb:NKB + kb + 1]), inc='sc')
            ppt.done('sc', cu)
            em.wait('gpsimd', 'sc', cu)
            cdu = em.op('gpsimd', lambda e, kb=kb, r0=r0, kp=kp: e.dma_start(
                out=UT[kb * 128:(kb + 1) * 128, r0:r0 + 128],
                in_=utst[kp][:]), inc='outu', inc_by=16)
            utgate[kp] = cdu
        em.wait('sync', 'outx', cdx)

    em.wait('gpsimd', 'outx')
    em.wait('gpsimd', 'outu')



def build_launchC():
    nc = bass.Bass('TRN2', target_bir_lowering=False)
    io = declare_io_C(nc)
    stack = contextlib.ExitStack()
    with stack:
        em = Em(nc, stack)
        emit_C(nc, em, stack, io)
        em.finalize()
    return nc


def host_prep_launchC(inputs, x1s, ohs, xsls, minps, maxps):
    bases = np.ascontiguousarray(
        np.asarray(inputs['L0_ffn_spline_bases']).reshape(64, 512))
    slopes = np.ascontiguousarray(
        np.asarray(inputs['L0_ffn_spline_slopes']).reshape(64, 512))
    osc = np.asarray(inputs['L0_ffn_output_scale']).reshape(64, 1)
    gb1 = np.stack([np.asarray(inputs['L1_attn_ln1_w']),
                    np.asarray(inputs['L1_attn_ln1_b'])], axis=0)
    mins = np.ascontiguousarray(np.stack([m[:, 0] for m in minps], axis=1))
    maxs = np.ascontiguousarray(np.stack([m[:, 0] for m in maxps], axis=1))
    ident = np.eye(128, dtype=np.float32)
    cvals = np.array([[EPS, -0.5, 0, 0, 0, 0, 0, 0]], np.float32)
    in_maps = []
    for c in range(NCORES):
        in_maps.append(dict(
            x1=x1s[c], oh=ohs[c], xsl=xsls[c], mins=mins, maxs=maxs,
            bases=bases, slopes=slopes, osc=osc, gb1=gb1, ident=ident,
            cvals=cvals))
    return in_maps


TCH = 1024
NCH = T // TCH

def declare_io_D(nc, internal=()):
    io = {}
    io['utall'] = nc.dram_tensor('utall', [NCORES, D, TPC], BF16, kind=None if 'utall' in internal else 'ExternalInput').ap()
    io['wxmt'] = nc.dram_tensor('wxmt', [D, DPC], F32, kind=None if 'wxmt' in internal else 'ExternalInput').ap()
    io['wzt'] = nc.dram_tensor('wzt', [D, DPC], F32, kind=None if 'wzt' in internal else 'ExternalInput').ap()
    io['cw'] = nc.dram_tensor('cw', [4, 128, D_CONV], F32, kind=None if 'cw' in internal else 'ExternalInput').ap()
    io['cb'] = nc.dram_tensor('cb', [4, 128], F32, kind=None if 'cb' in internal else 'ExternalInput').ap()
    io['xpt'] = nc.dram_tensor('xpt', [DPC, 160], F32, kind=None if 'xpt' in internal else 'ExternalInput').ap()
    io['dtpt'] = nc.dram_tensor('dtpt', [DT_RANK, DPC], F32, kind=None if 'dtpt' in internal else 'ExternalInput').ap()
    io['dtb'] = nc.dram_tensor('dtb', [4, 128], F32, kind=None if 'dtb' in internal else 'ExternalInput').ap()
    io['alog'] = nc.dram_tensor('alog', [4, 128, D_STATE], F32, kind=None if 'alog' in internal else 'ExternalInput').ap()
    io['dsk'] = nc.dram_tensor('dsk', [4, 128], F32, kind=None if 'dsk' in internal else 'ExternalInput').ap()
    io['cvals'] = nc.dram_tensor('cvals', [1, 8], F32, kind=None if 'cvals' in internal else 'ExternalInput').ap()
    io['g'] = nc.dram_tensor('g', [NCORES, DPC, TPC], BF16, kind=None if 'g' in internal else 'ExternalOutput').ap()
    io['xcst'] = nc.dram_tensor('xcst', [DPC, T], F32).ap()
    io['zst'] = nc.dram_tensor('zst', [DPC, T], F32).ap()
    io['xdblp'] = nc.dram_tensor('xdblp', [160, T], F32).ap()
    io['xdbl'] = nc.dram_tensor('xdbl', [160, T], F32).ap()
    return io


def emit_D(nc, em, stack, io):
    XC = io['xcst']
    Z = io['zst']
    UTALL = io['utall']
    WXMT = io['wxmt']
    WZT = io['wzt']
    CW = io['cw']
    CB = io['cb']
    XPT = io['xpt']
    DTPT = io['dtpt']
    DTB = io['dtb']
    ALOG = io['alog']
    DSK = io['dsk']
    CVALS = io['cvals']
    G = io['g']
    XCST = io['xcst']
    ZST = io['zst']
    XDBLP = io['xdblp']
    XDBL = io['xdbl']
    sb = lambda name, shp, dt: stack.enter_context(
        nc.sbuf_tensor(name, shp, dt, allow_name_mangling=True))
    wxm = sb('wxm', [128, NKB * DPC], BF16)
    wz = sb('wz', [128, NKB * DPC], BF16)
    xpt_bf = sb('xpt_bf', [128, 4 * 160], BF16)
    dtpt_bf = sb('dtpt_bf', [128, DPC], BF16)
    wld = sb('wld', [128, DPC], F32)
    utch = sb('utch', [128, NKB * 512], BF16)
    ext = sb('ext', [128, 4 * 515], F32)
    xcsb = sb('xcsb', [128, 512], F32)
    xcbf = sb('xcbf', [128, 512], BF16)
    zsb = sb('zsb', [128, 512], F32)
    xdsb0 = sb('xdsb0', [128, 512], F32)
    xdsb1 = sb('xdsb1', [32, 512], F32)
    cw_sb = sb('cw_sb', [128, 16], F32)
    cb_sb = sb('cb_sb', [128, 4], F32)
    dtb_sb = sb('dtb_sb', [128, 4], F32)
    dsk_sb = sb('dsk_sb', [128, 4], F32)
    a_sb4 = sb('a_sb4', [128, 4 * D_STATE], F32)
    st_sb = sb('st_sb', [128, 4 * D_STATE], F32)
    cv_sb = sb('cv_sb', [128, 8], F32)
    dt_sb = sb('dt_sb', [128, 4 * TCH], F32)
    dtx_sb = sb('dtx_sb', [128, 4 * TCH], F32)
    xcch = sb('xcch', [128, 4 * TCH], F32)
    yacc = sb('yacc', [128, 4 * TCH], F32)
    asb = [sb('asb0', [128, TCH], F32), sb('asb1', [128, TCH], F32)]
    usb = sb('usb', [128, TCH], F32)
    wsb = [sb('wsb0', [128, TCH], F32), sb('wsb1', [128, TCH], F32)]
    hsb = sb('hsb', [128, TCH], F32)
    brB = [sb(f'brB{i}', [128, TCH], F32) for i in range(2)]
    brC = [sb(f'brC{i}', [128, TCH], F32) for i in range(2)]
    xdbb = sb('xdbb', [128, TCH], BF16)
    zch = sb('zch', [128, TCH], F32)
    gb16 = sb('gb16', [128, TCH], BF16)
    sp1 = sb('sp1', [128, 512], F32)
    sp2 = sb('sp2', [128, 512], F32)
    ppx = PP(em, stack, nc, 'psx', [128, 512])
    ppz = PP(em, stack, nc, 'psz', [128, 512])
    psd0 = stack.enter_context(nc.psum_tensor('psd0', [128, 512], F32, allow_name_mangling=True))
    psd1 = stack.enter_context(nc.psum_tensor('psd1', [32, 512], F32, allow_name_mangling=True))
    psdt = stack.enter_context(nc.psum_tensor('psdt', [128, 512], F32, allow_name_mangling=True))

    for s in ['ldc', 'ldw', 'ldu', 'ldxd', 'ldxc', 'ldz', 'ldbr0',
              'ldbr1', 'outxc', 'outz', 'outxd', 'outg', 'cc', 'bar',
              'po', 'sc', 've', 'te']:
        em.sem(s)

    em.setup_consts([EPS, -0.5, 1.0, -1.0], cv_sb, CVALS)
    for pb in range(4):
        for t_, srcap in [(cw_sb, None), ]:
            pass
    for pb in range(4):
        em.dma('sync', cw_sb[:, pb * 4:(pb + 1) * 4], CW[pb], inc='ldc')
        em.dma('sync', cb_sb[:, pb:pb + 1],
               CB[pb:pb + 1, :].rearrange('a p -> p a'), inc='ldc')
        em.dma('sync', dtb_sb[:, pb:pb + 1],
               DTB[pb:pb + 1, :].rearrange('a p -> p a'), inc='ldc')
        em.dma('sync', dsk_sb[:, pb:pb + 1],
               DSK[pb:pb + 1, :].rearrange('a p -> p a'), inc='ldc')
        em.dma('sync', a_sb4[:, pb * D_STATE:(pb + 1) * D_STATE],
               ALOG[pb], inc='ldc')
    em.wait('scalar', 'ldc')
    em.wait('vector', 'ldc')
    em.op('scalar', lambda e: e.activation(a_sb4[:], a_sb4[:], AF.Exp))
    em.wait('vector', 'sc')
    em.op('vector', lambda e: e.tensor_scalar(a_sb4[:], a_sb4[:], -1.0,
                                              None, ALU.mult))
    em.op('vector', lambda e: e.memset(st_sb[:], 0.0))
    em.op('vector', lambda e: e.memset(ext[:], 0.0))
    def loadw(dst, W, ncols):
        for kb in range(W.shape[0] // 128):
            c = em.dma('sync', wld[:, 0:ncols], W[kb * 128:(kb + 1) * 128, :],
                       inc='ldw')
            em.wait('scalar', 'ldw', c)
            c2 = em.op('scalar', lambda e, kb=kb, dst=dst, ncols=ncols:
                       e.activation(dst[:, kb * ncols:(kb + 1) * ncols],
                                    wld[:, 0:ncols], AF.Copy), inc='sc')
            em.wait('sync', 'sc', c2)
    loadw(wxm, WXMT, DPC)
    loadw(wz, WZT, DPC)
    loadw(xpt_bf, XPT, 160)
    loadw(dtpt_bf, DTPT, DPC)
    em.wait('tensor', 'sc')

    em.wait('scalar', 've')
    em.wait('tensor', 've')
    for tch in range(T // 512):
        j, w0 = tch // 4, (tch % 4) * 512
        for kb in range(NKB):
            em.dma('sync', utch[:, kb * 512:(kb + 1) * 512],
                   UTALL[j, kb * 128:(kb + 1) * 128, w0:w0 + 512],
                   inc='ldu')
        cu = em.cnt['ldu']
        em.wait('tensor', 'ldu', cu)
        for pb in range(4):
            parts = [(wxm[:, kb * DPC + pb * 128:kb * DPC + (pb + 1) * 128],
                      utch[:, kb * 512:(kb + 1) * 512])
                     for kb in range(NKB)]
            psx = ppx.nxt(em)
            cm = mm_accum(em, psx[:], parts, inc='te')
            em.wait('scalar', 'te', cm)
            cev = em.op('scalar', lambda e, pb=pb, psx=psx: e.activation(
                ext[:, pb * 515 + 3:(pb + 1) * 515], psx[:], AF.Copy),
                inc='sc')
            ppx.done('sc', cev)
            em.wait('vector', 'sc', cev)
            em.op('vector', lambda e, pb=pb: e.tensor_scalar(
                xcsb[:], ext[:, pb * 515:pb * 515 + 512],
                cw_sb[:, pb * 4:pb * 4 + 1], None, ALU.mult))
            for k in range(1, 4):
                em.op('vector', lambda e, pb=pb, k=k: e.scalar_tensor_tensor(
                    xcsb[:], ext[:, pb * 515 + k:pb * 515 + k + 512],
                    cw_sb[:, pb * 4 + k:pb * 4 + k + 1], xcsb[:],
                    ALU.mult, ALU.add))
            cvt = em.op('vector', lambda e, pb=pb: e.tensor_copy(
                ext[:, pb * 515:pb * 515 + 3],
                ext[:, pb * 515 + 512:pb * 515 + 515]), inc='ve')
            em.wait('scalar', 've', cvt)
            csl = em.op('scalar', lambda e, pb=pb: e.activation(
                xcsb[:], xcsb[:], AF.Silu, bias=cb_sb[:, pb:pb + 1]),
                inc='sc')
            em.wait('gpsimd', 'sc', csl)
            cdx = em.op('gpsimd', lambda e, pb=pb, tch=tch: e.dma_start(
                out=XC[pb * 128:(pb + 1) * 128, tch * 512:(tch + 1) * 512],
                in_=xcsb[:]), inc='outxc', inc_by=16)
            cbf = em.op('scalar', lambda e: e.activation(
                xcbf[:], xcsb[:], AF.Copy), inc='sc')
            em.wait('tensor', 'sc', cbf)
            em.op('tensor', lambda e, pb=pb: e.matmul(
                psd0[:], xpt_bf[:, pb * 160:pb * 160 + 128], xcbf[:],
                start=(pb == 0), stop=(pb == 3), skip_group_check=True))
            cm2 = em.op('tensor', lambda e, pb=pb: e.matmul(
                psd1[:], xpt_bf[:, pb * 160 + 128:pb * 160 + 160], xcbf[:],
                start=(pb == 0), stop=(pb == 3), skip_group_check=True),
                inc='te')
            em.wait('scalar', 'outxc', cdx)
            em.wait('vector', 'outxc', cdx)
            parts = [(wz[:, kb * DPC + pb * 128:kb * DPC + (pb + 1) * 128],
                      utch[:, kb * 512:(kb + 1) * 512])
                     for kb in range(NKB)]
            psz = ppz.nxt(em)
            cmz = mm_accum(em, psz[:], parts, inc='te')
            em.wait('scalar', 'te', cmz)
            cez = em.op('scalar', lambda e, psz=psz: e.activation(
                zsb[:], psz[:], AF.Copy), inc='sc')
            ppz.done('sc', cez)
            em.wait('gpsimd', 'sc', cez)
            cdz = em.op('gpsimd', lambda e, pb=pb, tch=tch: e.dma_start(
                out=Z[pb * 128:(pb + 1) * 128, tch * 512:(tch + 1) * 512],
                in_=zsb[:]), inc='outz', inc_by=16)
            em.wait('scalar', 'outz', cdz)
        em.wait('scalar', 'te', cm2)
        em.op('scalar', lambda e: e.activation(
            xdsb0[:], psd0[:], AF.Copy))
        cxd = em.op('scalar', lambda e: e.activation(
            xdsb1[:], psd1[:], AF.Copy), inc='sc')
        em.wait('tensor', 'sc', cxd)
        em.wait('gpsimd', 'sc', cxd)
        em.op('gpsimd', lambda e, tch=tch: e.dma_start(
            out=XDBLP[0:128, tch * 512:(tch + 1) * 512], in_=xdsb0[:]),
            inc='outxd', inc_by=16)
        cdd = em.op('gpsimd', lambda e, tch=tch: e.dma_start(
            out=XDBLP[128:160, tch * 512:(tch + 1) * 512], in_=xdsb1[:]),
            inc='outxd', inc_by=16)
        em.wait('scalar', 'outxd', cdd)
        em.wait('sync', 'te')

    em.wait('gpsimd', 'outxd')
    em.coll('AllReduce', ALU.add, [XDBLP.opt()], [XDBL.opt()], 'cc')
    em.wait('gpsimd', 'cc', 1)
    em.op('gpsimd', lambda e: e.engine_nop(), inc='bar', inc_by=1)
    cgate = em.cnt['bar']
    em.wait('sync', 'bar', cgate)
    em.wait('scalar', 'bar', cgate)
    em.wait('vector', 'bar', cgate)
    em.wait('tensor', 'bar', cgate)

    for ch in range(NCH):
        c0 = ch * TCH
        cxd = em.dma('sync', sp1[:, 0:512], XDBL[0:128, c0:c0 + 512],
                     inc='ldxd')
        for half in range(2):
            if half == 1:
                cxd = em.dma('sync', sp1[:, 0:512],
                             XDBL[0:128, c0 + 512:c0 + 1024], inc='ldxd')
            em.wait('scalar', 'ldxd', cxd)
            ch2 = em.op('scalar', lambda e, half=half: e.activation(
                xdbb[:, half * 512:(half + 1) * 512], sp1[:, 0:512],
                AF.Copy), inc='sc')
            em.wait('sync', 'sc', ch2)
        em.wait('tensor', 'sc')
        em.wait('sync', 've')
        for pb in range(4):
            em.dma('sync', xcch[:, pb * TCH:(pb + 1) * TCH],
                   XC[pb * 128:(pb + 1) * 128, c0:c0 + TCH], inc='ldxc')
        cxcall = em.cnt['ldxc']
        for pb in range(4):
            for tsub in range(TCH // 512):
                cm = em.op('tensor', lambda e, pb=pb, tsub=tsub: e.matmul(
                    psdt[:], dtpt_bf[:, pb * 128:(pb + 1) * 128],
                    xdbb[:, tsub * 512:(tsub + 1) * 512],
                    start=True, stop=True), inc='te')
                em.wait('scalar', 'te', cm)
                em.wait('scalar', 've')
                em.op('scalar', lambda e, pb=pb: e.activation(
                    sp1[:, 0:512], psdt[:], AF.Identity,
                    bias=dtb_sb[:, pb:pb + 1]))
                em.op('scalar', lambda e: e.activation(
                    sp2[:, 0:512], sp1[:, 0:512], AF.Abs))
                em.op('scalar', lambda e: e.activation(
                    sp2[:, 0:512], sp2[:, 0:512], AF.Exp, scale=-1.0))
                em.op('scalar', lambda e: e.activation(
                    sp2[:, 0:512], sp2[:, 0:512], AF.Ln, bias=1.0))
                csr = em.op('scalar', lambda e: e.activation(
                    sp1[:, 0:512], sp1[:, 0:512], AF.Relu), inc='sc')
                em.wait('tensor', 'sc', csr)
                em.wait('vector', 'sc', csr)
                cdt = em.op('vector', lambda e, pb=pb, tsub=tsub:
                            e.tensor_tensor(
                    dt_sb[:, pb * TCH + tsub * 512:pb * TCH + (tsub + 1) * 512],
                    sp1[:, 0:512], sp2[:, 0:512], ALU.add), inc='ve')
            em.wait('vector', 'ldxc', cxcall)
            em.op('vector', lambda e, pb=pb: e.tensor_tensor(
                dtx_sb[:, pb * TCH:(pb + 1) * TCH],
                dt_sb[:, pb * TCH:(pb + 1) * TCH],
                xcch[:, pb * TCH:(pb + 1) * TCH], ALU.mult))
        em.wait('scalar', 've')
        asb_gate = [0, 0]
        brep_gate = [0, 0]
        wgate2 = [0, 0]
        for s in range(D_STATE):
            par = s % 2
            em.wait('sync', 've', brep_gate[par])
            cbr = em.dma('sync', brB[par][:],
                         XDBL[128 + s:129 + s, c0:c0 + TCH]
                         .partition_broadcast(128).squeeze(1),
                         inc=f'ldbr{par}')
            ccr = em.dma('sync', brC[par][:],
                         XDBL[144 + s:145 + s, c0:c0 + TCH]
                         .partition_broadcast(128).squeeze(1),
                         inc=f'ldbr{par}')
            em.wait('vector', f'ldbr{par}', ccr)
            for pb in range(4):
                col = pb * D_STATE + s
                it = s * 4 + pb
                ap2 = it % 2
                em.wait('scalar', 've', asb_gate[ap2])
                ca = em.op('scalar', lambda e, pb=pb, col=col, ap2=ap2:
                           e.activation(
                    asb[ap2][:], dt_sb[:, pb * TCH:(pb + 1) * TCH], AF.Exp,
                    scale=a_sb4[:, col:col + 1]), inc='sc')
                em.op('vector', lambda e, pb=pb, par=par: e.tensor_tensor(
                    usb[:], dtx_sb[:, pb * TCH:(pb + 1) * TCH],
                    brB[par][:], ALU.mult))
                em.wait('vector', 'sc', ca)
                em.op('vector', lambda e, col=col, ap2=ap2:
                      e.tensor_tensor_scan(
                    hsb[:], asb[ap2][:], usb[:], st_sb[:, col:col + 1],
                    ALU.mult, ALU.add))
                asb_gate[ap2] = em.cnt['ve']
                em.op('vector', lambda e, col=col: e.tensor_copy(
                    st_sb[:, col:col + 1], hsb[:, TCH - 1:TCH]))
                em.op('vector', lambda e, par=par: e.tensor_tensor(
                    usb[:], hsb[:], brC[par][:], ALU.mult))
                sfirst = (s == 0)
                if sfirst:
                    cy = em.op('vector', lambda e, pb=pb: e.tensor_copy(
                        yacc[:, pb * TCH:(pb + 1) * TCH], usb[:]),
                        inc='ve')
                else:
                    cy = em.op('vector', lambda e, pb=pb: e.tensor_tensor(
                        yacc[:, pb * TCH:(pb + 1) * TCH],
                        yacc[:, pb * TCH:(pb + 1) * TCH], usb[:], ALU.add),
                        inc='ve')
            brep_gate[par] = em.cnt['ve']
        for pb in range(4):
            cz = em.dma('sync', zch[:], Z[pb * 128:(pb + 1) * 128,
                                          c0:c0 + TCH], inc='ldz')
            em.wait('scalar', 'ldz', cz)
            csz = em.op('scalar', lambda e: e.activation(
                zch[:], zch[:], AF.Silu), inc='sc')
            em.wait('vector', 'sc', csz)
            em.op('vector', lambda e, pb=pb: e.tensor_scalar(
                usb[:], xcch[:, pb * TCH:(pb + 1) * TCH],
                dsk_sb[:, pb:pb + 1], None, ALU.mult))
            em.op('vector', lambda e, pb=pb: e.tensor_tensor(
                usb[:], usb[:], yacc[:, pb * TCH:(pb + 1) * TCH], ALU.add))
            cg = em.op('vector', lambda e: e.tensor_tensor(
                usb[:], usb[:], zch[:], ALU.mult), inc='ve')
            em.wait('scalar', 've', cg)
            cgb = em.op('scalar', lambda e: e.activation(
                gb16[:], usb[:], AF.Copy), inc='sc')
            em.wait('gpsimd', 'sc', cgb)
            cdg = em.op('gpsimd', lambda e, ch=ch, pb=pb: e.dma_start(
                out=G[ch // 2, pb * 128:(pb + 1) * 128,
                      (ch % 2) * TCH:(ch % 2 + 1) * TCH], in_=gb16[:]),
                inc='outg', inc_by=16)
            em.wait('scalar', 'outg', cdg)
            em.wait('sync', 've')
    em.wait('gpsimd', 'outg')



def build_launchD():
    nc = bass.Bass('TRN2', target_bir_lowering=False)
    io = declare_io_D(nc)
    stack = contextlib.ExitStack()
    with stack:
        em = Em(nc, stack)
        emit_D(nc, em, stack, io)
        em.finalize()
    return nc


def host_prep_launchD(inputs, uts):
    utall = np.ascontiguousarray(np.stack(uts, axis=0))
    ipw = np.asarray(inputs['L1_attn_in_proj_w'])
    cw = np.asarray(inputs['L1_attn_conv_w'])[:, 0, :]
    cb = np.asarray(inputs['L1_attn_conv_b'])
    xpw = np.asarray(inputs['L1_attn_x_proj_w'])
    dpw = np.asarray(inputs['L1_attn_dt_proj_w'])
    dpb = np.asarray(inputs['L1_attn_dt_proj_b'])
    alog = np.asarray(inputs['L1_attn_A_log'])
    dsk = np.asarray(inputs['L1_attn_D_skip'])
    cvals = np.array([[EPS, -0.5, 1.0, -1.0, 0, 0, 0, 0]], np.float32)
    in_maps = []
    for c in range(NCORES):
        s0, s1 = c * DPC, (c + 1) * DPC
        in_maps.append(dict(
            utall=utall,
            wxmt=np.ascontiguousarray(ipw[s0:s1, :].T),
            wzt=np.ascontiguousarray(ipw[4096 + s0:4096 + s1, :].T),
            cw=np.ascontiguousarray(cw[s0:s1].reshape(4, 128, D_CONV)),
            cb=np.ascontiguousarray(cb[s0:s1].reshape(4, 128)),
            xpt=np.ascontiguousarray(xpw[:, s0:s1].T),
            dtpt=np.ascontiguousarray(dpw[s0:s1, :].T),
            dtb=np.ascontiguousarray(dpb[s0:s1].reshape(4, 128)),
            alog=np.ascontiguousarray(alog[s0:s1].reshape(4, 128, D_STATE)),
            dsk=np.ascontiguousarray(dsk[s0:s1].reshape(4, 128)),
            cvals=cvals))
    return in_maps


def declare_io_E(nc, internal=()):
    io = {}
    io['gfull'] = nc.dram_tensor('gfull', [D_INNER, TPC], BF16, kind=None if 'gfull' in internal else 'ExternalInput').ap()
    io['x2'] = nc.dram_tensor('x2', [TPC, D], F32, kind=None if 'x2' in internal else 'ExternalInput').ap()
    io['opt'] = nc.dram_tensor('opt', [D_INNER, D], F32, kind=None if 'opt' in internal else 'ExternalInput').ap()
    io['gb2l'] = nc.dram_tensor('gb2l', [2, D], F32, kind=None if 'gb2l' in internal else 'ExternalInput').ap()
    io['fpt'] = nc.dram_tensor('fpt', [D, HIDDEN], F32, kind=None if 'fpt' in internal else 'ExternalInput').ap()
    io['fot'] = nc.dram_tensor('fot', [D_INNER, D], F32, kind=None if 'fot' in internal else 'ExternalInput').ap()
    io['ident'] = nc.dram_tensor('ident', [128, 128], F32, kind=None if 'ident' in internal else 'ExternalInput').ap()
    io['cvals'] = nc.dram_tensor('cvals', [1, 8], F32, kind=None if 'cvals' in internal else 'ExternalInput').ap()
    io['x3'] = nc.dram_tensor('x3', [TPC, D], F32, kind=None if 'x3' in internal else 'ExternalOutput').ap()
    io['mix'] = nc.dram_tensor('mix', [TPC, D], F32).ap()
    io['ares'] = nc.dram_tensor('ares', [TPC, D], F32).ap()
    io['ffa'] = nc.dram_tensor('ffa', [D_INNER, TPC], BF16).ap()
    io['t6'] = nc.dram_tensor('t6', [TPC, D], F32).ap()

    return io


def emit_E(nc, em, stack, io):
    NH = 8
    WCH = D // NH
    GFULL = io['gfull']
    X2 = io['x2']
    OPT = io['opt']
    GB2L = io['gb2l']
    FPT = io['fpt']
    FOT = io['fot']
    IDENT = io['ident']
    CVALS = io['cvals']
    X3 = io['x3']
    MIX = io['mix']
    ARES = io['ares']
    FFA = io['ffa']
    T6 = io['t6']
    sb = lambda name, shp, dt: stack.enter_context(
        nc.sbuf_tensor(name, shp, dt, allow_name_mangling=True))
    big = sb('big', [128, 32 * TPC], BF16)
    warena = [sb('warena0', [128, 32 * WCH], BF16),
              sb('warena1', [128, 32 * WCH], BF16)]
    fa_bf = [sb('fa_bf0', [128, NKB * 128], BF16)] * 2
    fg_bf = [sb('fg_bf0', [128, NKB * 128], BF16)] * 2
    wlda = sb('wlda', [128, 16 * WCH], F32)
    wldg = sb('wldg', [128, 16 * WCH], F32)
    t1tb = wldg[:, 0:D]
    sq2 = wlda[:, 0:D]
    xtb = wlda[:, D:2 * D] if 16 * WCH >= 2 * D else None
    assert 16 * WCH >= 2 * D
    stt = sb('stt', [128, 16], F32)
    ident_sb = sb('ident_sb', [128, 128], F32)
    cv_sb = sb('cv_sb', [128, 8], F32)
    gb_sb = sb('gb_sb', [128, 2 * NKB], F32)
    sg_sb = sb('sg_sb', [128, 512], F32)
    ffat = sb('ffat', [128, 512], BF16)
    mixt = sg_sb[:, 0:WCH]
    arch = sg_sb[:, WCH:2 * WCH]
    ppy = PP(em, stack, nc, 'psy', [128, WCH])
    ppa = PP(em, stack, nc, 'psa', [128, 512])
    ppg = PP(em, stack, nc, 'psg', [128, 512])
    ppt = PP(em, stack, nc, 'pst', [128, 128])

    for s in ['ldc', 'ldg', 'ldw', 'ldwg', 'ldx', 'ldm', 'lda', 'ldf',
              'outm', 'outa', 'outf', 'outt', 'outx', 'sc', 've', 'te']:
        em.sem(s)

    em.setup_consts([EPS, -0.5], cv_sb, CVALS)
    em.dma('sync', ident_sb[:], IDENT, inc='ldc')
    for i in range(2):
        for kb in range(NKB):
            em.dma('sync', gb_sb[:, i * NKB + kb:i * NKB + kb + 1],
                   GB2L[i:i + 1, kb * 128:(kb + 1) * 128].rearrange(
                       'a p -> p a'), inc='ldc')
    em.wait('scalar', 'ldc')
    em.wait('vector', 'ldc')
    em.wait('tensor', 'ldc')

    for kb in range(32):
        em.dma('sync', big[:, kb * TPC:(kb + 1) * TPC],
               GFULL[kb * 128:(kb + 1) * 128, :], inc='ldg')
    cg = em.cnt['ldg']
    em.wait('tensor', 'ldg', cg)

    wgate = [0, 0]
    def load_chunk(W, nh, par):
        em.wait('scalar', 'te', wgate[par])
        srcw = W[:, nh * WCH:(nh + 1) * WCH].rearrange(
            '(kb p) c -> p kb c', p=128)
        half = 16 * WCH
        cwa = em.dma('sync',
                     wlda[:].rearrange('p (kb c) -> p kb c', kb=16),
                     srcw[:, 0:16, :], inc='ldw')
        cwg = em.dma('sync',
                     wldg[:].rearrange('p (kb c) -> p kb c', kb=16),
                     srcw[:, 16:32, :], inc='ldwg')
        em.wait('scalar', 'ldw', cwa)
        em.op('scalar', lambda e, par=par: e.activation(
            warena[par][:, 0:half], wlda[:], AF.Copy), inc='sc')
        em.wait('scalar', 'ldwg', cwg)
        c2 = em.op('scalar', lambda e, par=par, half=half: e.activation(
            warena[par][:, half:2 * half], wldg[:], AF.Copy), inc='sc')
        em.wait('sync', 'sc', c2)
        return em.cnt['sc']

    for nh in range(NH):
        par = nh % 2
        cwl = load_chunk(OPT, nh, par)
        em.wait('tensor', 'sc', cwl)
        for tb in range(NTB):
            parts = [(big[:, kb * TPC + tb * 128:kb * TPC + (tb + 1) * 128],
                      warena[par][:, kb * WCH:(kb + 1) * WCH])
                     for kb in range(32)]
            psy = ppy.nxt(em)
            cm = mm_accum(em, psy[:], parts, inc='te')
            if tb == NTB - 1:
                wgate[par] = cm
            em.wait('scalar', 'te', cm)
            ce = em.op('scalar', lambda e, psy=psy: e.activation(
                mixt[:], psy[:], AF.Copy), inc='sc')
            ppy.done('sc', ce)
            em.wait('gpsimd', 'sc', ce)
            cd = em.op('gpsimd', lambda e, tb=tb, nh=nh: e.dma_start(
                out=MIX[tb * 128:(tb + 1) * 128, nh * WCH:(nh + 1) * WCH],
                in_=mixt[:]), inc='outm', inc_by=16)
            em.wait('scalar', 'outm', cd)

    em.wait('scalar', 'te')
    em.wait('sync', 'te')
    for tb in range(NTB):
        cx = em.dma('sync', xtb[:], X2[tb * 128:(tb + 1) * 128, :],
                    inc='ldx')
        cmx = em.dma('sync', t1tb[:], MIX[tb * 128:(tb + 1) * 128, :],
                     inc='ldm')
        em.wait('scalar', 'ldx', cx)
        em.wait('vector', 'ldx', cx)
        emit_plain_ln(em, xtb[:], xtb[:], stt, sq2[:])
        em.wait('vector', 'sc')
        em.wait('vector', 'ldm', cmx)
        cv = em.op('vector', lambda e: e.tensor_tensor(
            xtb[:], xtb[:], t1tb[:], ALU.add), inc='ve')
        em.wait('gpsimd', 've', cv)
        cda = em.op('gpsimd', lambda e, tb=tb: e.dma_start(
            out=ARES[tb * 128:(tb + 1) * 128, :], in_=xtb[:]),
            inc='outa', inc_by=16)
        em.wait('scalar', 've', cv)
        c2 = emit_plain_ln(em, xtb[:], t1tb[:], stt, sq2[:])
        em.wait('tensor', 'sc', c2)
        for kb in range(NKB):
            pstk = ppt.nxt(em)
            ct = em.op('tensor', lambda e, kb=kb, pstk=pstk: e.transpose(
                pstk[:], t1tb[:, kb * 128:(kb + 1) * 128], ident_sb[:]),
                inc='te')
            em.wait('scalar', 'te', ct)
            ce2 = em.op('scalar', lambda e, kb=kb, tb=tb, pstk=pstk:
                        e.activation(
                big[:, kb * TPC + tb * 128:kb * TPC + (tb + 1) * 128],
                pstk[:], AF.Identity, scale=gb_sb[:, kb:kb + 1],
                bias=gb_sb[:, NKB + kb:NKB + kb + 1]), inc='sc')
            ppt.done('sc', ce2)
        em.wait('sync', 'sc')
        em.wait('sync', 'outa', cda)

    em.wait('sync', 'sc')
    for cb in range(32):
        jp = 0
        em.wait('scalar', 'te')
        src_a = FPT[:, cb * 128:(cb + 1) * 128].rearrange(
            '(kb p) c -> p kb c', p=128)
        src_g = FPT[:, 4096 + cb * 128:4096 + (cb + 1) * 128].rearrange(
            '(kb p) c -> p kb c', p=128)
        cwa = em.dma('sync',
                     wlda[:, 0:NKB * 128].rearrange('p (kb c) -> p kb c',
                                                    kb=NKB), src_a, inc='ldw')
        cwg = em.dma('sync',
                     wldg[:, 0:NKB * 128].rearrange('p (kb c) -> p kb c',
                                                    kb=NKB), src_g,
                     inc='ldwg')
        em.wait('scalar', 'ldw', cwa)
        em.op('scalar', lambda e, jp=jp: e.activation(
            fa_bf[jp][:], wlda[:, 0:NKB * 128], AF.Copy), inc='sc')
        em.wait('scalar', 'ldwg', cwg)
        c2 = em.op('scalar', lambda e, jp=jp: e.activation(
            fg_bf[jp][:], wldg[:, 0:NKB * 128], AF.Copy), inc='sc')
        em.wait('sync', 'sc', c2)
        em.wait('tensor', 'sc', c2)
        for tch in range(4):
            parts = [(fa_bf[jp][:, kb * 128:(kb + 1) * 128],
                      big[:, kb * TPC + tch * 512:kb * TPC + (tch + 1) * 512])
                     for kb in range(NKB)]
            psa = ppa.nxt(em)
            mm_accum(em, psa[:], parts, inc='te')
            parts = [(fg_bf[jp][:, kb * 128:(kb + 1) * 128],
                      big[:, kb * TPC + tch * 512:kb * TPC + (tch + 1) * 512])
                     for kb in range(NKB)]
            psg = ppg.nxt(em)
            cm = mm_accum(em, psg[:], parts, inc='te')
            if tch == 3:
                wgate[jp] = cm
            em.wait('scalar', 'te', cm)
            cs = em.op('scalar', lambda e, psg=psg: e.activation(
                sg_sb[:], psg[:], AF.Silu), inc='sc')
            ppg.done('sc', cs)
            em.wait('vector', 'sc', cs)
            cv = em.op('vector', lambda e, psa=psa: e.tensor_tensor(
                ffat[:], sg_sb[:], psa[:], ALU.mult), inc='ve')
            ppa.done('ve', cv)
            em.wait('gpsimd', 've', cv)
            cd = em.op('gpsimd', lambda e, cb=cb, tch=tch: e.dma_start(
                out=FFA[cb * 128:(cb + 1) * 128,
                        tch * 512:(tch + 1) * 512], in_=ffat[:]),
                inc='outf', inc_by=16)
            em.wait('vector', 'outf', cd)

    em.wait('sync', 'te')
    em.wait('sync', 'outf')
    for kb in range(32):
        em.dma('sync', big[:, kb * TPC:(kb + 1) * TPC],
               FFA[kb * 128:(kb + 1) * 128, :], inc='ldf')
    cf = em.cnt['ldf']
    em.wait('tensor', 'ldf', cf)
    for nh in range(NH):
        par = nh % 2
        cwl = load_chunk(FOT, nh, par)
        em.wait('tensor', 'sc', cwl)
        for tb in range(NTB):
            parts = [(big[:, kb * TPC + tb * 128:kb * TPC + (tb + 1) * 128],
                      warena[par][:, kb * WCH:(kb + 1) * WCH])
                     for kb in range(32)]
            psy = ppy.nxt(em)
            cm = mm_accum(em, psy[:], parts, inc='te')
            if tb == NTB - 1:
                wgate[par] = cm
            ca = em.dma('sync', arch[:],
                        ARES[tb * 128:(tb + 1) * 128,
                             nh * WCH:(nh + 1) * WCH], inc='lda')
            em.wait('scalar', 'te', cm)
            ce = em.op('scalar', lambda e, psy=psy: e.activation(
                mixt[:], psy[:], AF.Copy), inc='sc')
            ppy.done('sc', ce)
            em.wait('vector', 'sc', ce)
            em.wait('vector', 'lda', ca)
            cv = em.op('vector', lambda e: e.tensor_tensor(
                mixt[:], mixt[:], arch[:], ALU.add), inc='ve')
            em.wait('gpsimd', 've', cv)
            cd = em.op('gpsimd', lambda e, tb=tb, nh=nh: e.dma_start(
                out=T6[tb * 128:(tb + 1) * 128, nh * WCH:(nh + 1) * WCH],
                in_=mixt[:]), inc='outt', inc_by=16)
            em.wait('scalar', 'outt', cd)
            em.wait('vector', 'outt', cd)
            em.wait('sync', 've', cv)

    for tb in range(NTB):
        em.wait('sync', 'sc')
        em.wait('sync', 've')
        cx = em.dma('sync', xtb[:], X2[tb * 128:(tb + 1) * 128, :],
                    inc='ldx')
        ct6 = em.dma('sync', t1tb[:], T6[tb * 128:(tb + 1) * 128, :],
                     inc='ldm')
        em.wait('scalar', 'ldm', ct6)
        em.wait('vector', 'ldm', ct6)
        emit_plain_ln(em, t1tb[:], t1tb[:], stt, sq2[:])
        em.wait('vector', 'sc')
        em.wait('vector', 'ldx', cx)
        cv = em.op('vector', lambda e: e.tensor_tensor(
            xtb[:], xtb[:], t1tb[:], ALU.add), inc='ve')
        em.wait('gpsimd', 've', cv)
        cd = em.op('gpsimd', lambda e, tb=tb: e.dma_start(
            out=X3[tb * 128:(tb + 1) * 128, :], in_=xtb[:]),
            inc='outx', inc_by=16)
        em.wait('sync', 'outx', cd)
    em.wait('gpsimd', 'outx')



def build_launchE():
    nc = bass.Bass('TRN2', target_bir_lowering=False)
    io = declare_io_E(nc)
    stack = contextlib.ExitStack()
    with stack:
        em = Em(nc, stack)
        emit_E(nc, em, stack, io)
        em.finalize()
    return nc


def host_prep_launchE(inputs, gs, x2s):
    gfulls = []
    for j in range(NCORES):
        gfulls.append(np.ascontiguousarray(
            np.concatenate([gs[c][j] for c in range(NCORES)], axis=0)))
    opt = np.ascontiguousarray(np.asarray(inputs['L1_attn_out_proj_w']).T)
    gb2l = np.stack([np.asarray(inputs['L1_attn_ln2_w']),
                     np.asarray(inputs['L1_attn_ln2_b'])], axis=0)
    fpt = np.ascontiguousarray(np.asarray(inputs['L1_attn_ff_proj_w']).T)
    fot = np.ascontiguousarray(np.asarray(inputs['L1_attn_ff_out_w']).T)
    ident = np.eye(128, dtype=np.float32)
    cvals = np.array([[EPS, -0.5, 0, 0, 0, 0, 0, 0]], np.float32)
    in_maps = []
    for c in range(NCORES):
        in_maps.append(dict(gfull=gfulls[c], x2=x2s[c], opt=opt, gb2l=gb2l,
                            fpt=fpt, fot=fot, ident=ident, cvals=cvals))
    return in_maps


def declare_io_F(nc, internal=()):
    io = {}
    io['x3'] = nc.dram_tensor('x3', [TPC, D], F32, kind=None if 'x3' in internal else 'ExternalInput').ap()
    io['rw9'] = nc.dram_tensor('rw9', [D, 9], F32, kind=None if 'rw9' in internal else 'ExternalInput').ap()
    io['gupt'] = nc.dram_tensor('gupt', [E_MOE, D, 2 * I_MOE], F32, kind=None if 'gupt' in internal else 'ExternalInput').ap()
    io['dpt'] = nc.dram_tensor('dpt', [E_MOE, I_MOE, D], F32, kind=None if 'dpt' in internal else 'ExternalInput').ap()
    io['sgt'] = nc.dram_tensor('sgt', [D, I_SH], F32, kind=None if 'sgt' in internal else 'ExternalInput').ap()
    io['sut'] = nc.dram_tensor('sut', [D, I_SH], F32, kind=None if 'sut' in internal else 'ExternalInput').ap()
    io['sdt'] = nc.dram_tensor('sdt', [I_SH, D], F32, kind=None if 'sdt' in internal else 'ExternalInput').ap()
    io['ident'] = nc.dram_tensor('ident', [128, 128], F32, kind=None if 'ident' in internal else 'ExternalInput').ap()
    io['cvals'] = nc.dram_tensor('cvals', [1, 8], F32, kind=None if 'cvals' in internal else 'ExternalInput').ap()
    io['tab'] = nc.dram_tensor('tab', [2, 8], F32, kind=None if 'tab' in internal else 'ExternalInput').ap()
    io['x4'] = nc.dram_tensor('x4', [TPC, D], F32, kind=None if 'x4' in internal else 'ExternalOutput').ap()
    io['oacc'] = nc.dram_tensor('oacc', [TPC, D], F32).ap()
    io['shact'] = nc.dram_tensor('shact', [I_SH, TPC], BF16).ap()
    io['combt'] = nc.dram_tensor('combt', [E_MOE, TPC], F32).ap()

    return io


def emit_F(nc, em, stack, io):
    NJ = I_MOE // 128
    NJS = I_SH // 128
    X3 = io['x3']
    RW9 = io['rw9']
    GUPT = io['gupt']
    DPT = io['dpt']
    SGT = io['sgt']
    SUT = io['sut']
    SDT = io['sdt']
    IDENT = io['ident']
    CVALS = io['cvals']
    TAB = io['tab']
    X4 = io['x4']
    OACC = io['oacc']
    SHACT = io['shact']
    COMBT = io['combt']
    sb = lambda name, shp, dt: stack.enter_context(
        nc.sbuf_tensor(name, shp, dt, allow_name_mangling=True))
    big = sb('big', [128, NKB * TPC], BF16)
    act = sb('act', [128, NJ * TPC], BF16)
    fa_bf = [sb('fa_bf0', [128, NKB * 128], BF16),
             sb('fa_bf1', [128, NKB * 128], BF16)]
    fg_bf = [sb('fg_bf0', [128, NKB * 128], BF16),
             sb('fg_bf1', [128, NKB * 128], BF16)]
    shtb = sb('shtb', [128, NJS * 128], BF16)
    wld = sb('wld', [128, 512], F32)
    wldd = sb('wldd', [128, 6 * 512], F32)
    xtb = sb('xtb', [128, D], F32)
    t1tb = sb('t1tb', [128, D], F32)
    sq2 = sb('sq2', [128, D], F32)
    wlda = sq2
    wldg = t1tb
    stt = sb('stt', [128, 16], F32)
    ident_sb = sb('ident_sb', [128, 128], F32)
    cv_sb = sb('cv_sb', [128, 8], F32)
    rw_bf = sb('rw_bf', [128, NKB * 9], BF16)
    lg = sb('lg', [128, 16], F32)
    iot8 = sb('iot8', [128, 8], F32)
    iot8m = sb('iot8m', [128, 8], F32)
    comb = sb('comb', [128, 8], F32)
    combT = sb('combT', [8, TPC], F32)
    sg_all = sb('sg_all', [128, NTB], F32)
    cre = sb('cre', [128, 4 * 512], F32)
    sg_sb = sb('sg_sb', [128, 512], F32)
    au_sb = sb('au_sb', [128, 512], F32)
    act16 = sb('act16', [128, 512], BF16)
    otile = [sb('otile0', [128, 512], F32), sb('otile1', [128, 512], F32)]
    psr = stack.enter_context(nc.psum_tensor('psr', [128, 9], F32, allow_name_mangling=True))
    ppa = PP(em, stack, nc, 'psa', [128, 512])
    ppg = PP(em, stack, nc, 'psg', [128, 512])
    ppd = PP(em, stack, nc, 'psd', [128, 512])
    ppt = PP(em, stack, nc, 'pst', None,
             banks=[ppg.ps[0][:, 0:128], ppa.ps[0][:, 0:128]])
    pst = ppg.ps[1]

    for s in ['ldc', 'ldx', 'ldw', 'ldwg', 'ldacc', 'ldsh', 'ldcre',
              'outacc', 'outsh', 'outx', 'outct', 'sc', 've', 'te']:
        em.sem(s)

    em.setup_consts([EPS, -0.5], cv_sb, CVALS)
    em.dma('sync', ident_sb[:], IDENT, inc='ldc')
    em.dma('sync', iot8[:], TAB[0:1, :].partition_broadcast(128).squeeze(1),
           inc='ldc')
    em.dma('sync', iot8m[:], TAB[1:2, :].partition_broadcast(128).squeeze(1),
           inc='ldc')
    for kb in range(NKB):
        cw = em.dma('sync', wld[:, 0:9], RW9[kb * 128:(kb + 1) * 128, :],
                    inc='ldw')
        em.wait('scalar', 'ldw', cw)
        c2 = em.op('scalar', lambda e, kb=kb: e.activation(
            rw_bf[:, kb * 9:(kb + 1) * 9], wld[:, 0:9], AF.Copy),
            inc='sc')
        em.wait('sync', 'sc', c2)
    em.wait('scalar', 'ldc')
    em.wait('vector', 'ldc')
    em.wait('tensor', 'ldc')
    em.wait('tensor', 'sc')

    for tb in range(NTB):
        cx = em.dma('sync', xtb[:], X3[tb * 128:(tb + 1) * 128, :],
                    inc='ldx')
        em.wait('scalar', 'ldx', cx)
        em.wait('vector', 'ldx', cx)
        c7 = emit_plain_ln(em, xtb[:], t1tb[:], stt, sq2[:])
        em.wait('tensor', 'sc', c7)
        for kb in range(NKB):
            pstk = ppt.nxt(em)
            ct = em.op('tensor', lambda e, kb=kb, pstk=pstk: e.transpose(
                pstk[:], t1tb[:, kb * 128:(kb + 1) * 128], ident_sb[:]),
                inc='te')
            em.wait('scalar', 'te', ct)
            ce2 = em.op('scalar', lambda e, kb=kb, tb=tb, pstk=pstk:
                        e.activation(
                big[:, kb * TPC + tb * 128:kb * TPC + (tb + 1) * 128],
                pstk[:], AF.Copy), inc='sc')
            ppt.done('sc', ce2)
        em.wait('tensor', 'sc')
        parts = [(big[:, kb * TPC + tb * 128:kb * TPC + (tb + 1) * 128],
                  rw_bf[:, kb * 9:(kb + 1) * 9]) for kb in range(NKB)]
        cm = mm_accum(em, psr[:], parts, inc='te')
        em.wait('scalar', 'te', cm)
        ce = em.op('scalar', lambda e: e.activation(
            lg[:, 0:9], psr[:], AF.Copy), inc='sc')
        em.wait('tensor', 'sc', ce)
        em.wait('vector', 'sc', ce)
        em.op('vector', lambda e: e.tensor_reduce(
            stt[:, 8:9], lg[:, 0:8], AX.X, ALU.max, negate=True))
        cs = em.op('vector', lambda e: e.tensor_copy(
            stt[:, 8:9], stt[:, 8:9]), inc='ve')
        em.wait('scalar', 've', cs)
        cex = em.op('scalar', lambda e: e.activation(
            lg[:, 0:8], lg[:, 0:8], AF.Exp, bias=stt[:, 8:9],
            accum_out=stt[:, 9:10]), inc='sc')
        em.wait('vector', 'sc', cex)
        em.op('vector', lambda e: e.reciprocal(stt[:, 10:11],
                                               stt[:, 9:10]))
        em.op('vector', lambda e: e.tensor_scalar(
            lg[:, 0:8], lg[:, 0:8], stt[:, 10:11], None, ALU.mult))
        em.op('vector', lambda e: e.tensor_reduce(
            stt[:, 11:12], lg[:, 0:8], AX.X, ALU.max))
        em.op('vector', lambda e: e.tensor_scalar(
            comb[:], lg[:, 0:8], stt[:, 11:12], None, ALU.is_ge))
        em.op('vector', lambda e: e.tensor_tensor(
            comb[:], comb[:], iot8m[:], ALU.mult))
        em.op('vector', lambda e: e.tensor_scalar(
            comb[:], comb[:], BIGC, None, ALU.add))
        em.op('vector', lambda e: e.tensor_reduce(
            stt[:, 12:13], comb[:], AX.X, ALU.min))
        em.op('vector', lambda e: e.tensor_scalar(
            comb[:], iot8[:], stt[:, 12:13], None, ALU.is_equal))
        em.op('vector', lambda e: e.tensor_tensor(
            sq2[:, 16:24], lg[:, 0:8], comb[:], ALU.mult))
        em.op('vector', lambda e: e.tensor_tensor(
            sq2[:, 16:24], lg[:, 0:8], sq2[:, 16:24], ALU.subtract))
        em.op('vector', lambda e: e.tensor_reduce(
            stt[:, 13:14], sq2[:, 16:24], AX.X, ALU.max))
        em.op('vector', lambda e: e.tensor_scalar(
            sq2[:, 0:8], sq2[:, 16:24], stt[:, 13:14], None, ALU.is_ge))
        em.op('vector', lambda e: e.tensor_tensor(
            sq2[:, 0:8], sq2[:, 0:8], iot8m[:], ALU.mult))
        em.op('vector', lambda e: e.tensor_scalar(
            sq2[:, 0:8], sq2[:, 0:8], BIGC, None, ALU.add))
        em.op('vector', lambda e: e.tensor_reduce(
            stt[:, 14:15], sq2[:, 0:8], AX.X, ALU.min))
        em.op('vector', lambda e: e.tensor_tensor(
            stt[:, 15:16], stt[:, 11:12], stt[:, 13:14], ALU.add))
        em.op('vector', lambda e: e.reciprocal(stt[:, 15:16],
                                               stt[:, 15:16]))
        em.op('vector', lambda e: e.tensor_tensor(
            stt[:, 11:12], stt[:, 11:12], stt[:, 15:16], ALU.mult))
        em.op('vector', lambda e: e.tensor_tensor(
            stt[:, 13:14], stt[:, 13:14], stt[:, 15:16], ALU.mult))
        em.op('vector', lambda e: e.tensor_scalar(
            comb[:], comb[:], stt[:, 11:12], None, ALU.mult))
        em.op('vector', lambda e: e.tensor_scalar(
            sq2[:, 8:16], iot8[:], stt[:, 14:15], None, ALU.is_equal))
        cv = em.op('vector', lambda e: e.scalar_tensor_tensor(
            comb[:], sq2[:, 8:16], stt[:, 13:14], comb[:], ALU.mult,
            ALU.add), inc='ve')
        em.wait('tensor', 've', cv)
        em.wait('tensor', 'sc')
        ct = em.op('tensor', lambda e: e.transpose(
            pst[0:8, 0:128], comb[:], ident_sb[:]), inc='te')
        em.wait('scalar', 'te', ct)
        em.op('scalar', lambda e, tb=tb: e.activation(
            combT[:, tb * 128:(tb + 1) * 128], pst[0:8, 0:128], AF.Copy),
            inc='sc')
        csg = em.op('scalar', lambda e, tb=tb: e.activation(
            sg_all[:, tb:tb + 1], lg[:, 8:9], AF.Sigmoid), inc='sc')
        em.wait('sync', 'sc', csg)
        em.wait('sync', 've')
    em.wait('gpsimd', 'sc')
    cct = em.dma('gpsimd', COMBT, combT[:], inc='outct')
    em.wait('sync', 'outct', cct)

    fggate = [0, 0]
    ogate = [0] * (4 * NTB)
    pgate = [0, 0]
    def load_pair(Wsrc, coloff, cb):
        pass
    for e in range(E_MOE):
        em.wait('sync', 've')
        ccre = em.dma('sync', cre[:],
                      COMBT[e:e + 1, :].partition_broadcast(128)
                      .squeeze(1), inc='ldcre')
        em.wait('vector', 'ldcre', ccre)
        for j in range(NJ):
            jp = j % 2
            em.wait('scalar', 'te', fggate[jp])
            src_a = GUPT[e, :, j * 128:(j + 1) * 128].rearrange(
                '(kb p) c -> p kb c', p=128)
            src_g = GUPT[e, :, I_MOE + j * 128:I_MOE + (j + 1) * 128].rearrange('(kb p) c -> p kb c', p=128)
            cwa = em.dma('sync', wlda[:].rearrange('p (kb c) -> p kb c',
                                                   kb=NKB), src_a,
                         inc='ldw')
            cwg = em.dma('sync', wldg[:].rearrange('p (kb c) -> p kb c',
                                                   kb=NKB), src_g,
                         inc='ldwg')
            em.wait('scalar', 'ldw', cwa)
            em.op('scalar', lambda e_, jp=jp: e_.activation(
                fa_bf[jp][:], wlda[:], AF.Copy), inc='sc')
            em.wait('scalar', 'ldwg', cwg)
            c2 = em.op('scalar', lambda e_, jp=jp: e_.activation(
                fg_bf[jp][:], wldg[:], AF.Copy), inc='sc')
            em.wait('sync', 'sc', c2)
            em.wait('tensor', 'sc', c2)
            for tch in range(4):
                parts = [(fa_bf[jp][:, kb * 128:(kb + 1) * 128],
                          big[:, kb * TPC + tch * 512:
                              kb * TPC + (tch + 1) * 512])
                         for kb in range(NKB)]
                psg = ppg.nxt(em)
                mm_accum(em, psg[:], parts, inc='te')
                parts = [(fg_bf[jp][:, kb * 128:(kb + 1) * 128],
                          big[:, kb * TPC + tch * 512:
                              kb * TPC + (tch + 1) * 512])
                         for kb in range(NKB)]
                psa = ppa.nxt(em)
                cm = mm_accum(em, psa[:], parts, inc='te')
                if tch == 3:
                    fggate[jp] = cm
                em.wait('scalar', 'te', cm)
                cs = em.op('scalar', lambda e_, psg=psg: e_.activation(
                    sg_sb[:], psg[:], AF.Silu), inc='sc')
                ppg.done('sc', cs)
                em.wait('vector', 'sc', cs)
                cv0 = em.op('vector', lambda e_, psa=psa: e_.tensor_tensor(
                    au_sb[:], sg_sb[:], psa[:], ALU.mult), inc='ve')
                ppa.done('ve', cv0)
                cv = em.op('vector', lambda e_, tch=tch: e_.tensor_tensor(
                    au_sb[:], au_sb[:], cre[:, tch * 512:(tch + 1) * 512],
                    ALU.mult), inc='ve')
                em.wait('scalar', 've', cv)
                ca = em.op('scalar', lambda e_, j=j, tch=tch: e_.activation(
                    act[:, j * TPC + tch * 512:j * TPC + (tch + 1) * 512],
                    au_sb[:], AF.Copy), inc='sc')
                em.wait('vector', 'sc', ca)
        em.wait('tensor', 'sc')
        for nh in range(4):
            em.wait('scalar', 'te')
            src_d = DPT[e, :, nh * 512:(nh + 1) * 512].rearrange(
                '(j p) c -> p j c', p=128)
            for j0 in range(0, NJ, 6):
                nj = min(6, NJ - j0)
                cw = em.dma('sync',
                            wldd[:, 0:nj * 512].rearrange('p (j c) -> p j c',
                                                          j=nj),
                            src_d[:, j0:j0 + nj, :], inc='ldw')
                em.wait('scalar', 'ldw', cw)
                c2 = em.op('scalar', lambda e_, j0=j0, nj=nj: e_.activation(
                    shtb[:, j0 * 512:(j0 + nj) * 512], wldd[:, 0:nj * 512],
                    AF.Copy), inc='sc')
                em.wait('sync', 'sc', c2)
            em.wait('tensor', 'sc')
            for tb in range(NTB):
                parts = [(act[:, j * TPC + tb * 128:j * TPC + (tb + 1) * 128],
                          shtb[:, j * 512:(j + 1) * 512])
                         for j in range(NJ)]
                psd = ppd.nxt(em)
                cm = mm_accum(em, psd[:], parts, inc='te')
                em.wait('scalar', 'te', cm)
                op = otile[tb % 2]
                em.wait('scalar', 'outacc', max(ogate[nh * NTB + tb],
                                                pgate[tb % 2]))
                ce = em.op('scalar', lambda e_, psd=psd, op=op: e_.activation(
                    op[:], psd[:], AF.Copy), inc='sc')
                ppd.done('sc', ce)
                em.wait('gpsimd', 'sc', ce)
                if e == 0:
                    cd = em.op('gpsimd', lambda e_, tb=tb, nh=nh, op=op:
                               e_.dma_start(
                        out=OACC[tb * 128:(tb + 1) * 128,
                                 nh * 512:(nh + 1) * 512], in_=op[:]),
                        inc='outacc', inc_by=16)
                else:
                    cd = em.op('gpsimd', lambda e_, tb=tb, nh=nh, op=op:
                               e_.dma_start(
                        out=OACC[tb * 128:(tb + 1) * 128,
                                 nh * 512:(nh + 1) * 512], in_=op[:],
                        accum_op=ALU.add), inc='outacc', inc_by=16)
                ogate[nh * NTB + tb] = cd
                pgate[tb % 2] = cd

    for cb in range(NJS):
        jp = cb % 2
        em.wait('scalar', 'te', fggate[jp])
        src_a = SGT[:, cb * 128:(cb + 1) * 128].rearrange(
            '(kb p) c -> p kb c', p=128)
        src_g = SUT[:, cb * 128:(cb + 1) * 128].rearrange(
            '(kb p) c -> p kb c', p=128)
        cwa = em.dma('sync', wlda[:].rearrange('p (kb c) -> p kb c',
                                               kb=NKB), src_a, inc='ldw')
        cwg = em.dma('sync', wldg[:].rearrange('p (kb c) -> p kb c',
                                               kb=NKB), src_g, inc='ldwg')
        em.wait('scalar', 'ldw', cwa)
        em.op('scalar', lambda e_, jp=jp: e_.activation(
            fa_bf[jp][:], wlda[:], AF.Copy), inc='sc')
        em.wait('scalar', 'ldwg', cwg)
        c2 = em.op('scalar', lambda e_, jp=jp: e_.activation(
            fg_bf[jp][:], wldg[:], AF.Copy), inc='sc')
        em.wait('sync', 'sc', c2)
        em.wait('tensor', 'sc', c2)
        for tch in range(4):
            parts = [(fa_bf[jp][:, kb * 128:(kb + 1) * 128],
                      big[:, kb * TPC + tch * 512:kb * TPC + (tch + 1) * 512])
                     for kb in range(NKB)]
            psg = ppg.nxt(em)
            mm_accum(em, psg[:], parts, inc='te')
            parts = [(fg_bf[jp][:, kb * 128:(kb + 1) * 128],
                      big[:, kb * TPC + tch * 512:kb * TPC + (tch + 1) * 512])
                     for kb in range(NKB)]
            psa = ppa.nxt(em)
            cm = mm_accum(em, psa[:], parts, inc='te')
            if tch == 3:
                fggate[jp] = cm
            em.wait('scalar', 'te', cm)
            cs = em.op('scalar', lambda e_, psg=psg: e_.activation(
                sg_sb[:], psg[:], AF.Silu), inc='sc')
            ppg.done('sc', cs)
            em.wait('vector', 'sc', cs)
            cv = em.op('vector', lambda e_, psa=psa: e_.tensor_tensor(
                act16[:], sg_sb[:], psa[:], ALU.mult), inc='ve')
            ppa.done('ve', cv)
            em.wait('gpsimd', 've', cv)
            cd = em.op('gpsimd', lambda e_, cb=cb, tch=tch: e_.dma_start(
                out=SHACT[cb * 128:(cb + 1) * 128,
                          tch * 512:(tch + 1) * 512], in_=act16[:]),
                inc='outsh', inc_by=16)
            em.wait('vector', 'outsh', cd)

    em.wait('sync', 'te')
    em.wait('sync', 'outsh')
    for nh in range(4):
        em.wait('scalar', 'te')
        src_d = SDT[:, nh * 512:(nh + 1) * 512].rearrange(
            '(j p) c -> p j c', p=128)
        for j0 in range(0, NJS, 6):
            nj = min(6, NJS - j0)
            cw = em.dma('sync',
                        wldd[:, 0:nj * 512].rearrange('p (j c) -> p j c',
                                                      j=nj),
                        src_d[:, j0:j0 + nj, :], inc='ldw')
            em.wait('scalar', 'ldw', cw)
            c2 = em.op('scalar', lambda e_, j0=j0, nj=nj: e_.activation(
                act[:, j0 * 512:(j0 + nj) * 512], wldd[:, 0:nj * 512],
                AF.Copy), inc='sc')
            em.wait('sync', 'sc', c2)
        em.wait('tensor', 'sc')
        for tb in range(NTB):
            em.wait('sync', 'te')
            csh = em.dma('sync',
                         shtb[:].rearrange('p (j c) -> p j c', j=NJS),
                         SHACT[:, tb * 128:(tb + 1) * 128].rearrange(
                             '(j p) c -> p j c', p=128), inc='ldsh')
            em.wait('tensor', 'ldsh', csh)
            parts = [(shtb[:, j * 128:(j + 1) * 128],
                      act[:, j * 512:(j + 1) * 512])
                     for j in range(NJS)]
            psd = ppd.nxt(em)
            cm = mm_accum(em, psd[:], parts, inc='te')
            em.wait('scalar', 'te', cm)
            op = otile[tb % 2]
            em.wait('scalar', 'outacc', max(ogate[nh * NTB + tb],
                                            pgate[tb % 2]))
            ce = em.op('scalar', lambda e_, tb=tb, psd=psd, op=op:
                       e_.activation(
                op[:], psd[:], AF.Identity,
                scale=sg_all[:, tb:tb + 1]), inc='sc')
            ppd.done('sc', ce)
            em.wait('gpsimd', 'sc', ce)
            cd = em.op('gpsimd', lambda e_, tb=tb, nh=nh, op=op: e_.dma_start(
                out=OACC[tb * 128:(tb + 1) * 128,
                         nh * 512:(nh + 1) * 512], in_=op[:],
                accum_op=ALU.add), inc='outacc', inc_by=16)
            ogate[nh * NTB + tb] = cd
            pgate[tb % 2] = cd

    em.wait('sync', 'outacc')
    for tb in range(NTB):
        em.wait('sync', 'sc')
        em.wait('sync', 've')
        cx = em.dma('sync', xtb[:], X3[tb * 128:(tb + 1) * 128, :],
                    inc='ldx')
        ct8 = em.dma('sync', t1tb[:], OACC[tb * 128:(tb + 1) * 128, :],
                     inc='ldacc')
        em.wait('scalar', 'ldacc', ct8)
        em.wait('vector', 'ldacc', ct8)
        emit_plain_ln(em, t1tb[:], t1tb[:], stt, sq2[:])
        em.wait('vector', 'sc')
        em.wait('vector', 'ldx', cx)
        cv = em.op('vector', lambda e_: e_.tensor_tensor(
            xtb[:], xtb[:], t1tb[:], ALU.add), inc='ve')
        em.wait('gpsimd', 've', cv)
        cd = em.op('gpsimd', lambda e_, tb=tb: e_.dma_start(
            out=X4[tb * 128:(tb + 1) * 128, :], in_=xtb[:]),
            inc='outx', inc_by=16)
        em.wait('sync', 'outx', cd)
    em.wait('gpsimd', 'outx')



def build_launchF():
    nc = bass.Bass('TRN2', target_bir_lowering=False)
    io = declare_io_F(nc)
    stack = contextlib.ExitStack()
    with stack:
        em = Em(nc, stack)
        emit_F(nc, em, stack, io)
        em.finalize()
    return nc


def host_prep_launchF(inputs, x3s):
    rw9 = np.ascontiguousarray(np.concatenate(
        [np.asarray(inputs['L1_ffn_router_weight']).T,
         np.asarray(inputs['L1_ffn_shared_gate_w']).T], axis=1))
    gupt = np.ascontiguousarray(
        np.asarray(inputs['L1_ffn_gate_up_proj']).transpose(0, 2, 1))
    dpt = np.ascontiguousarray(
        np.asarray(inputs['L1_ffn_down_proj']).transpose(0, 2, 1))
    sgt = np.ascontiguousarray(np.asarray(inputs['L1_ffn_sh_gate_w']).T)
    sut = np.ascontiguousarray(np.asarray(inputs['L1_ffn_sh_up_w']).T)
    sdt = np.ascontiguousarray(np.asarray(inputs['L1_ffn_sh_down_w']).T)
    ident = np.eye(128, dtype=np.float32)
    cvals = np.array([[EPS, -0.5, 0, 0, 0, 0, 0, 0]], np.float32)
    tab = np.zeros((2, 8), np.float32)
    tab[0] = np.arange(8)
    tab[1] = np.arange(8) - BIGC
    in_maps = []
    for c in range(NCORES):
        in_maps.append(dict(x3=x3s[c], rw9=rw9, gupt=gupt, dpt=dpt, sgt=sgt,
                            sut=sut, sdt=sdt, ident=ident, cvals=cvals,
                            tab=tab))
    return in_maps


def build_fused():
    nc = bass.Bass('TRN2', target_bir_lowering=False)
    IO = {}
    def ext(nm, shp, dt):
        IO[nm] = nc.dram_tensor(nm, shp, dt, kind='ExternalInput').ap()
    def outt(nm, shp, dt):
        IO[nm] = nc.dram_tensor(nm, shp, dt, kind='ExternalOutput').ap()
    def itn(nm, shp, dt):
        IO[nm] = nc.dram_tensor(nm, shp, dt).ap()

    ext('x0', [TPC, D], F32)
    ext('wq', [D, D], F32); ext('wk', [D, D], F32); ext('wv', [D, D], F32)
    ext('bqkv', [3, D], F32); ext('gb', [2, D], F32)
    ext('cos', [TPC, ROT // 2], F32); ext('sin', [TPC, ROT // 2], F32)
    ext('ones', [128, 128], F32); ext('ident', [128, 128], F32)
    ext('cvals', [1, 8], F32)
    ext('wout', [D, D], F32); ext('bout', [1, D], F32); ext('gb2', [2, D], F32)
    ext('slopes', [NUM_TILES, D_SLICE * GRID], F32); ext('tab', [3, 64], F32)
    ext('bases', [64, 512], F32); ext('osc', [64, 1], F32); ext('gb1', [2, D], F32)
    ext('wxmt', [D, DPC], F32); ext('wzt', [D, DPC], F32)
    ext('cw', [4, 128, D_CONV], F32); ext('cb', [4, 128], F32)
    ext('xpt', [DPC, 160], F32); ext('dtpt', [DT_RANK, DPC], F32)
    ext('dtb', [4, 128], F32); ext('alog', [4, 128, D_STATE], F32)
    ext('dsk', [4, 128], F32)
    ext('opt', [D_INNER, D], F32); ext('gb2l', [2, D], F32)
    ext('fpt', [D, HIDDEN], F32); ext('fot', [D_INNER, D], F32)
    ext('rw9', [D, 9], F32)
    ext('gupt', [E_MOE, D, 2 * I_MOE], F32); ext('dpt', [E_MOE, I_MOE, D], F32)
    ext('sgt', [D, I_SH], F32); ext('sut', [D, I_SH], F32)
    ext('sdt', [I_SH, D], F32); ext('tab8', [2, 8], F32)
    outt('x4', [TPC, D], F32)
    itn('qt', [D, TPC], F32); itn('apart', [1, D], F32); itn('asum', [1, D], F32)
    itn('kst', [TPC, D], F32)
    itn('x1', [TPC, D], F32); itn('oh', [TPC, 64], F32); itn('xsl', [TPC, 32], F32)
    itn('minp', [64, 1], F32); itn('maxp', [64, 1], F32)
    itn('minr', [64, 1], F32); itn('maxr', [64, 1], F32)
    itn('ss', [NUM_TILES, D_SLICE], F32)
    itn('x2', [TPC, D], F32); itn('ut', [D, TPC], BF16)
    itn('utallf', [NCORES * D, TPC], BF16)
    itn('xcst', [DPC, T], F32); itn('zst', [DPC, T], F32)
    itn('xdblp', [160, T], F32); itn('xdbl', [160, T], F32)
    itn('gf2', [NCORES * DPC, TPC], BF16); itn('gout', [NCORES * DPC, TPC], BF16)
    itn('x3', [TPC, D], F32)
    itn('mix', [TPC, D], F32); itn('ares', [TPC, D], F32)
    itn('ffa', [D_INNER, TPC], BF16); itn('t6', [TPC, D], F32)
    itn('oacc', [TPC, D], F32); itn('shact', [I_SH, TPC], BF16)
    itn('combt', [E_MOE, TPC], F32)

    gstack = contextlib.ExitStack()
    with gstack:
        em = Em(nc, gstack)
        em.setup_consts([EPS, -0.5, 1.0, -1.0], None, IO['cvals'])

        def coll(name, kind, op, in_ap, out_ap):
            h = em.sem(name)
            def fn(e, h=h, in_ap=in_ap, out_ap=out_ap):
                with allow_ici_all_to_all():
                    e.collective_compute(
                        kind, op, replica_groups=[list(range(NCORES))],
                        ins=[in_ap.opt()], outs=[out_ap.opt()]).then_inc(h)
            em.q['gpsimd'].append(fn)
            for eng in Em.ENGINES:
                em.wait(eng, name, 1)

        with contextlib.ExitStack() as st:
            emit_A(nc, em, st, IO)
            em.barrier()
        coll('ccA', 'AllReduce', ALU.add, IO['apart'], IO['asum'])
        ioB = dict(IO)
        with contextlib.ExitStack() as st:
            emit_B(nc, em, st, ioB)
            em.barrier()
        coll('ccMn', 'AllReduce', ALU.min, IO['minp'], IO['minr'])
        coll('ccMx', 'AllReduce', ALU.max, IO['maxp'], IO['maxr'])
        with contextlib.ExitStack() as st:
            emit_C(nc, em, st, IO)
            em.barrier()
        coll('ccU', 'AllGather', ALU.bypass, IO['ut'], IO['utallf'])
        ioD = dict(IO)
        ioD['utall'] = IO['utallf'].rearrange('(j d) t -> j d t', d=D)
        ioD['g'] = IO['gf2'].rearrange('(j d) t -> j d t', d=DPC)
        with contextlib.ExitStack() as st:
            emit_D(nc, em, st, ioD)
            em.barrier()
        coll('ccG', 'AllToAll', ALU.bypass, IO['gf2'], IO['gout'])
        ioE = dict(IO)
        ioE['gfull'] = IO['gout']
        with contextlib.ExitStack() as st:
            emit_E(nc, em, st, ioE)
            em.barrier()
        ioF = dict(IO)
        ioF['tab'] = IO['tab8']
        with contextlib.ExitStack() as st:
            emit_F(nc, em, st, ioF)
            em.barrier()
        em.finalize()
    return nc


def host_prep_fused(inputs):
    import ml_dtypes
    dummy = np.zeros((1, 1), np.float32)
    d8 = [dummy] * NCORES
    hA = host_prep_launchA(inputs)
    hB = host_prep_launchB(inputs, d8, [np.zeros((1, D), np.float32)] * NCORES)
    hC = host_prep_launchC(inputs, d8, d8, d8,
                           [np.zeros((64, 1), np.float32)] * NCORES,
                           [np.zeros((64, 1), np.float32)] * NCORES)
    hD = host_prep_launchD(inputs, [np.zeros((1, 1), ml_dtypes.bfloat16)] * NCORES)
    hE = host_prep_launchE(inputs, [[np.zeros((1, 1), ml_dtypes.bfloat16)] * NCORES] * NCORES, d8)
    hF = host_prep_launchF(inputs, d8)
    in_maps = []
    for c in range(NCORES):
        m = {}
        m.update(hA[c])
        for k in ['wout', 'bout', 'gb2', 'slopes', 'tab']:
            m[k] = hB[c][k]
        for k in ['bases', 'osc', 'gb1']:
            m[k] = hC[c][k]
        for k in ['wxmt', 'wzt', 'cw', 'cb', 'xpt', 'dtpt', 'dtb', 'alog',
                  'dsk']:
            m[k] = hD[c][k]
        for k in ['opt', 'gb2l', 'fpt', 'fot']:
            m[k] = hE[c][k]
        for k in ['rw9', 'gupt', 'dpt', 'sgt', 'sut', 'sdt']:
            m[k] = hF[c][k]
        m['tab8'] = hF[c]['tab']
        m['cvals'] = np.array([[EPS, -0.5, 1.0, -1.0, 0, 0, 0, 0]], np.float32)
        in_maps.append(m)
    return in_maps


def _rotary_tables(t0, t1):
    j = np.arange(ROT // 2, dtype=np.float64)
    inv = 10000.0 ** (-2.0 * j / ROT)
    pos = np.arange(t0, t1, dtype=np.float64)[:, None]
    fr = pos * inv[None, :]
    return (np.cos(fr).astype(np.float32), np.sin(fr).astype(np.float32))


def host_prep_launchA(inputs):
    X = np.asarray(inputs['L0_attn_X'])[0]
    W_in = np.asarray(inputs['L0_attn_W_in'])
    b_in = np.asarray(inputs['L0_attn_b_in'])
    gamma = np.asarray(inputs['L0_attn_gamma'])
    beta = np.asarray(inputs['L0_attn_beta'])
    wq = np.ascontiguousarray(W_in[:, 0:D])
    wk = np.ascontiguousarray(W_in[:, D:2 * D])
    wv = np.ascontiguousarray(W_in[:, 2 * D:3 * D])
    bqkv = np.ascontiguousarray(b_in.reshape(3, D))
    gb = np.stack([gamma, beta], axis=0)
    ones = np.ones((128, 128), np.float32)
    ident = np.eye(128, dtype=np.float32)
    in_maps = []
    for c in range(NCORES):
        cos, sin = _rotary_tables(c * TPC, (c + 1) * TPC)
        in_maps.append(dict(
            x0=np.ascontiguousarray(X[c * TPC:(c + 1) * TPC]),
            wq=wq, wk=wk, wv=wv, bqkv=bqkv, gb=gb,
            cos=cos, sin=sin, ones=ones, ident=ident,
            cvals=np.array([[EPS, -0.5, 0, 0, 0, 0, 0, 0]], np.float32)))
    return in_maps


def kernel(**inputs):
    cores = list(range(NCORES))
    nc = build_fused()
    in_maps = host_prep_fused(inputs)
    with allow_ici_all_to_all():
        res = run_bass_kernel_spmd(nc, in_maps, core_ids=cores)
    x4 = np.concatenate([res.results[c]['x4'] for c in range(NCORES)], axis=0)
    return x4.reshape(1, T, D).astype(np.float32)


def kernel_multilaunch(**inputs):

    import ml_dtypes

    def as_bf16(a):
        a = np.asarray(a)
        if a.itemsize == 2 and a.dtype != ml_dtypes.bfloat16:
            return a.view(ml_dtypes.bfloat16)
        return a

    cores = list(range(NCORES))
    with allow_ici_all_to_all():
        resA = run_bass_kernel_spmd(build_launchA(), host_prep_launchA(inputs),
                                    core_ids=cores)
        qts = [resA.results[c]['qt'] for c in range(NCORES)]
        aparts = [resA.results[c]['apart'] for c in range(NCORES)]
        resB = run_bass_kernel_spmd(build_launchB(),
                                    host_prep_launchB(inputs, qts, aparts),
                                    core_ids=cores)
        x1s = [resB.results[c]['x1'] for c in range(NCORES)]
        ohs = [resB.results[c]['oh'] for c in range(NCORES)]
        xsls = [resB.results[c]['xsl'] for c in range(NCORES)]
        minps = [resB.results[c]['minp'] for c in range(NCORES)]
        maxps = [resB.results[c]['maxp'] for c in range(NCORES)]
        resC = run_bass_kernel_spmd(
            build_launchC(),
            host_prep_launchC(inputs, x1s, ohs, xsls, minps, maxps),
            core_ids=cores)
        x2s = [resC.results[c]['x2'] for c in range(NCORES)]
        uts = [as_bf16(resC.results[c]['ut']) for c in range(NCORES)]
        resD = run_bass_kernel_spmd(build_launchD(),
                                    host_prep_launchD(inputs, uts),
                                    core_ids=cores)
        gs = [as_bf16(resD.results[c]['g']) for c in range(NCORES)]
        resE = run_bass_kernel_spmd(build_launchE(),
                                    host_prep_launchE(inputs, gs, x2s),
                                    core_ids=cores)
        x3s = [resE.results[c]['x3'] for c in range(NCORES)]
        resF = run_bass_kernel_spmd(build_launchF(),
                                    host_prep_launchF(inputs, x3s),
                                    core_ids=cores)
        x4 = np.concatenate([resF.results[c]['x4'] for c in range(NCORES)],
                            axis=0)
    return x4.reshape(1, T, D).astype(np.float32)
